# version 03
# baseline (speedup 1.0000x reference)
import contextlib
import numpy as np
import concourse.bass as bass
import concourse.mybir as mybir
from concourse.bass_utils import run_bass_kernel_spmd

F32 = mybir.dt.float32
BF16 = mybir.dt.bfloat16
AF = mybir.ActivationFunctionType
ALU = mybir.AluOpType
AX = mybir.AxisListType.X

NCORE = 8
B = 4
T = 4096
D = 1024
NT = 16
TOK = NT * 128
EPS = 1e-5

DEPTH_PLAN = [3, 3, 3, 3, 4, 4, 4, 4, 7, 7, 7, 7, 8, 8, 8, 8]

PERM = np.concatenate([np.arange(0, D, 2), np.arange(1, D, 2)])


def _chunks_of(half):
    if half == 0:
        return list(range(0, 8)) + list(range(24, 32))
    return list(range(8, 24))


def _own_tokens(half):
    idx = []
    for c in _chunks_of(half):
        idx.extend(range(c * 128, (c + 1) * 128))
    return np.array(idx)


_SEM_VEC = {"a_epi", "a_evO", "a_kev", "a_mgen", "a_msk", "a_qev", "a_rope",
            "a_tre2", "a_vev", "c_D2", "c_M", "c_X", "c_ac1", "c_acev",
            "c_btev", "c_c32", "c_cd1", "c_cv1", "c_cvh", "c_dc0", "c_dcev",
            "c_dev", "c_dt0", "c_dt2", "c_fmev", "c_gev", "c_halo", "c_lnx",
            "c_opev", "c_rms1", "c_rsm", "c_sb1", "c_sini", "c_sprep",
            "c_supd", "c_trc", "c_tre", "c_xev", "c_xsev", "c_y2", "c_y3",
            "c_yas", "c_yc2", "c_ycev", "c_ypsfree", "c_ytev", "c_zev", "ev",
            "evA", "evD", "lnc2free", "m_cum", "m_gev", "m_hev", "m_oev",
            "m_offev", "m_rbev", "m_res", "m_rowf", "m_rt1", "m_rt2",
            "m_trel", "m_wcap", "m_wgl", "trc", "tre"}
_SEM_PE = {"a_kmm", "a_omm", "a_pvmm", "a_qkmm", "a_qmm", "a_trm2", "a_vmm",
           "c_acm", "c_btm", "c_cdm", "c_dbm", "c_gm", "c_imm", "c_lbm",
           "c_opm", "c_row", "c_sb2", "c_trf", "c_trm", "c_um", "c_xsm",
           "c_ycm", "c_ydm", "c_yom", "c_ytm", "m_gmm", "m_hmm", "m_offm",
           "m_omm", "m_posm", "m_rbm", "m_summ", "m_trml", "mmA", "mmB0",
           "mmB1", "mmD", "trm"}
_SEM_ACT = {"a_exp", "ac", "acA", "acB0", "acB1", "c_E", "c_ac0", "c_cd0",
            "c_cv2", "c_cvdone", "c_dc1", "c_dt1", "c_ea", "c_rms0",
            "c_rms2", "c_sb0", "c_zs", "m_hact", "m_sg"}
_SEM_GPSE = {"a_cst", "c_cst", "c_s0", "cst", "m_capc", "m_cst"}


def _sem_pool(name):
    if name in _SEM_VEC:
        return "_vec"
    if name in _SEM_PE:
        return "_pe"
    if name in _SEM_ACT:
        return "_act"
    if name in _SEM_GPSE:
        return "_gpse"
    return name


class PB:

    def __init__(self, nc, es):
        self.nc = nc
        self.es = es
        self.blk = es.enter_context(nc.Block())
        self.sems = {}

    def semh(self, name):
        key = _sem_pool(name)
        if key not in self.sems:
            self.sems[key] = [self.es.enter_context(self.nc.semaphore(key)), 0]
        return self.sems[key]

    def inc(self, name, instr, n=1):
        h = self.semh(name)
        instr.then_inc(h[0], n)
        h[1] += n
        return h[1]

    def val(self, name):
        return self.semh(name)[1]

    def wait(self, eng, name, val=None):
        h = self.semh(name)
        v = h[1] if val is None else val
        if v > 0:
            eng.wait_ge(h[0], v)

    def on(self, engname, f):
        getattr(self.blk, engname)(f)


def _sb(nc, es, name, shape, dtype):
    return es.enter_context(nc.sbuf_tensor(name, list(shape), dtype))


def _ps(nc, es, name, shape, dtype=F32):
    return es.enter_context(nc.psum_tensor(name, list(shape), dtype))


def emit_const_pool(pb, nc, es, values):
    cp = _sb(nc, es, "cpool", [128, max(1, len(values))], F32)
    aps = {}
    def g0(g):
        for i, v in enumerate(values):
            ins = g.memset(cp[:, i:i + 1], float(v))
        pb.inc("cst", ins)
    pb.on("gpsimd", g0)
    for i, v in enumerate(values):
        aps[v] = cp[:, i:i + 1]
    for eng in ("vector", "scalar", "tensor"):
        def w(e):
            pb.wait(e, "cst")
        pb.on(eng, w)
    return aps


def mk_ln_scratch(nc, es, pfx, eps_ap=None):
    return {"eps": eps_ap,
            "junk": _sb(nc, es, pfx + "junk", [128, 2048], BF16),
            "lnt": _sb(nc, es, pfx + "lnt", [128, 1024], F32),
            "c1": _sb(nc, es, pfx + "c1", [128, 1], F32),
            "c2": _sb(nc, es, pfx + "c2", [128, 1], F32),
            "c3": _sb(nc, es, pfx + "c3", [128, 1], F32),
            "c4": _sb(nc, es, pfx + "c4", [128, 1], F32)}


def emit_ln(pb, sc, rtile, out_tile, ready=None, gain_bc=None, bias_bc=None, width=1024):
    c1, c2, c3, c4 = sc["c1"], sc["c2"], sc["c3"], sc["c4"]
    junk = sc["junk"]

    def v1(v):
        if ready is not None:
            pb.wait(v, ready[0], ready[1])
        v.drain()
        v.tensor_reduce(c1[:, :], rtile, axis=AX, op=ALU.add)
        v.drain()
        pb.inc("ev", v.tensor_scalar(c1[:, :], c1[:, :], 1.0 / width, None, op0=ALU.mult))
    pb.on("vector", v1)

    def s1(s):
        if ready is not None:
            pb.wait(s, ready[0], ready[1])
        pb.wait(s, "lnc2free")
        pb.inc("ac", s.activation(junk[:, :width], rtile, AF.Square, accum_out=c2[:, :]))
    pb.on("scalar", s1)

    def v2(v):
        pb.wait(v, "ac")
        pb.wait(v, "ev")
        v.tensor_scalar(c2[:, :], c2[:, :], 1.0 / width, None, op0=ALU.mult)
        v.drain()
        v.tensor_tensor(c3[:, :], c1[:, :], c1[:, :], op=ALU.mult)
        v.drain()
        pb.inc("lnc2free", v.tensor_tensor(c2[:, :], c2[:, :], c3[:, :], op=ALU.subtract))
    pb.on("vector", v2)

    def s2(s):
        pb.wait(s, "lnc2free")
        pb.inc("ac", s.activation(c3[:, :], c2[:, :], AF.Sqrt, bias=sc["eps"]))
    pb.on("scalar", s2)

    def v3(v):
        pb.wait(v, "ac")
        v.reciprocal(c4[:, :], c3[:, :])
        v.drain()
        if gain_bc is None:
            pb.inc("lnw", v.tensor_scalar(out_tile, rtile, c1[:, :], c4[:, :],
                                          op0=ALU.subtract, op1=ALU.mult))
        else:
            v.tensor_scalar(sc["lnt"][:, :width], rtile, c1[:, :], c4[:, :],
                            op0=ALU.subtract, op1=ALU.mult)
            v.drain()
            v.tensor_tensor(sc["lnt"][:, :width], sc["lnt"][:, :width], gain_bc, op=ALU.mult)
            v.drain()
            pb.inc("lnw", v.tensor_tensor(out_tile, sc["lnt"][:, :width], bias_bc, op=ALU.add))
    pb.on("vector", v3)


def emit_cast_transpose(pb, stream_tiles, xbf, ident_bf, xT_dst, tps, tag):
    for k, (src, pos) in enumerate(stream_tiles):
        def vc(v, src=src):
            pb.wait(v, "trm")
            pb.inc("trc", v.tensor_copy(xbf[:, :], src))
        pb.on("vector", vc)

        def tt(t, k=k):
            pb.wait(t, "wld")
            pb.wait(t, "trc")
            pb.wait(t, "tre", max(0, pb.val("tre") - 1))
            for dc in range(7):
                t.transpose(tps[:, k % 2, dc * 128:(dc + 1) * 128],
                            xbf[:, dc * 128:(dc + 1) * 128], ident_bf[:, :])
            pb.inc("trm", t.transpose(tps[:, k % 2, 7 * 128:8 * 128],
                                      xbf[:, 7 * 128:8 * 128], ident_bf[:, :]))
        pb.on("tensor", tt)

        def ve(v, k=k, pos=pos):
            pb.wait(v, "trm")
            pb.inc("tre", v.tensor_copy(
                xT_dst[:, :, pos * 128:(pos + 1) * 128],
                tps[:, k % 2, :].rearrange("p (a b) -> p a b", a=8)))
        pb.on("vector", ve)


FFN_PARTS = 99

def emit_ffn(pb, es, io):
    nc = pb.nc
    stream = io["stream"]
    sc = io["lnsc"]
    identb = io["identb"]
    SLAB = 4
    NSLAB = NT // SLAB
    ST = SLAB * 128

    xT = _sb(nc, es, "fd_xT", [128, 8, ST], BF16)
    t1a = _sb(nc, es, "fd_t1a", [128, 2, ST], BF16)
    t1 = _sb(nc, es, "fd_t1", [128, 32, ST], BF16)
    t2 = _sb(nc, es, "fd_t2", [128, 32, ST], BF16)
    xbf = _sb(nc, es, "fd_xbf", [128, 1024], BF16)
    w1b = _sb(nc, es, "fd_w1b", [128, 2, 8, 128], BF16)
    w2b = _sb(nc, es, "fd_w2b", [128, 2, 8, 128], BF16)
    hb = _sb(nc, es, "fd_hb", [128, 2, 32, 128], BF16)
    s1c = _sb(nc, es, "fd_s1c", [128, 32], F32)
    s2c = _sb(nc, es, "fd_s2c", [128, 32], F32)
    sh1c = _sb(nc, es, "fd_sh1c", [128, 32], F32)
    sh2c = _sb(nc, es, "fd_sh2c", [128, 32], F32)
    s3bc = _sb(nc, es, "fd_s3bc", [128, 1024], F32)
    rt = _sb(nc, es, "fd_rt", [128, SLAB, 1024], F32)
    tps = _ps(nc, es, "fd_tps", [128, 2, 1024], BF16)
    psA = _ps(nc, es, "fd_psA", [128, 2, 512], F32)
    psB = _ps(nc, es, "fd_psB", [128, 2, 512], F32)
    psO = _ps(nc, es, "fd_psO", [128, 2, 512], F32)

    w1r = io["w1t"].rearrange("(dc p) f -> p dc f", p=128)
    w2r = io["w2t"].rearrange("(dc p) f -> p dc f", p=128)
    h1r = io["h1t"].rearrange("(kc p) f -> p kc f", p=128)
    h2r = io["h2t"].rearrange("(kc p) f -> p kc f", p=128)
    w3r = io["w3t"].rearrange("(kc p) f -> p kc f", p=128)

    def ld0(g):
        pb.inc("wld", g.dma_start(s1c[:, :], io["s1c"][:, :]), 16)
        pb.inc("wld", g.dma_start(s2c[:, :], io["s2c"][:, :]), 16)
        pb.inc("wld", g.dma_start(sh1c[:, :], io["sh1c"][:, :]), 16)
        pb.inc("wld", g.dma_start(sh2c[:, :], io["sh2c"][:, :]), 16)
        pb.inc("wld", g.dma_start(s3bc[:, :], io["s3bc"][:, :]), 16)
    pb.on("gpsimd", ld0)

    for sl in range(NSLAB if FFN_PARTS >= 99 else 1):
        tiles = list(range(sl * SLAB, (sl + 1) * SLAB))
        if FFN_PARTS < 1:
            break
        emit_cast_transpose(pb, [(stream[:, i, :], pos) for pos, i in enumerate(tiles)],
                            xbf, identb, xT, tps, f"f{sl}")

        mmA_after, evA_after, acA_after = {}, {}, {}
        for fc in range(32 if FFN_PARTS >= 2 else 0):
            bf = fc % 2
            def lw(g, fc=fc, bf=bf):
                pb.wait(g, "wld")
                if fc >= 2:
                    pb.wait(g, "mmA", mmA_after[fc - 2])
                else:
                    pb.wait(g, "mmA")
                pb.inc("wldA", g.dma_start(w1b[:, bf], w1r[:, :, fc * 128:(fc + 1) * 128]), 16)
                pb.inc("wldA", g.dma_start(w2b[:, bf], w2r[:, :, fc * 128:(fc + 1) * 128]), 16)
            pb.on("gpsimd", lw)

            def mm(t, fc=fc, bf=bf):
                pb.wait(t, "wldA")
                pb.wait(t, "tre")
                if fc >= 2:
                    pb.wait(t, "acA", acA_after[fc - 2])
                    pb.wait(t, "evA", evA_after[fc - 2])
                else:
                    pb.wait(t, "acB1")
                for dc in range(8):
                    t.matmul(psA[:, bf], w1b[:, bf, dc, :], xT[:, dc, :],
                             start=(dc == 0), stop=(dc == 7))
                for dc in range(8):
                    ins = t.matmul(psB[:, bf], w2b[:, bf, dc, :], xT[:, dc, :],
                                   start=(dc == 0), stop=(dc == 7))
                mmA_after[fc] = pb.inc("mmA", ins)
            pb.on("tensor", mm)

            def ev_a(s, fc=fc, bf=bf):
                pb.wait(s, "mmA", mmA_after[fc])
                if fc >= 2:
                    pb.wait(s, "evA", evA_after[fc - 2])
                else:
                    pb.wait(s, "evA")
                acA_after[fc] = pb.inc("acA", s.activation(
                    t1a[:, bf, :], psA[:, bf], AF.Silu, scale=s1c[:, fc:fc + 1]))
            pb.on("scalar", ev_a)

            def ev_b(v, fc=fc, bf=bf):
                pb.wait(v, "acA", acA_after[fc])
                evA_after[fc] = pb.inc("evA", v.scalar_tensor_tensor(
                    t1[:, fc, :], psB[:, bf], s2c[:, fc:fc + 1], t1a[:, bf, :],
                    op0=ALU.mult, op1=ALU.mult))
            pb.on("vector", ev_b)

        for step, (hr, scol, src, dst) in enumerate(
                ([(h1r, sh1c, t1, t2), (h2r, sh2c, t2, t1)][:max(0, FFN_PARTS - 2)])):
            mmB_after, acB_after = {}, {}
            sem_m, sem_a, sem_w = f"mmB{step}", f"acB{step}", f"wldB{step}"
            for fc in range(32):
                bf = fc % 2
                def lh(g, fc=fc, bf=bf):
                    if fc >= 2:
                        pb.wait(g, sem_m, mmB_after[fc - 2])
                    else:
                        pb.wait(g, "mmB0")
                        pb.wait(g, "mmB1")
                    pb.inc(sem_w, g.dma_start(hb[:, bf], hr[:, :, fc * 128:(fc + 1) * 128]), 16)
                pb.on("gpsimd", lh)

                def mm(t, fc=fc, bf=bf):
                    pb.wait(t, sem_w)
                    if step == 0:
                        pb.wait(t, "evA")
                    else:
                        pb.wait(t, "acB0")
                        pb.wait(t, "evA")
                    if fc >= 2:
                        pb.wait(t, sem_a, acB_after[fc - 2])
                    for kc in range(32):
                        ins = t.matmul(psA[:, bf], hb[:, bf, kc, :], src[:, kc, :],
                                       start=(kc == 0), stop=(kc == 31))
                    mmB_after[fc] = pb.inc(sem_m, ins)
                pb.on("tensor", mm)

                def ev(s, fc=fc, bf=bf):
                    pb.wait(s, sem_m, mmB_after[fc])
                    acB_after[fc] = pb.inc(sem_a, s.activation(
                        dst[:, fc, :], psA[:, bf], AF.Gelu, scale=scol[:, fc:fc + 1]))
                pb.on("scalar", ev)

        evD_after = {}
        for half in range(2 if FFN_PARTS >= 5 else 0):
            def lw3(g, half=half):
                pb.wait(g, "mmB1")
                pb.wait(g, "mmD")
                pb.inc("wldD", g.dma_start(
                    t2[:, :, :], w3r[:, :, half * 512:(half + 1) * 512]), 16)
            pb.on("gpsimd", lw3)
            for ti in range(SLAB):
                def mm(t, half=half, ti=ti):
                    pb.wait(t, "wldD")
                    pb.wait(t, "acB1")
                    seq = half * SLAB + ti
                    if seq >= 2:
                        pb.wait(t, "evD", evD_after[seq - 2])
                    else:
                        pb.wait(t, "evD")
                    for kc in range(32):
                        ins = t.matmul(psO[:, ti % 2], t1[:, kc, ti * 128:(ti + 1) * 128],
                                       t2[:, kc, :], start=(kc == 0), stop=(kc == 31))
                    pb.inc("mmD", ins)
                pb.on("tensor", mm)

                def ev(v, half=half, ti=ti):
                    pb.wait(v, "mmD")
                    if half == 0:
                        pb.wait(v, "ac")
                    v.tensor_tensor(rt[:, ti, half * 512:(half + 1) * 512], psO[:, ti % 2],
                                    s3bc[:, half * 512:(half + 1) * 512], op=ALU.mult)
                    v.drain()
                    evD_after[half * SLAB + ti] = pb.inc("evD", v.tensor_tensor(
                        rt[:, ti, half * 512:(half + 1) * 512],
                        rt[:, ti, half * 512:(half + 1) * 512],
                        stream[:, tiles[ti], half * 512:(half + 1) * 512], op=ALU.add))
                pb.on("vector", ev)
        for ti in range(SLAB if FFN_PARTS >= 6 else 0):
            emit_ln(pb, sc, rt[:, ti, :], stream[:, tiles[ti], :],
                    ready=("evD", evD_after[SLAB + ti]))


def emit_stream_load(pb, stream, x_h):
    def ld(sync):
        for i in range(NT):
            pb.inc("sld", sync.dma_start(stream[:, i, :],
                                         x_h[i * 128:(i + 1) * 128, :]), 16)
    pb.on("sync", ld)
    for eng in ("vector", "scalar", "tensor", "gpsimd"):
        def w(e):
            pb.wait(e, "sld")
        pb.on(eng, w)


def emit_stream_store(pb, stream, y_h, wait_sems):
    def st(sync):
        for s in wait_sems:
            pb.wait(sync, s)
        for i in range(NT):
            pb.inc("ost", sync.dma_start(y_h[i * 128:(i + 1) * 128, :],
                                         stream[:, i, :]), 16)
        pb.wait(sync, "ost")
    pb.on("sync", st)


def _new_nc():
    return bass.Bass("TRN2", target_bir_lowering=False, detect_race_conditions=False)


def _dram_in(nc, name, arr):
    return nc.dram_tensor(name, list(arr.shape), F32, kind="ExternalInput").ap()


def build_ffn_launch(staged):
    nc = _new_nc()
    hs = {k: _dram_in(nc, k, v) for k, v in staged.items()}
    x_h = nc.dram_tensor("x_in", [TOK, D], F32, kind="ExternalInput").ap()
    y_h = nc.dram_tensor("y_out", [TOK, D], F32, kind="ExternalOutput").ap()
    es = contextlib.ExitStack()
    with es:
        pb = PB(nc, es)
        stream = _sb(nc, es, "stream", [128, NT, 1024], F32)
        identb = _sb(nc, es, "identb", [128, 128], BF16)
        csts = emit_const_pool(pb, nc, es, [EPS, 1.0])
        lnsc = mk_ln_scratch(nc, es, "ln_", csts[EPS])

        def ldi(g):
            pb.inc("wld", g.dma_start(identb[:, :], hs["ident"][:, :]), 16)
        pb.on("gpsimd", ldi)

        emit_stream_load(pb, stream, x_h)
        io = dict(hs)
        io.update(stream=stream, identb=identb, lnsc=lnsc)
        emit_ffn(pb, es, io)
        emit_stream_store(pb, stream, y_h, ["lnw"] if FFN_PARTS >= 6 else [])
    return nc




POS_E = [0, 0, 0, 0, 1, 1, 1, 1, 6, 6, 6, 6, 7, 7, 7, 7]
POS_O = [2, 2, 2, 2, 3, 3, 3, 3, 4, 4, 4, 4, 5, 5, 5, 5]


def emit_rope(pb, rx, xfm, ctab, stab, tmp1, tmp2, tag):
    def body(v):
        for i in range(4):
            xE = xfm[:, i, :]
            xO = xfm[:, 4 + i, :]
            c = ctab[:, i, :]
            sn = stab[:, i, :]
            v.tensor_tensor(tmp1[:, :], xE, c, op=ALU.mult)
            v.tensor_tensor(tmp2[:, :], xO, sn, op=ALU.mult)
            v.drain()
            v.tensor_tensor(rx[:, i, :], tmp1[:, :], tmp2[:, :], op=ALU.subtract)
            v.tensor_tensor(tmp1[:, :], xO, c, op=ALU.mult)
            v.tensor_tensor(tmp2[:, :], xE, sn, op=ALU.mult)
            v.drain()
            ins = v.tensor_tensor(rx[:, 4 + i, :], tmp1[:, :], tmp2[:, :], op=ALU.add)
        pb.inc("a_rope", ins)
    pb.on("vector", body)


def emit_attn(pb, es, io):
    nc = pb.nc
    stream = io["stream"]
    sc = io["lnsc"]
    identb = io["identb"]

    k_dram = nc.dram_tensor("a_kdram", [1024, 4096], BF16, kind="Internal").ap()
    v_dram = nc.dram_tensor("a_vdram", [32, 128, 16, 65], BF16, kind="Internal").ap()
    ao_dram = nc.dram_tensor("a_aodram", [TOK, 1024], BF16, kind="Internal").ap()
    q_dram = nc.dram_tensor("a_qdram", [1024, TOK], BF16, kind="Internal").ap()

    qslot = _sb(nc, es, "a_qslot", [128, 2, 8, 128], BF16)
    kslab = _sb(nc, es, "a_kslab", [128, 2, 8, 512], BF16)
    vslab = _sb(nc, es, "a_vslab", [128, 2, 4, 16, 65], BF16)
    ptsb = _sb(nc, es, "a_ptsb", [128, 2, 512], BF16)
    maskE = _sb(nc, es, "a_maskE", [128, 512], BF16)
    maskO = _sb(nc, es, "a_maskO", [128, 512], BF16)
    kabs = _sb(nc, es, "a_kabs", [128, 512], F32)
    visb = _sb(nc, es, "a_visb", [128, 16, 8], F32)
    dgapE = _sb(nc, es, "a_dgapE", [128, 16], F32)
    dgapO = _sb(nc, es, "a_dgapO", [128, 16], F32)
    attn_sb = _sb(nc, es, "a_attnsb", [128, 1024], BF16)
    rcp = _sb(nc, es, "a_rcp", [128, 1], F32)
    wob = _sb(nc, es, "a_wob", [128, 8, 1024], BF16)
    wqb = wob

    es1 = contextlib.ExitStack()
    xfm = _sb(nc, es1, "a_xfm", [128, 8, 512], BF16)
    rx = _sb(nc, es1, "a_rx", [128, 8, 512], BF16)
    ctab = _sb(nc, es1, "a_ctab", [128, 4, 512], BF16)
    stab = _sb(nc, es1, "a_stab", [128, 4, 512], BF16)
    tmp1 = _sb(nc, es1, "a_tmp1", [128, 512], BF16)
    tmp2 = _sb(nc, es1, "a_tmp2", [128, 512], BF16)
    wvb = _sb(nc, es1, "a_wvb", [128, 8, 1024], BF16)
    wkb = _sb(nc, es1, "a_wkb", [128, 8, 1024], BF16)

    vstage = _sb(nc, es1, "a_vstage", [128, 16, 65], BF16)
    kstage = _sb(nc, es1, "a_kstage", [128, 8, 512], BF16)

    psV = _ps(nc, es, "a_psV", [128, 2, 512], F32)

    xk_r = io["x_fmp"].rearrange("(dc p) t -> p dc t", p=128)
    xq_r = io["x_fmq"].rearrange("(dc p) t -> p dc t", p=128)
    cosk_r = io["cos_k"].rearrange("(c p) t -> p c t", p=128)
    sink_r = io["sin_k"].rearrange("(c p) t -> p c t", p=128)
    cosq_r = io["cos_q"].rearrange("(c p) t -> p c t", p=128)
    sinq_r = io["sin_q"].rearrange("(c p) t -> p c t", p=128)
    wv_r = io["wvT"].rearrange("(dc p) f -> p dc f", p=128)
    wk_r = io["wkT"].rearrange("(dc p) f -> p dc f", p=128)
    wq_r = io["wqT"].rearrange("(dc p) f -> p dc f", p=128)
    wo_r = io["woT"].rearrange("(dc p) f -> p dc f", p=128)
    k_dram_r = k_dram.rearrange("(fc p) t -> p fc t", p=128)
    q_dram_r = q_dram.rearrange("(fc p) t -> p fc t", p=128)
    v_dram_r = v_dram.rearrange("kc p h c -> p kc (h c)")

    def ld0(g):
        pb.inc("a_wld", g.dma_start(wvb[:, :, :], wv_r[:, :, :]), 16)
        pb.inc("a_wld", g.dma_start(wkb[:, :, :], wk_r[:, :, :]), 16)
        pb.inc("a_wld", g.dma_start(wqb[:, :, :], wq_r[:, :, :]), 16)
        pb.inc("a_wld", g.dma_start(kabs[:, :], io["kabs"][:, :]), 16)
        pb.inc("a_wld", g.dma_start(visb[:, :, :], io["visb"][:, :, :]), 16)
        pb.inc("a_wld", g.dma_start(dgapE[:, :], io["dgapE"][:, :]), 16)
        pb.inc("a_wld", g.dma_start(dgapO[:, :], io["dgapO"][:, :]), 16)
        pb.inc("a_cst", g.memset(vstage[:, :, 64:65], 1.0))
    pb.on("gpsimd", ld0)

    vev_after = {}
    kev_after = {}
    qev_after = {}
    for ts in range(8):
        def ldx(sync, ts=ts):
            pb.wait(sync, "a_p1done")
            pb.inc("a_xld", sync.dma_start(ctab[:, :, :], cosk_r[:, :, ts * 512:(ts + 1) * 512]), 16)
            pb.inc("a_xld", sync.dma_start(stab[:, :, :], sink_r[:, :, ts * 512:(ts + 1) * 512]), 16)
        pb.on("sync", ldx)

        def ldx2(g, ts=ts):
            pb.wait(g, "a_p1done")
            pb.inc("a_xldg", g.dma_start(xfm[:, :, :], xk_r[:, :, ts * 512:(ts + 1) * 512]), 16)
        pb.on("gpsimd", ldx2)

        for tt in range(4):
            def mmv(t, ts=ts, tt=tt):
                pb.wait(t, "a_xld")
                pb.wait(t, "a_xldg")
                pb.wait(t, "a_wld")
                pb.wait(t, "a_vev")
                pb.wait(t, "a_kev")
                for hf in range(2):
                    for dc in range(8):
                        ins = t.matmul(psV[:, hf, :], xfm[:, dc, tt * 128:(tt + 1) * 128],
                                       wvb[:, dc, hf * 512:(hf + 1) * 512],
                                       start=(dc == 0), stop=(dc == 7))
                pb.inc("a_vmm", ins)
            pb.on("tensor", mmv)

            def evv(v, ts=ts, tt=tt):
                pb.wait(v, "a_vmm")
                pb.wait(v, "a_cst")
                pb.wait(v, "a_vst")
                v.tensor_copy(vstage[:, 0:8, 0:64],
                              psV[:, 0, :].rearrange("p (h c) -> p h c", h=8))
                v.drain()
                pb.inc("a_vev", v.tensor_copy(
                    vstage[:, 8:16, 0:64], psV[:, 1, :].rearrange("p (h c) -> p h c", h=8)))
            pb.on("vector", evv)

            def stv(sync, ts=ts, tt=tt):
                pb.wait(sync, "a_vev")
                pb.inc("a_vst", sync.dma_start(v_dram[ts * 4 + tt, :, :, :], vstage[:, :, :]), 16)
            pb.on("sync", stv)

        def wrope(v):
            pb.wait(v, "a_xld")
            pb.wait(v, "a_xldg")
            v.drain()
        pb.on("vector", wrope)
        emit_rope(pb, rx, xfm, ctab, stab, tmp1, tmp2, f"rk{ts}")

        for fc in range(8):
            def mmk(t, ts=ts, fc=fc):
                pb.wait(t, "a_rope")
                if fc < 2:
                    pb.wait(t, "a_vev")
                    pb.wait(t, "a_kev")
                else:
                    pb.wait(t, "a_kev", kev_after[(ts, fc - 2)])
                for dc in range(8):
                    ins = t.matmul(psV[:, fc % 2, :], wkb[:, dc, fc * 128:(fc + 1) * 128],
                                   rx[:, dc, :], start=(dc == 0), stop=(dc == 7))
                pb.inc("a_kmm", ins)
            pb.on("tensor", mmk)

            def evk(v, ts=ts, fc=fc):
                pb.wait(v, "a_kmm")
                kev_after[(ts, fc)] = pb.inc("a_kev", v.tensor_copy(kstage[:, fc, :], psV[:, fc % 2, :]))
            pb.on("vector", evk)

        def stk(sync, ts=ts):
            pb.wait(sync, "a_kev")
            pb.inc("a_p1done", sync.dma_start(
                k_dram_r[:, :, ts * 512:(ts + 1) * 512], kstage[:, :, :]), 16)
        pb.on("sync", stk)

    for qs in range(4):
        def ldxq(g, qs=qs):
            pb.wait(g, "a_p1done")
            if qs >= 1:
                pb.wait(g, "a_qev", qev_after[(qs - 1, 7)])
            pb.inc("a_xldg", g.dma_start(xfm[:, :, :], xq_r[:, :, qs * 512:(qs + 1) * 512]), 16)
        pb.on("gpsimd", ldxq)

        def ldxq2(sync, qs=qs):
            pb.wait(sync, "a_p1done")
            if qs >= 1:
                pb.wait(sync, "a_qev", qev_after[(qs - 1, 7)])
            pb.inc("a_xld", sync.dma_start(ctab[:, :, :], cosq_r[:, :, qs * 512:(qs + 1) * 512]), 16)
            pb.inc("a_xld", sync.dma_start(stab[:, :, :], sinq_r[:, :, qs * 512:(qs + 1) * 512]), 16)
        pb.on("sync", ldxq2)

        def wrope2(v):
            pb.wait(v, "a_xld")
            pb.wait(v, "a_xldg")
            v.drain()
        pb.on("vector", wrope2)
        emit_rope(pb, rx, xfm, ctab, stab, tmp1, tmp2, f"rq{qs}")

        for fc in range(8):
            def mmq(t, qs=qs, fc=fc):
                pb.wait(t, "a_rope")
                if qs == 0 and fc < 2:
                    pb.wait(t, "a_kev")
                elif fc < 2:
                    pb.wait(t, "a_qev", qev_after[(qs - 1, 6 + fc)])
                else:
                    pb.wait(t, "a_qev", qev_after[(qs, fc - 2)])
                for dc in range(8):
                    ins = t.matmul(psV[:, fc % 2, :], wqb[:, dc, fc * 128:(fc + 1) * 128],
                                   rx[:, dc, :], start=(dc == 0), stop=(dc == 7))
                pb.inc("a_qmm", ins)
            pb.on("tensor", mmq)

            def evq(v, qs=qs, fc=fc):
                pb.wait(v, "a_qmm")
                if fc == 0:
                    pb.wait(v, "a_qst")
                qev_after[(qs, fc)] = pb.inc("a_qev", v.tensor_copy(
                    kstage[:, fc, :], psV[:, fc % 2, :]))
            pb.on("vector", evq)

        def stq(sync, qs=qs):
            pb.wait(sync, "a_qev", qev_after[(qs, 7)])
            pb.inc("a_qst", sync.dma_start(
                q_dram_r[:, :, qs * 512:(qs + 1) * 512], kstage[:, :, :]), 16)
        pb.on("sync", stq)

    es1.close()

    with contextlib.ExitStack() as es2:
        psSP = _ps(nc, es2, "a_psSP", [128, 2, 512], F32)
        psPO = _ps(nc, es2, "a_psPO", [128, 2, 1024], F32)

        qk_slot_end = {}
        for s in range(NT):
            Ds = DEPTH_PLAN[s]

            def ldq(sync, s=s):
                pb.wait(sync, "a_qst")
                if s >= 2:
                    pb.wait(sync, "a_qkmm", qk_slot_end[s - 2])
                pb.inc("a_qld", sync.dma_start(qslot[:, s % 2, :, :],
                                               q_dram_r[:, :, s * 128:(s + 1) * 128]), 16)
            pb.on("sync", ldq)

            def genmask(v, s=s):
                v.drain()
                v.tensor_scalar(maskE[:, :], kabs[:, :], dgapE[:, s:s + 1], None, op0=ALU.is_le)
                v.drain()
                pb.inc("a_mgen", v.tensor_scalar(maskO[:, :], kabs[:, :], dgapO[:, s:s + 1], None, op0=ALU.is_le))
            pb.on("vector", genmask)

            exp_after = {}
            msk_after = {}
            pvh_after = {}
            qk_after = {}
            for kt in range(Ds):
                bf = kt % 2
                def ldkv(g, s=s, kt=kt, bf=bf):
                    pb.wait(g, "a_p1done")
                    pb.wait(g, "a_vst")
                    if kt >= 2:
                        pb.wait(g, "a_qkmm", qk_after[(kt - 2, 15)])
                        pb.wait(g, "a_pvmm", pvh_after[(kt - 2, 15)])
                    else:
                        pb.wait(g, "a_pvmm")
                    pb.inc("a_kvld", g.dma_start(kslab[:, bf, :, :],
                                                 k_dram_r[:, :, kt * 512:(kt + 1) * 512]), 16)
                    pb.inc("a_kvld", g.dma_start(
                        vslab[:, bf, :, :, :].rearrange("p a h c -> p a (h c)"),
                        v_dram_r[:, kt * 4:(kt + 1) * 4, :]), 16)
                pb.on("gpsimd", ldkv)

                for hh in range(16):
                    sp = hh % 2
                    base = 64 * (hh % 2)
                    fc = hh // 2
                    def mmqk(t, s=s, kt=kt, hh=hh, sp=sp, base=base, fc=fc, bf=bf):
                        pb.wait(t, "a_kvld")
                        pb.wait(t, "a_qld")
                        pb.wait(t, "a_exp")
                        for kc in range(4):
                            ins = t.matmul(psSP[:, sp, kc * 128:(kc + 1) * 128],
                                           kslab[base:base + 64, bf, fc, kc * 128:(kc + 1) * 128],
                                           qslot[base:base + 64, s % 2, fc, :],
                                           start=(kc == 0), stop=(kc == 3))
                        qk_after[(kt, hh)] = pb.inc("a_qkmm", ins)
                    pb.on("tensor", mmqk)

                    def expp(sx, s=s, kt=kt, hh=hh, sp=sp):
                        pb.wait(sx, "a_qkmm", qk_after[(kt, hh)])
                        pb.wait(sx, "a_pvmm")
                        pb.wait(sx, "a_msk")
                        exp_after[(kt, hh)] = pb.inc("a_exp", sx.activation(
                            ptsb[:, hh % 2, :], psSP[:, sp, :], AF.Exp, scale=0.125,
                            bias=visb[:, s, kt:kt + 1]))
                    pb.on("scalar", expp)

                    do_masks = (kt == POS_E[s]) or (kt == POS_O[s])
                    if do_masks:
                        def mskv(v, s=s, kt=kt, hh=hh):
                            pb.wait(v, "a_exp", exp_after[(kt, hh)])
                            pb.wait(v, "a_mgen")
                            ins = None
                            if kt == POS_E[s]:
                                ins = v.tensor_tensor(ptsb[:, hh % 2, :], ptsb[:, hh % 2, :],
                                                      maskE[:, :], op=ALU.mult)
                                v.drain()
                            if kt == POS_O[s]:
                                ins = v.tensor_tensor(ptsb[:, hh % 2, :], ptsb[:, hh % 2, :],
                                                      maskO[:, :], op=ALU.mult)
                                v.drain()
                            msk_after[(kt, hh)] = pb.inc("a_msk", ins)
                        pb.on("vector", mskv)

                    def mmpv(t, s=s, kt=kt, hh=hh, do_masks=do_masks, bf=bf, Ds=Ds):
                        if do_masks:
                            pb.wait(t, "a_msk", msk_after[(kt, hh)])
                        else:
                            pb.wait(t, "a_exp", exp_after[(kt, hh)])
                        if kt == 0 and hh == 0:
                            pb.wait(t, "a_epi")
                        bank = hh // 8
                        col = (hh % 8) * 128
                        first = (kt == 0 and hh % 4 == 0)
                        last = (kt == Ds - 1 and hh % 4 == 3)
                        for kc in range(4):
                            ins = t.matmul(psPO[:, bank, col:col + 65],
                                           ptsb[:, hh % 2, kc * 128:(kc + 1) * 128],
                                           vslab[:, bf, kc, hh, :],
                                           start=(first and kc == 0), stop=(last and kc == 3))
                        pvh_after[(kt, hh)] = pb.inc("a_pvmm", ins)
                    pb.on("tensor", mmpv)

            for hh in range(16):
                def epi(v, s=s, hh=hh):
                    pb.wait(v, "a_pvmm")
                    if hh == 0 and s > 0:
                        pb.wait(v, "a_aost")
                    v.drain()
                    v.reciprocal(rcp[:, :], psPO[:, hh // 8, (hh % 8) * 128 + 64:(hh % 8) * 128 + 65])
                    v.drain()
                    pb.inc("a_epi", v.tensor_scalar(
                        attn_sb[:, hh * 64:(hh + 1) * 64],
                        psPO[:, hh // 8, (hh % 8) * 128:(hh % 8) * 128 + 64],
                        rcp[:, :], None, op0=ALU.mult))
                pb.on("vector", epi)

            def sta(sync, s=s):
                pb.wait(sync, "a_epi")
                pb.inc("a_aost", sync.dma_start(ao_dram[s * 128:(s + 1) * 128, :], attn_sb[:, :]), 16)
            pb.on("sync", sta)
            qk_slot_end[s] = pb.val("a_qkmm")

    tps = _ps(nc, es, "a_tps", [128, 2, 1024], BF16)
    abf = _sb(nc, es, "a_abf", [128, 1024], BF16)
    aT = _sb(nc, es, "a_aT", [128, 8, 128], BF16)
    bobc = _sb(nc, es, "a_bobc", [128, 1024], F32)
    rt = _sb(nc, es, "a_rt", [128, 1024], F32)

    def ldwo(g):
        pb.wait(g, "a_qmm")
        pb.wait(g, "a_qst")
        pb.wait(g, "a_vst")
        pb.inc("a_wld2", g.dma_start(wob[:, :, :], wo_r[:, :, :]), 16)
        pb.inc("a_wld2", g.dma_start(bobc[:, :], io["bo_bc"][:, :]), 16)
    pb.on("gpsimd", ldwo)

    def ldstream(sync):
        for i in range(NT):
            pb.inc("a_sld", sync.dma_start(stream[:, i, :],
                                           io["x_tm"][i * 128:(i + 1) * 128, :]), 16)
    pb.on("sync", ldstream)

    evO_after = {}
    tre2_after = {}
    for i in range(NT):
        def lda(sync, i=i):
            pb.wait(sync, "a_aost")
            pb.wait(sync, "a_trm2")
            pb.inc("a_ald", sync.dma_start(abf[:, :], ao_dram[i * 128:(i + 1) * 128, :]), 16)
        pb.on("sync", lda)

        def trp(t, i=i):
            pb.wait(t, "a_ald")
            if i >= 2:
                pb.wait(t, "a_tre2", tre2_after[i - 2])
            for dc in range(7):
                t.transpose(tps[:, i % 2, dc * 128:(dc + 1) * 128],
                            abf[:, dc * 128:(dc + 1) * 128], identb[:, :])
            pb.inc("a_trm2", t.transpose(tps[:, i % 2, 7 * 128:8 * 128],
                                         abf[:, 7 * 128:8 * 128], identb[:, :]))
        pb.on("tensor", trp)

        def tev(v, i=i):
            pb.wait(v, "a_trm2")
            pb.wait(v, "a_omm")
            tre2_after[i] = pb.inc("a_tre2", v.tensor_copy(
                aT[:, :, :], tps[:, i % 2, :].rearrange("p (a b) -> p a b", a=8)))
        pb.on("vector", tev)

        def mmo(t, i=i):
            pb.wait(t, "a_wld2")
            pb.wait(t, "a_tre2", tre2_after[i])
            pb.wait(t, "a_evO", evO_after.get(i - 1, 0))
            for hf in range(2):
                for dc in range(8):
                    ins = t.matmul(psV[:, hf, :], aT[:, dc, :],
                                   wob[:, dc, hf * 512:(hf + 1) * 512],
                                   start=(dc == 0), stop=(dc == 7))
            pb.inc("a_omm", ins)
        pb.on("tensor", mmo)

        def evo(v, i=i):
            pb.wait(v, "a_omm")
            pb.wait(v, "a_sld")
            pb.wait(v, "ac")
            for hf in range(2):
                v.tensor_tensor(rt[:, hf * 512:(hf + 1) * 512], psV[:, hf, :],
                                bobc[:, hf * 512:(hf + 1) * 512], op=ALU.add)
            v.drain()
            for hf in range(2):
                ins = v.tensor_tensor(rt[:, hf * 512:(hf + 1) * 512],
                                      rt[:, hf * 512:(hf + 1) * 512],
                                      stream[:, i, hf * 512:(hf + 1) * 512], op=ALU.add)
            evO_after[i] = pb.inc("a_evO", ins)
        pb.on("vector", evo)

        emit_ln(pb, sc, rt[:, :], stream[:, i, :], ready=("a_evO", evO_after[i]))


def stage_attn(inp):
    d = {}
    perm = PERM
    Wq = np.asarray(inp["L0_attn_Wq"], np.float32)
    Wk = np.asarray(inp["L0_attn_Wk"], np.float32)
    Wv = np.asarray(inp["L0_attn_Wv"], np.float32)
    Wo = np.asarray(inp["L0_attn_Wo"], np.float32)
    bo = np.asarray(inp["L0_attn_bo"], np.float32)
    d["wqT"] = np.ascontiguousarray(Wq[:, perm].T)
    d["wkT"] = np.ascontiguousarray(Wk[:, perm].T)
    d["wvT"] = np.ascontiguousarray(Wv[:, perm].T)
    d["woT"] = np.ascontiguousarray(Wo.T)
    d["bo_bc"] = np.ascontiguousarray(np.broadcast_to(bo, (128, 1024)))
    import ml_dtypes
    theta = (10000.0 ** (-np.arange(0, D, 2, dtype=np.float64) / D))
    ang = np.arange(T, dtype=np.float64)[:, None] * theta[None, :]
    d["cos_k"] = np.ascontiguousarray(np.cos(ang).T.astype(ml_dtypes.bfloat16))
    d["sin_k"] = np.ascontiguousarray(np.sin(ang).T.astype(ml_dtypes.bfloat16))
    i_ = np.arange(128)[:, None, None]
    c_ = np.arange(4)[None, :, None]
    j_ = np.arange(128)[None, None, :]
    d["kabs"] = np.ascontiguousarray((c_ * 128 + i_ - j_).reshape(128, 512).astype(np.float32))
    d["ident"] = np.eye(128, dtype=np.float32)
    return d


def stage_attn_core(inp, core, shared):
    import ml_dtypes
    b, half = core // 2, core % 2
    x = np.asarray(inp["L0_attn_input"], np.float32)[b]
    own = _own_tokens(half)
    d = {}
    d["x_tm"] = np.ascontiguousarray(x[own])
    d["x_fmp"] = np.ascontiguousarray(x[:, PERM].T)
    d["x_fmq"] = np.ascontiguousarray(x[own][:, PERM].T)
    theta = (10000.0 ** (-np.arange(0, D, 2, dtype=np.float64) / D))
    angq = own[:, None].astype(np.float64) * theta[None, :]
    d["cos_q"] = np.ascontiguousarray(np.cos(angq).T.astype(ml_dtypes.bfloat16))
    d["sin_q"] = np.ascontiguousarray(np.sin(angq).T.astype(ml_dtypes.bfloat16))
    chunks = _chunks_of(half)
    qa = np.array([c * 128 for c in chunks])
    dtrue = np.array([(c * 128 + 128 + 511) // 512 for c in chunks])
    visb = np.zeros((128, NT, 8), np.float32)
    for s in range(NT):
        visb[:, s, dtrue[s]:] = -1e30
    d["visb"] = visb
    dgE = np.zeros((128, NT), np.float32)
    dgO = np.zeros((128, NT), np.float32)
    for s in range(NT):
        pe, po = POS_E[s], POS_O[s]
        dgE[:, s] = (qa[s] - pe * 512) if pe == dtrue[s] - 1 else 1e9
        dgO[:, s] = (qa[s] - po * 512) if po == dtrue[s] - 1 else 1e9
    d["dgapE"] = dgE
    d["dgapO"] = dgO
    return d


def build_attn_launch(shared):
    nc = _new_nc()
    hs = {k: nc.dram_tensor(k, list(v.shape), BF16 if v.dtype.kind == "V" or str(v.dtype) == "bfloat16" else F32,
                            kind="ExternalInput").ap()
          for k, v in shared.items()}
    percore_shapes = {"x_tm": (TOK, D), "x_fmp": (D, T), "x_fmq": (D, TOK),
                      "cos_q": (512, TOK), "sin_q": (512, TOK),
                      "visb": (128, NT, 8), "dgapE": (128, NT), "dgapO": (128, NT)}
    for k, shp in percore_shapes.items():
        dt = BF16 if k in ("cos_q", "sin_q") else F32
        hs[k] = nc.dram_tensor(k, list(shp), dt, kind="ExternalInput").ap()
    y_h = nc.dram_tensor("y_out", [TOK, D], F32, kind="ExternalOutput").ap()
    es = contextlib.ExitStack()
    with es:
        pb = PB(nc, es)
        stream = _sb(nc, es, "stream", [128, NT, 1024], F32)
        identb = _sb(nc, es, "identb", [128, 128], BF16)
        csts = emit_const_pool(pb, nc, es, [EPS, 1.0])
        lnsc = mk_ln_scratch(nc, es, "ln_", csts[EPS])

        def ldi(g):
            pb.inc("wld", g.dma_start(identb[:, :], hs["ident"][:, :]), 16)
        pb.on("gpsimd", ldi)

        io = dict(hs)
        io.update(stream=stream, identb=identb, lnsc=lnsc)
        emit_attn(pb, es, io)
        emit_stream_store(pb, stream, y_h, ["lnw"])
    return nc


def emit_moe(pb, es, io):
    nc = pb.nc
    stream = io["stream"]
    sc = io["lnsc"]
    identb = io["identb"]
    NE = 8

    with contextlib.ExitStack() as e1:
        xT = _sb(nc, e1, "m_xT", [128, 8, TOK], BF16)
        acc = _sb(nc, e1, "m_acc", [128, NT, 1024], BF16)
        hbuf = _sb(nc, e1, "m_hbuf", [128, 16, 512], BF16)
        xbf = _sb(nc, e1, "m_xbf", [128, 1024], BF16)
        w1bi = _sb(nc, e1, "m_w1bi", [128, 2, 8, 128], BF16)
        w1bg = _sb(nc, e1, "m_w1bg", [128, 2, 8, 128], BF16)
        w2b = _sb(nc, e1, "m_w2b", [128, 16, 512], BF16)
        wgb = _sb(nc, e1, "m_wgb", [128, 8, 8], BF16)
        wgf = _sb(nc, e1, "m_wgf", [128, 8, 8], F32)
        wglo = _sb(nc, e1, "m_wglo", [128, 8, 8], BF16)
        xlo = _sb(nc, e1, "m_xlo", [128, 1024], BF16)
        xloT = _sb(nc, e1, "m_xloT", [128, 8, 128], BF16)
        logit = _sb(nc, e1, "m_logit", [128, NT, 8], F32)
        wcoef = _sb(nc, e1, "m_wcoef", [128, NT, 8], F32)
        mk1s = _sb(nc, e1, "m_mk1s", [128, NT, 8], F32)
        wcB = _sb(nc, e1, "m_wcB", [128, NT, 8], F32)
        pcnt = _sb(nc, e1, "m_pcnt", [128, NT, 8], F32)
        utri = _sb(nc, e1, "m_utri", [128, 128], F32)
        worder = _sb(nc, e1, "m_worder", [16, 2], F32)
        ones_c = _sb(nc, e1, "m_onesc", [128, 1], F32)
        ones_r = _sb(nc, e1, "m_onesr", [1, 128], F32)
        sums_sb = _sb(nc, e1, "m_sums", [1, 16, 8], F32)
        cum_sb = _sb(nc, e1, "m_cum", [1, 16, 8], F32)
        tot2 = _sb(nc, e1, "m_tot2", [1, 2, 8], F32)
        gath = _sb(nc, e1, "m_gath", [16, 8], F32)
        offs2 = _sb(nc, e1, "m_offs2", [2, 8], F32)
        offsr = _sb(nc, e1, "m_offsr", [1, 16], F32)
        rowfull = _sb(nc, e1, "m_rowfull", [1, 16, 8], F32)
        rowb = _sb(nc, e1, "m_rowb", [128, 16, 8], F32)
        posb = _sb(nc, e1, "m_posb", [128, 8], F32)
        kp = _sb(nc, e1, "m_kp", [128, 8], F32)
        ccin = nc.dram_tensor("m_ccin", [2, 8], F32, kind="Internal").ap()
        ccout = nc.dram_tensor("m_ccout", [16, 8], F32, kind="Internal").ap()
        t2a = _sb(nc, e1, "m_t2a", [128, 2, 512], BF16)
        v1c = _sb(nc, e1, "m_v1", [128, 1], F32)
        v2c = _sb(nc, e1, "m_v2", [128, 1], F32)
        g1c = _sb(nc, e1, "m_g1", [128, 1], F32)
        g2c = _sb(nc, e1, "m_g2", [128, 1], F32)
        mk1 = _sb(nc, e1, "m_mk1", [128, 8], F32)
        mk2 = _sb(nc, e1, "m_mk2", [128, 8], F32)
        rt = _sb(nc, e1, "m_rt", [128, 1024], F32)
        tps = _ps(nc, e1, "m_tps", [128, 2, 1024], BF16)
        psA = _ps(nc, e1, "m_psA", [128, 2, 512], F32)
        psB = _ps(nc, e1, "m_psB", [128, 2, 512], F32)
        psO = _ps(nc, e1, "m_psO", [128, 2, 512], F32)

        w1r = io["moe_w1"].rearrange("e (dc p) f -> e p dc f", p=128)
        w2r = io["moe_w2"].rearrange("e (ic p) d -> e p ic d", p=128)
        wgr = io["moe_wg"].rearrange("(dc p) f -> p dc f", p=128)

        def ld0(g):
            pb.inc("m_wld", g.dma_start(wgb[:, :, :], wgr[:, :, :]), 16)
            pb.inc("m_wld", g.dma_start(wgf[:, :, :], wgr[:, :, :]), 16)
            pb.inc("m_cst", g.memset(acc[:, :, :], 0.0))
        pb.on("gpsimd", ld0)

        def wglo_mk(v):
            pb.wait(v, "m_wld")
            pb.inc("m_wgl", v.tensor_tensor(wglo[:, :, :], wgf[:, :, :], wgb[:, :, :],
                                            op=ALU.subtract))
        pb.on("vector", wglo_mk)

        gev_after = {}
        for i in range(NT):
            def vc(v, i=i):
                pb.wait(v, "trm")
                v.tensor_copy(xbf[:, :], stream[:, i, :])
                v.drain()
                pb.inc("trc", v.tensor_tensor(xlo[:, :], stream[:, i, :], xbf[:, :],
                                              op=ALU.subtract))
            pb.on("vector", vc)

            def tt(t, i=i):
                pb.wait(t, "trc")
                pb.wait(t, "m_trel")
                pb.wait(t, "tre", max(0, pb.val("tre") - 1))
                for dc in range(8):
                    t.transpose(tps[:, i % 2, dc * 128:(dc + 1) * 128],
                                xbf[:, dc * 128:(dc + 1) * 128], identb[:, :])
                pb.inc("trm", t.transpose(tps[:, (i + 1) % 2, 0:128],
                                          xlo[:, 0:128], identb[:, :]))
            pb.on("tensor", tt)

            def ve(v, i=i):
                pb.wait(v, "trm")
                pb.inc("tre", v.tensor_copy(
                    xT[:, :, i * 128:(i + 1) * 128],
                    tps[:, i % 2, :].rearrange("p (a b) -> p a b", a=8)))
            pb.on("vector", ve)

            def tt2(t, i=i):
                pb.wait(t, "tre")
                for dc in range(1, 8):
                    t.transpose(tps[:, (i + 1) % 2, dc * 128:(dc + 1) * 128],
                                xlo[:, dc * 128:(dc + 1) * 128], identb[:, :])
                pb.inc("m_trml", t.transpose(tps[:, (i + 1) % 2, 0:128],
                                             xlo[:, 0:128], identb[:, :]))
            pb.on("tensor", tt2)

            def ve2(v, i=i):
                pb.wait(v, "m_trml")
                pb.wait(v, "m_gmm")
                pb.inc("m_trel", v.tensor_copy(
                    xloT[:, :, :], tps[:, (i + 1) % 2, :].rearrange("p (a b) -> p a b", a=8)))
            pb.on("vector", ve2)

            def mmg(t, i=i):
                pb.wait(t, "m_wld")
                pb.wait(t, "m_wgl")
                pb.wait(t, "m_trel")
                if i >= 1:
                    pb.wait(t, "m_gev", gev_after[i - 1])
                for dc in range(8):
                    t.matmul(psA[:, i % 2, 0:8], xT[:, dc, i * 128:(i + 1) * 128],
                             wgb[:, dc, :], start=(dc == 0), stop=False)
                for dc in range(8):
                    t.matmul(psA[:, i % 2, 0:8], xT[:, dc, i * 128:(i + 1) * 128],
                             wglo[:, dc, :], start=False, stop=False)
                for dc in range(8):
                    ins = t.matmul(psA[:, i % 2, 0:8], xloT[:, dc, :],
                                   wgb[:, dc, :], start=False, stop=(dc == 7))
                pb.inc("m_gmm", ins)
            pb.on("tensor", mmg)

            def evg(v, i=i):
                pb.wait(v, "m_gmm")
                gev_after[i] = pb.inc("m_gev", v.tensor_copy(logit[:, i, :], psA[:, i % 2, 0:8]))
            pb.on("vector", evg)

        for i in range(NT):
            def route(v, i=i):
                pb.wait(v, "m_gev")
                lg = logit[:, i, :]
                v.drain()
                v.tensor_reduce(v1c[:, :], lg, axis=AX, op=ALU.max)
                v.drain()
                v.tensor_scalar(mk1s[:, i, :], lg, v1c[:, :], None, op0=ALU.is_ge)
                v.drain()
                v.scalar_tensor_tensor(mk2[:, :], mk1s[:, i, :], -1e30, lg, op0=ALU.mult, op1=ALU.add)
                v.drain()
                v.tensor_reduce(v2c[:, :], mk2[:, :], axis=AX, op=ALU.max)
                v.drain()
                v.tensor_scalar(mk2[:, :], mk2[:, :], v2c[:, :], None, op0=ALU.is_ge)
                v.drain()
                pb.inc("m_rt1", v.tensor_tensor(v1c[:, :], v1c[:, :], v2c[:, :], op=ALU.subtract))
            pb.on("vector", route)

            def sg(sx, i=i):
                pb.wait(sx, "m_rt1")
                pb.inc("m_sg", sx.activation(g1c[:, :], v1c[:, :], AF.Sigmoid))
            pb.on("scalar", sg)

            def route2(v, i=i):
                pb.wait(v, "m_sg")
                v.tensor_scalar(g2c[:, :], g1c[:, :], -1.0, 1.0, op0=ALU.mult, op1=ALU.add)
                v.drain()
                v.tensor_scalar(wcoef[:, i, :], mk1s[:, i, :], g1c[:, :], None, op0=ALU.mult)
                v.tensor_scalar(wcB[:, i, :], mk2[:, :], g2c[:, :], None, op0=ALU.mult)
                v.drain()
                pb.inc("m_rt2", v.tensor_tensor(pcnt[:, i, :], mk1s[:, i, :], mk2[:, :], op=ALU.add))
            pb.on("vector", route2)

        CAP = 5120.0

        def ldcap(g):
            pb.inc("m_capld", g.dma_start(utri[:, :], io["utri"][:, :]), 16)
            pb.inc("m_capld", g.dma_start(worder[:, :], io["worder"][:, :]), 16)
            g.memset(ones_c[:, :], 1.0)
            pb.inc("m_capc", g.memset(ones_r[:, :], 1.0))
        pb.on("gpsimd", ldcap)

        def mmsum(t):
            pb.wait(t, "m_rt2")
            pb.wait(t, "m_capc")
            pb.wait(t, "m_capld")
            for si in range(NT):
                ins = t.matmul(psB[0:1, 0, si * 8:(si + 1) * 8], ones_c[:, :], pcnt[:, si, :],
                               start=(si == 0), stop=(si == NT - 1))
            pb.inc("m_summ", ins)
        pb.on("tensor", mmsum)

        def evsum(v):
            pb.wait(v, "m_summ")
            v.tensor_copy(sums_sb[0:1, :, :], psB[0:1, 0, 0:128].rearrange("p (a b) -> p a b", a=16))
            v.drain()
            v.memset(cum_sb[0:1, 0, :], 0.0)
            v.memset(cum_sb[0:1, 8, :], 0.0)
            v.drain()
            for u in range(2):
                for si in range(1, 8):
                    sl = u * 8 + si
                    v.tensor_tensor(cum_sb[0:1, sl, :], cum_sb[0:1, sl - 1, :],
                                    sums_sb[0:1, sl - 1, :], op=ALU.add)
                    v.drain()
            pb.inc("m_cum", v.tensor_tensor(
                tot2[0:1, :, :], cum_sb[0:1, 7::8, :], sums_sb[0:1, 7::8, :], op=ALU.add))
        pb.on("vector", evsum)

        def ccst(sync):
            pb.wait(sync, "m_cum")
            pb.inc("m_ccst", sync.dma_start(ccin, tot2[0:1, :, :].rearrange("p a b -> p (a b)")), 16)
        pb.on("sync", ccst)

        def ccgo(g):
            pb.wait(g, "m_ccst")
            g.collective_compute("AllGather", ALU.bypass,
                                 replica_groups=[list(range(NCORE))],
                                 ins=[ccin], outs=[ccout]).then_inc(pb.semh("m_cc")[0], 1)
            pb.semh("m_cc")[1] += 1
        pb.on("gpsimd", ccgo)

        def ccld(sync):
            pb.wait(sync, "m_cc")
            pb.inc("m_ccld", sync.dma_start(gath[:, :], ccout), 16)
        pb.on("sync", ccld)

        def mmoff(t):
            pb.wait(t, "m_ccld")
            pb.inc("m_offm", t.matmul(psA[0:2, 0, 0:8], worder[:, :], gath[:, :],
                                      start=True, stop=True))
        pb.on("tensor", mmoff)

        def evoff(v):
            pb.wait(v, "m_offm")
            pb.inc("m_offev", v.tensor_copy(offs2[:, :], psA[0:2, 0, 0:8]))
        pb.on("vector", evoff)

        def offdma(sync):
            pb.wait(sync, "m_offev")
            pb.inc("m_offfl", sync.dma_start(offsr[:, :], offs2[:, :]), 16)
        pb.on("sync", offdma)

        def mkrow(v):
            pb.wait(v, "m_offfl")
            orow = bass.AP(offsr, 0, [[16, 1], [8, 2], [0, 8], [1, 8]])
            pb.inc("m_rowf", v.tensor_tensor(
                rowfull[0:1, :, :].rearrange("p (u s) e -> p u s e", u=2),
                cum_sb[0:1, :, :].rearrange("p (u s) e -> p u s e", u=2),
                orow, op=ALU.add))
        pb.on("vector", mkrow)

        def mmrow(t):
            pb.wait(t, "m_rowf")
            pb.inc("m_rbm", t.matmul(psA[:, 1, 0:128], ones_r[:, :],
                                     rowfull[0:1, :, :].rearrange("p a b -> p (a b)"),
                                     start=True, stop=True))
        pb.on("tensor", mmrow)

        def evrow(v):
            pb.wait(v, "m_rbm")
            pb.inc("m_rbev", v.tensor_copy(rowb[:, :, :],
                                           psA[:, 1, 0:128].rearrange("p (a b) -> p a b", a=16)))
        pb.on("vector", evrow)

        pos_after = {}
        for si in range(NT):
            def mmpos(t, si=si):
                pb.wait(t, "m_rbev")
                if si >= 2:
                    pb.wait(t, "m_wcap", pos_after[si - 2])
                ins = t.matmul(psB[:, si % 2, 0:8], utri[:, :], pcnt[:, si, :],
                               start=True, stop=True)
                pb.inc("m_posm", ins)
            pb.on("tensor", mmpos)

            def evpos(v, si=si):
                pb.wait(v, "m_posm")
                v.tensor_tensor(posb[:, :], psB[:, si % 2, 0:8], rowb[:, si, :], op=ALU.add)
                v.drain()
                v.tensor_scalar(kp[:, :], posb[:, :], CAP, None, op0=ALU.is_lt)
                v.drain()
                v.tensor_tensor(wcoef[:, si, :], wcoef[:, si, :], kp[:, :], op=ALU.mult)
                v.drain()
                v.tensor_tensor(posb[:, :], posb[:, :], mk1s[:, si, :], op=ALU.add)
                v.drain()
                v.tensor_scalar(kp[:, :], posb[:, :], CAP, None, op0=ALU.is_lt)
                v.drain()
                v.tensor_tensor(kp[:, :], wcB[:, si, :], kp[:, :], op=ALU.mult)
                v.drain()
                pos_after[si] = pb.inc("m_wcap", v.tensor_tensor(
                    wcoef[:, si, :], wcoef[:, si, :], kp[:, :], op=ALU.add))
            pb.on("vector", evpos)

        NSL = 4
        for e in range(NE):
            for sl in range(NSL):
                mmh_after, evh_after = {}, {}
                for fi in range(16):
                    bf = fi % 2
                    def ldw1(g, e=e, sl=sl, fi=fi, bf=bf):
                        if fi >= 2:
                            pb.wait(g, "m_hmm", mmh_after[fi - 2])
                        else:
                            pb.wait(g, "m_hmm")
                        pb.inc("m_w1ld", g.dma_start(w1bi[:, bf], w1r[e, :, :, fi * 128:(fi + 1) * 128]), 16)
                        pb.inc("m_w1ld", g.dma_start(w1bg[:, bf], w1r[e, :, :, 2048 + fi * 128:2048 + (fi + 1) * 128]), 16)
                    pb.on("gpsimd", ldw1)

                    def mmh(t, e=e, sl=sl, fi=fi, bf=bf):
                        pb.wait(t, "m_w1ld")
                        pb.wait(t, "tre")
                        if fi >= 2:
                            pb.wait(t, "m_hev", evh_after[fi - 2])
                        else:
                            pb.wait(t, "m_hev")
                            pb.wait(t, "m_gev")
                            pb.wait(t, "m_wcap", pb.val("m_wcap"))
                        for dc in range(8):
                            t.matmul(psA[:, bf], w1bi[:, bf, dc, :], xT[:, dc, sl * 512:(sl + 1) * 512],
                                     start=(dc == 0), stop=(dc == 7))
                        for dc in range(8):
                            ins = t.matmul(psB[:, bf], w1bg[:, bf, dc, :], xT[:, dc, sl * 512:(sl + 1) * 512],
                                           start=(dc == 0), stop=(dc == 7))
                        mmh_after[fi] = pb.inc("m_hmm", ins)
                    pb.on("tensor", mmh)

                    def evh1(sx, e=e, fi=fi, bf=bf):
                        pb.wait(sx, "m_hmm", mmh_after[fi])
                        if fi >= 2:
                            pb.wait(sx, "m_hev", evh_after[fi - 2])
                        pb.inc("m_hact", sx.activation(t2a[:, bf, :], psA[:, bf], AF.Gelu))
                    pb.on("scalar", evh1)

                    def evh2(v, e=e, fi=fi, bf=bf):
                        pb.wait(v, "m_hact")
                        pb.wait(v, "m_omm")
                        evh_after[fi] = pb.inc("m_hev", v.tensor_tensor(
                            hbuf[:, fi, :], t2a[:, bf, :], psB[:, bf], op=ALU.mult))
                    pb.on("vector", evh2)

                for hf in range(2):
                    def ldw2(g, e=e, sl=sl, hf=hf):
                        pb.wait(g, "m_omm")
                        pb.inc("m_w2ld", g.dma_start(w2b[:, :, :],
                                                     w2r[e][:, :, hf * 512:(hf + 1) * 512]), 16)
                    pb.on("gpsimd", ldw2)
                    for tt in range(4):
                        def mmo(t, e=e, sl=sl, tt=tt, hf=hf):
                            pb.wait(t, "m_w2ld")
                            pb.wait(t, "m_hev")
                            pb.wait(t, "m_oev", max(0, pb.val("m_oev") - 1))
                            for ic in range(16):
                                ins = t.matmul(psO[:, tt % 2],
                                               hbuf[:, ic, tt * 128:(tt + 1) * 128],
                                               w2b[:, ic, :],
                                               start=(ic == 0), stop=(ic == 15))
                            pb.inc("m_omm", ins)
                        pb.on("tensor", mmo)

                        def evo(v, e=e, sl=sl, tt=tt, hf=hf):
                            pb.wait(v, "m_omm")
                            pb.wait(v, "m_wcap")
                            pb.wait(v, "m_cst")
                            ti = sl * 4 + tt
                            pb.inc("m_oev", v.scalar_tensor_tensor(
                                acc[:, ti, hf * 512:(hf + 1) * 512], psO[:, tt % 2],
                                wcoef[:, ti, e:e + 1],
                                acc[:, ti, hf * 512:(hf + 1) * 512], op0=ALU.mult, op1=ALU.add))
                        pb.on("vector", evo)

        for i in range(NT):
            def evr(v, i=i):
                pb.wait(v, "m_oev")
                pb.wait(v, "ac")
                pb.inc("m_res", v.tensor_tensor(rt[:, :], acc[:, i, :], stream[:, i, :], op=ALU.add))
            pb.on("vector", evr)
            emit_ln(pb, sc, rt[:, :], stream[:, i, :], ready=("m_res", pb.val("m_res")))


def stage_moe(inp):
    d = {}
    d["moe_wg"] = _t(inp["L0_ffn_Wg"])
    d["moe_w1"] = np.ascontiguousarray(np.asarray(inp["L0_ffn_W1"], np.float32))
    d["moe_w2"] = np.ascontiguousarray(np.asarray(inp["L0_ffn_W2"], np.float32))
    d["ident"] = np.eye(128, dtype=np.float32)
    d["utri"] = np.triu(np.ones((128, 128), np.float32), 1)
    return d


def _run_seq_pos(core, run):
    b, half = core // 2, core % 2
    if half == 0:
        return 4 * b + (0 if run == 0 else 3)
    return 4 * b + (1 if run == 0 else 2)


def stage_moe_core(core):
    w = np.zeros((16, 2), np.float32)
    for rc in range(NCORE):
        for ru in range(2):
            j = rc * 2 + ru
            for myu in range(2):
                if _run_seq_pos(rc, ru) < _run_seq_pos(core, myu):
                    w[j, myu] = 1.0
    return {"worder": w}


def build_moe_launch(staged):
    nc = _new_nc()
    hs = {k: _dram_in(nc, k, v) for k, v in staged.items()}
    hs["worder"] = nc.dram_tensor("worder", [16, 2], F32, kind="ExternalInput").ap()
    x_h = nc.dram_tensor("x_in", [TOK, D], F32, kind="ExternalInput").ap()
    y_h = nc.dram_tensor("y_out", [TOK, D], F32, kind="ExternalOutput").ap()
    es = contextlib.ExitStack()
    with es:
        pb = PB(nc, es)
        stream = _sb(nc, es, "stream", [128, NT, 1024], F32)
        identb = _sb(nc, es, "identb", [128, 128], BF16)
        csts = emit_const_pool(pb, nc, es, [EPS, 1.0])
        lnsc = mk_ln_scratch(nc, es, "ln_", csts[EPS])

        def ldi(g):
            pb.inc("wld", g.dma_start(identb[:, :], hs["ident"][:, :]), 16)
        pb.on("gpsimd", ldi)

        emit_stream_load(pb, stream, x_h)
        io = dict(hs)
        io.update(stream=stream, identb=identb, lnsc=lnsc)
        emit_moe(pb, es, io)
        emit_stream_store(pb, stream, y_h, ["lnw"])
    return nc


def emit_mamba(pb, es, io):
    nc = pb.nc
    stream = io["stream"]
    sc = io["lnsc"]
    identb = io["identb"]

    z_dram = nc.dram_tensor("c_zdram", [TOK, 2048], BF16, kind="Internal").ap()
    xbcr_dram = nc.dram_tensor("c_xbcrdram", [128, 18, 2, 1024], BF16, kind="Internal").ap()
    y_dram = nc.dram_tensor("c_ydram", [TOK, 2048], F32, kind="Internal").ap()
    hl_in = nc.dram_tensor("c_hlin", [128, 108], BF16, kind="Internal").ap()
    hl_out = nc.dram_tensor("c_hlout", [256, 108], BF16, kind="Internal").ap()
    st_in = nc.dram_tensor("c_stin", [257, 2048], F32, kind="Internal").ap()
    st_out = nc.dram_tensor("c_stout", [514, 2048], F32, kind="Internal").ap()
    io["_dbg"] = {"y": y_dram, "z": z_dram, "x": xbcr_dram}

    ipr = io["c_ipT"].rearrange("(dc p) f -> p dc f", p=128)
    opr = io["c_opT"].rearrange("(ic p) f -> p ic f", p=128)

    dtr = _sb(nc, es, "c_dtr", [128, NT, 32], F32)
    acum_all = _sb(nc, es, "c_acumall", [128, NT, 32], F32)
    cdrow_all = _sb(nc, es, "c_cdrow", [1, NT, 32], F32)
    Call = _sb(nc, es, "c_Call", [128, NT, 128], BF16)
    SiniA = _sb(nc, es, "c_SiniA", [128, 2048], F32)
    SiniB = _sb(nc, es, "c_SiniB", [128, 2048], F32)
    identf = _sb(nc, es, "c_identf", [128, 128], F32)
    ones_r = _sb(nc, es, "c_onesr", [1, 128], F32)
    rsum = _sb(nc, es, "c_rsum", [1, 32], F32)
    rrow = _sb(nc, es, "c_rrow", [1, 32], F32)
    crho = _sb(nc, es, "c_crho", [128, 1], F32)
    crho2 = _sb(nc, es, "c_crho2", [1, 1], F32)
    cw1 = _sb(nc, es, "c_cw1", [128, 1], F32)
    cdsel = _sb(nc, es, "c_cdsel", [1, 1], F32)
    cdseli = _sb(nc, es, "c_cdseli", [1, 1], F32)
    cselA = _sb(nc, es, "c_cselA", [128, 1], F32)
    cselE = _sb(nc, es, "c_cselE", [128, 1], F32)
    cselO = _sb(nc, es, "c_cselO", [128, 1], F32)
    haloA = _sb(nc, es, "c_haloA", [128, 18, 3], BF16)
    haloB = _sb(nc, es, "c_haloB", [128, 18, 3], BF16)
    tps = _ps(nc, es, "c_tps", [128, 2, 1024], BF16)
    psR = _ps(nc, es, "c_psR", [128, 1024], F32)
    psY = _ps(nc, es, "c_psY", [128, 512], F32)
    psYo = _ps(nc, es, "c_psYo", [128, 512], F32)
    psU = _ps(nc, es, "c_psU", [128, 512], F32)

    def ld0(g):
        pb.inc("c_wld", g.dma_start(identf[:, :], io["ident"][:, :]), 16)
        pb.inc("c_wld", g.dma_start(crho[:, :], io["c_rho"][:, :]), 16)
        pb.inc("c_wld", g.dma_start(crho2[:, :], io["c_rho2"][:, :]), 16)
        pb.inc("c_wld", g.dma_start(cw1[:, :], io["c_w1"][:, :]), 16)
        pb.inc("c_wld", g.dma_start(cdsel[:, :], io["c_dsel"][:, :]), 16)
        pb.inc("c_wld", g.dma_start(cdseli[:, :], io["c_dseli"][:, :]), 16)
        pb.inc("c_wld", g.dma_start(cselA[:, :], io["c_selA"][:, :]), 16)
        pb.inc("c_wld", g.dma_start(cselE[:, :], io["c_selE"][:, :]), 16)
        pb.inc("c_wld", g.dma_start(cselO[:, :], io["c_selO"][:, :]), 16)
        pb.inc("c_cst", g.memset(ones_r[:, :], 1.0))
    pb.on("gpsimd", ld0)

    with contextlib.ExitStack() as e1:
        xn = _sb(nc, e1, "c_xn", [128, NT, 1024], BF16)
        nwb = _sb(nc, e1, "c_nwb", [128, 1024], F32)
        nbb = _sb(nc, e1, "c_nbb", [128, 1024], F32)
        ipb = _sb(nc, e1, "c_ipb", [128, 8, 2048], BF16)
        xnT = _sb(nc, e1, "c_xnT", [128, 8, 512], BF16)
        xbf1 = _sb(nc, e1, "c_xbf1", [128, 1024], BF16)
        zst = _sb(nc, e1, "c_zst", [128, 2048], BF16)
        xbcst = _sb(nc, e1, "c_xbcst", [128, 18, 512], BF16)

        def ldnw(g):
            pb.inc("c_wld", g.dma_start(nwb[:, :], io["c_normw"][:, :]), 16)
            pb.inc("c_wld", g.dma_start(nbb[:, :], io["c_normb"][:, :]), 16)
        pb.on("gpsimd", ldnw)

        for i in range(NT):
            def wv(v, i=i):
                pb.wait(v, "c_wld")
                pb.inc("c_lnx", v.tensor_copy(sc["lnt"][:, 0:1], stream[:, i, 0:1]))
            pb.on("vector", wv)
            emit_ln(pb, sc, stream[:, i, :], xn[:, i, :],
                    ready=("c_lnx", pb.val("c_lnx")), gain_bc=nwb[:, :], bias_bc=nbb[:, :])

        for sl in range(4):
            tiles = list(range(sl * 4, sl * 4 + 4))
            for pos, i in enumerate(tiles):
                def vc(v, i=i):
                    pb.wait(v, "lnw")
                    pb.wait(v, "c_trm")
                    pb.inc("c_trc", v.tensor_copy(xbf1[:, :], xn[:, i, :]))
                pb.on("vector", vc)

                def tt(t, pos=pos):
                    pb.wait(t, "c_trc")
                    pb.wait(t, "c_tre", max(0, pb.val("c_tre") - 1))
                    for dc in range(7):
                        t.transpose(tps[:, pos % 2, dc * 128:(dc + 1) * 128],
                                    xbf1[:, dc * 128:(dc + 1) * 128], identb[:, :])
                    pb.inc("c_trm", t.transpose(tps[:, pos % 2, 896:1024],
                                                xbf1[:, 896:1024], identb[:, :]))
                pb.on("tensor", tt)

                def ve(v, pos=pos):
                    pb.wait(v, "c_trm")
                    pb.inc("c_tre", v.tensor_copy(
                        xnT[:, :, pos * 128:(pos + 1) * 128],
                        tps[:, pos % 2, :].rearrange("p (a b) -> p a b", a=8)))
                pb.on("vector", ve)

            def ldz(g, sl=sl):
                pb.wait(g, "c_imm")
                pb.inc("c_ipld", g.dma_start(ipb[:, :, 0:2048], ipr[:, :, 0:2048]), 16)
            pb.on("gpsimd", ldz)
            for tt_ in range(4):
                for cg in range(4):
                    def mmz(t, sl=sl, tt_=tt_, cg=cg):
                        pb.wait(t, "c_ipld")
                        pb.wait(t, "c_tre")
                        pb.wait(t, "c_zev", max(0, pb.val("c_zev") - 1))
                        for dc in range(8):
                            ins = t.matmul(psR[:, (cg % 2) * 512:(cg % 2) * 512 + 512],
                                           xnT[:, dc, tt_ * 128:(tt_ + 1) * 128],
                                           ipb[:, dc, cg * 512:(cg + 1) * 512],
                                           start=(dc == 0), stop=(dc == 7))
                        pb.inc("c_imm", ins)
                    pb.on("tensor", mmz)

                    def evz(v, sl=sl, tt_=tt_, cg=cg):
                        pb.wait(v, "c_imm")
                        if cg == 0:
                            pb.wait(v, "c_zst")
                        pb.inc("c_zev", v.tensor_copy(zst[:, cg * 512:(cg + 1) * 512],
                                                      psR[:, (cg % 2) * 512:(cg % 2) * 512 + 512]))
                    pb.on("vector", evz)

                def stz(sync, sl=sl, tt_=tt_):
                    pb.wait(sync, "c_zev")
                    i = sl * 4 + tt_
                    pb.inc("c_zst", sync.dma_start(z_dram[i * 128:(i + 1) * 128, :], zst[:, :]), 16)
                pb.on("sync", stz)

            def ldxbc(g, sl=sl):
                pb.wait(g, "c_imm")
                pb.inc("c_ipld", g.dma_start(ipb[:, :, 0:2048], ipr[:, :, 2048:4096]), 16)
            pb.on("gpsimd", ldxbc)
            for fc in range(16):
                def mmx(t, sl=sl, fc=fc):
                    pb.wait(t, "c_ipld")
                    pb.wait(t, "c_xev", max(0, pb.val("c_xev") - 1))
                    for dc in range(8):
                        ins = t.matmul(psR[:, (fc % 2) * 512:(fc % 2) * 512 + 512],
                                       ipb[:, dc, fc * 128:(fc + 1) * 128],
                                       xnT[:, dc, :], start=(dc == 0), stop=(dc == 7))
                    pb.inc("c_imm", ins)
                pb.on("tensor", mmx)

                def evx(v, sl=sl, fc=fc):
                    pb.wait(v, "c_imm")
                    if fc == 0:
                        pb.wait(v, "c_xst")
                    pb.inc("c_xev", v.tensor_copy(xbcst[:, fc, :],
                                                  psR[:, (fc % 2) * 512:(fc % 2) * 512 + 512]))
                pb.on("vector", evx)

            def ldxbc2(g, sl=sl):
                pb.wait(g, "c_imm")
                pb.inc("c_ipld", g.dma_start(ipb[:, :, 0:256], ipr[:, :, 4096:4352]), 16)
            pb.on("gpsimd", ldxbc2)
            for fc in range(16, 18):
                def mmx(t, sl=sl, fc=fc):
                    pb.wait(t, "c_ipld")
                    pb.wait(t, "c_xev", max(0, pb.val("c_xev") - 1))
                    for dc in range(8):
                        ins = t.matmul(psR[:, (fc % 2) * 512:(fc % 2) * 512 + 512],
                                       ipb[:, dc, (fc - 16) * 128:(fc - 15) * 128],
                                       xnT[:, dc, :], start=(dc == 0), stop=(dc == 7))
                    pb.inc("c_imm", ins)
                pb.on("tensor", mmx)

                def evx(v, sl=sl, fc=fc):
                    pb.wait(v, "c_imm")
                    pb.inc("c_xev", v.tensor_copy(xbcst[:, fc, :],
                                                  psR[:, (fc % 2) * 512:(fc % 2) * 512 + 512]))
                pb.on("vector", evx)

            def stx(sync, sl=sl):
                pb.wait(sync, "c_xev")
                pb.inc("c_xst", sync.dma_start(
                    xbcr_dram[:, :, sl // 2, (sl % 2) * 512:(sl % 2) * 512 + 512],
                    xbcst[:, :, :]), 16)
            pb.on("sync", stx)

            def lddt(g, sl=sl):
                pb.wait(g, "c_imm")
                pb.inc("c_ipld", g.dma_start(ipb[:, :, 0:32], ipr[:, :, 4352:4384]), 16)
            pb.on("gpsimd", lddt)
            for tt_ in range(4):
                def mmd(t, sl=sl, tt_=tt_):
                    pb.wait(t, "c_ipld")
                    pb.wait(t, "c_dev", max(0, pb.val("c_dev") - 1))
                    for dc in range(8):
                        ins = t.matmul(psR[:, (tt_ % 2) * 512:(tt_ % 2) * 512 + 32],
                                       xnT[:, dc, tt_ * 128:(tt_ + 1) * 128],
                                       ipb[:, dc, 0:32], start=(dc == 0), stop=(dc == 7))
                    pb.inc("c_imm", ins)
                pb.on("tensor", mmd)

                def evd(v, sl=sl, tt_=tt_):
                    pb.wait(v, "c_imm")
                    pb.inc("c_dev", v.tensor_copy(dtr[:, sl * 4 + tt_, :],
                                                  psR[:, (tt_ % 2) * 512:(tt_ % 2) * 512 + 32]))
                pb.on("vector", evd)

    def hput(sync):
        pb.wait(sync, "c_xst")
        pb.inc("c_hst", sync.dma_start(
            hl_in.rearrange("p (k f) -> p k f", k=2)[:, 0, :].rearrange("p (fc c) -> p fc c", fc=18),
            xbcr_dram[:, :, 0, 1021:1024]), 16)
        pb.inc("c_hst", sync.dma_start(
            hl_in.rearrange("p (k f) -> p k f", k=2)[:, 1, :].rearrange("p (fc c) -> p fc c", fc=18),
            xbcr_dram[:, :, 1, 1021:1024]), 16)
    pb.on("sync", hput)

    def hcc(g):
        pb.wait(g, "c_hst")
        g.collective_compute("AllGather", ALU.bypass,
                             replica_groups=[[0, 1], [2, 3], [4, 5], [6, 7]],
                             ins=[hl_in], outs=[hl_out]).then_inc(pb.semh("c_hcc")[0], 1)
        pb.semh("c_hcc")[1] += 1
    pb.on("gpsimd", hcc)

    with contextlib.ExitStack() as e2:
        heven = _sb(nc, e2, "c_heven", [128, 2, 18, 3], BF16)
        hodd = _sb(nc, e2, "c_hodd", [128, 2, 18, 3], BF16)

        def hget(sync):
            pb.wait(sync, "c_hcc")
            pb.inc("c_hld", sync.dma_start(
                heven[:, :, :, :].rearrange("p k fc c -> p (k fc c)"), hl_out[0:128, :]), 16)
            pb.inc("c_hld", sync.dma_start(
                hodd[:, :, :, :].rearrange("p k fc c -> p (k fc c)"), hl_out[128:256, :]), 16)
        pb.on("sync", hget)

        def hblend(v):
            pb.wait(v, "c_hld")
            pb.wait(v, "c_wld")
            v.tensor_scalar(haloA[:, :, :], heven[:, 0], cselA[:, :], None, op0=ALU.mult)
            v.drain()
            v.tensor_scalar(haloB[:, :, :], hodd[:, 1], cselE[:, :], None, op0=ALU.mult)
            v.drain()
            pb.inc("c_halo", v.scalar_tensor_tensor(
                haloB[:, :, :].rearrange("p a b -> p (a b)"),
                hodd[:, 0].rearrange("p a b -> p (a b)"),
                cselO[:, :], haloB[:, :, :].rearrange("p a b -> p (a b)"),
                op0=ALU.mult, op1=ALU.add))
        pb.on("vector", hblend)

    with contextlib.ExitStack() as e3:
        xconv = _sb(nc, e3, "c_xconv", [128, 18, 515], BF16)
        xbcp = _sb(nc, e3, "c_xbcp", [128, 18, 512], BF16)
        convw = _sb(nc, e3, "c_convw_s", [128, 18, 4], F32)
        convb = _sb(nc, e3, "c_convb_s", [128, 18], F32)
        acc1 = _sb(nc, e3, "c_acc1", [128, 512], F32)
        acc2 = _sb(nc, e3, "c_acc2", [128, 512], F32)
        dtb = _sb(nc, e3, "c_dtb_s", [128, 32], F32)
        Albc = _sb(nc, e3, "c_Albc", [128, 32], F32)
        Abc = _sb(nc, e3, "c_Abc", [128, 32], F32)
        Dbc = _sb(nc, e3, "c_Dbc", [128, 32], F32)
        dt_tm = _sb(nc, e3, "c_dttm", [128, 32], F32)
        sp1 = _sb(nc, e3, "c_sp1", [128, 32], F32)
        sp2 = _sb(nc, e3, "c_sp2", [128, 32], F32)
        a_tm = _sb(nc, e3, "c_atm", [128, 32], F32)
        acum_fm = _sb(nc, e3, "c_acumfm", [32, 128], F32)
        acum_flat = _sb(nc, e3, "c_acumflat", [1, 4096], F32)
        ea_tm = _sb(nc, e3, "c_eatm", [128, 32], F32)
        dd_tm = _sb(nc, e3, "c_ddtm", [128, 32], F32)
        decay_tm = _sb(nc, e3, "c_decaytm", [128, 32], F32)
        dxr = _sb(nc, e3, "c_dxr", [1, 32], F32)
        dbc = _sb(nc, e3, "c_dbc", [128, 32], F32)
        cdxr = _sb(nc, e3, "c_cdxr", [1, 32], F32)
        cdbc = _sb(nc, e3, "c_cdbc", [128, 32], F32)
        S = _sb(nc, e3, "c_S", [128, 2048], F32)
        Gm = _sb(nc, e3, "c_Gm", [128, 128], BF16)
        Btm = _sb(nc, e3, "c_Btm", [128, 128], BF16)
        xs_tm = _sb(nc, e3, "c_xstm", [128, 2048], BF16)
        X1 = _sb(nc, e3, "c_X1", [128, 2048], BF16)
        X2 = _sb(nc, e3, "c_X2", [128, 2048], BF16)
        D1 = _sb(nc, e3, "c_D1", [128, 1024], F32)
        Ebuf = _sb(nc, e3, "c_Ebuf", [128, 1024], BF16)
        Mq = _sb(nc, e3, "c_Mq", [128, 1024], BF16)
        ytile = _sb(nc, e3, "c_ytile", [128, 2048], F32)
        C32f = _sb(nc, e3, "c_C32f", [128, 128], F32)
        ytmp = _sb(nc, e3, "c_ytmp", [128, 512], F32)
        ytmp2 = _sb(nc, e3, "c_ytmp2", [128, 512], F32)

        def ldc(g):
            pb.inc("c_s0", g.memset(S[:, :], 0.0))
            pb.inc("c_wld", g.dma_start(convw[:, :, :], io["c_convw"][:, :, :]), 16)
            pb.inc("c_wld", g.dma_start(convb[:, :], io["c_convb"][:, :]), 16)
            pb.inc("c_wld", g.dma_start(dtb[:, :], io["c_dtb"][:, :]), 16)
            pb.inc("c_wld", g.dma_start(Albc[:, :], io["c_Alog"][:, :]), 16)
            pb.inc("c_wld", g.dma_start(Dbc[:, :], io["c_D"][:, :]), 16)
        pb.on("gpsimd", ldc)

        def mkA(sx):
            pb.wait(sx, "c_wld")
            pb.inc("c_ac0", sx.activation(Abc[:, :], Albc[:, :], AF.Exp, scale=1.0))
        pb.on("scalar", mkA)

        def mkA2(v):
            pb.wait(v, "c_ac0")
            pb.inc("c_ac1", v.tensor_scalar(Abc[:, :], Abc[:, :], -1.0, None, op0=ALU.mult))
        pb.on("vector", mkA2)

        for sl in range(4):
            buf = sl % 2
            runstart = (sl % 2 == 0)
            run = sl // 2
            def ldcv(sync, sl=sl, run=run, runstart=runstart):
                pb.wait(sync, "c_xst")
                pb.wait(sync, "c_cv1")
                if runstart:
                    pb.inc("c_cvld", sync.dma_start(xconv[:, :, 3:515],
                                                    xbcr_dram[:, :, run, 0:512]), 16)
                else:
                    pb.inc("c_cvld", sync.dma_start(xconv[:, :, 0:515],
                                                    xbcr_dram[:, :, run, 509:1024]), 16)
            pb.on("sync", ldcv)

            if runstart:
                def cph(v, sl=sl, run=run):
                    pb.wait(v, "c_cvld")
                    pb.wait(v, "c_halo")
                    src = haloA if run == 0 else haloB
                    pb.inc("c_cvh", v.tensor_copy(xconv[:, :, 0:3], src[:, :, :]))
                pb.on("vector", cph)

            for fc in range(18):
                def cvx(v, sl=sl, fc=fc, buf=buf):
                    pb.wait(v, "c_cvld")
                    pb.wait(v, "c_cvh")
                    pb.wait(v, "c_wld")
                    pb.wait(v, "c_xsm")
                    pb.wait(v, "c_gm")
                    pb.wait(v, "c_btm")
                    pb.wait(v, "c_c32")
                    pb.wait(v, "c_cv2")
                    v.tensor_scalar(acc1[:, :], xconv[:, fc, 0:512], convw[:, fc, 0:1],
                                    None, op0=ALU.mult)
                    v.drain()
                    v.scalar_tensor_tensor(acc2[:, :], xconv[:, fc, 1:513], convw[:, fc, 1:2],
                                           acc1[:, :], op0=ALU.mult, op1=ALU.add)
                    v.drain()
                    v.scalar_tensor_tensor(acc1[:, :], xconv[:, fc, 2:514], convw[:, fc, 2:3],
                                           acc2[:, :], op0=ALU.mult, op1=ALU.add)
                    v.drain()
                    pb.inc("c_cv1", v.scalar_tensor_tensor(
                        acc2[:, :], xconv[:, fc, 3:515], convw[:, fc, 3:4],
                        acc1[:, :], op0=ALU.mult, op1=ALU.add))
                pb.on("vector", cvx)

                def cvs(sx, sl=sl, fc=fc, buf=buf):
                    pb.wait(sx, "c_cv1")
                    ins = sx.activation(xbcp[:, fc, :], acc2[:, :], AF.Silu,
                                        bias=convb[:, fc:fc + 1])
                    pb.inc("c_cv2", ins)
                    if fc == 17:
                        pb.inc("c_cvdone", sx.activation(
                            xbcp[:, 17, 0:1], acc2[:, 0:1], AF.Silu,
                            bias=convb[:, 17:18]))
                pb.on("scalar", cvs)

            for s4 in range(4):
                s = sl * 4 + s4
                ccol = s4 * 128
                def dta(v, s=s):
                    pb.wait(v, "c_cv2")
                    pb.wait(v, "c_ac1")
                    if s == 1 and io.get("_dbg0"):
                        pb.wait(v, "c_d0")
                    pb.inc("c_dt0", v.tensor_tensor(dt_tm[:, :], dtr[:, s, :], dtb[:, :], op=ALU.add))
                pb.on("vector", dta)

                def dtb_(sx, s=s):
                    pb.wait(sx, "c_dt0")
                    sx.activation(sp1[:, :], dt_tm[:, :], AF.Abs)
                    sx.drain()
                    sx.activation(sp1[:, :], sp1[:, :], AF.Exp, scale=-1.0)
                    sx.drain()
                    sx.activation(sp1[:, :], sp1[:, :], AF.Ln, bias=sc["one"])
                    sx.drain()
                    pb.inc("c_dt1", sx.activation(sp2[:, :], dt_tm[:, :], AF.Relu))
                pb.on("scalar", dtb_)

                def dtc(v, s=s):
                    pb.wait(v, "c_dt1")
                    v.tensor_tensor(dt_tm[:, :], sp1[:, :], sp2[:, :], op=ALU.add)
                    v.drain()
                    pb.inc("c_dt2", v.tensor_tensor(a_tm[:, :], dt_tm[:, :], Abc[:, :], op=ALU.mult))
                pb.on("vector", dtc)

                def mmac(t, s=s):
                    pb.wait(t, "c_dt2")
                    pb.wait(t, "c_wld")
                    pb.wait(t, "c_D2")
                    pb.wait(t, "c_acev", max(0, pb.val("c_acev")))
                    pb.inc("c_acm", t.matmul(psR[:, 0:32], io["utri_incl_sb"][:, :],
                                             a_tm[:, :], start=True, stop=True))
                pb.on("tensor", mmac)

                def evac_(v, s=s):
                    pb.wait(v, "c_acm")
                    pb.inc("c_acev", v.tensor_copy(acum_all[:, s, :], psR[:, 0:32]))
                pb.on("vector", evac_)

                def mmtr(t, s=s):
                    pb.wait(t, "c_acev")
                    pb.wait(t, "c_ypsfree", max(0, pb.val("c_ypsfree")))
                    pb.wait(t, "c_fmev", max(0, pb.val("c_fmev")))
                    pb.inc("c_trf", t.transpose(psYo[0:32, 0:128], acum_all[:, s, :], identf[:, :]))
                pb.on("tensor", mmtr)

                def evfm(v, s=s):
                    pb.wait(v, "c_trf")
                    pb.inc("c_fmev", v.tensor_copy(acum_fm[:, :], psYo[0:32, 0:128]))
                pb.on("vector", evfm)

                def flat(sync, s=s):
                    pb.wait(sync, "c_fmev")
                    pb.inc("c_flat", sync.dma_start(acum_flat[:, :], acum_fm[:, :]), 16)
                pb.on("sync", flat)

                def rsm(v, s=s):
                    pb.wait(v, "c_flat")
                    if s == 0:
                        v.memset(rsum[:, :], 0.0)
                        v.drain()
                    if s == 8:
                        v.tensor_scalar(rsum[:, :], rsum[:, :], crho2[:, :], None, op0=ALU.mult)
                        v.drain()
                    v.tensor_copy(cdrow_all[:, s, :], rsum[:, :])
                    v.drain()
                    arow = bass.AP(acum_flat, 127, [[4096, 1], [128, 32]])
                    pb.inc("c_rsm", v.tensor_tensor(rsum[:, :], rsum[:, :], arow, op=ALU.add))
                pb.on("vector", rsm)

                def eaa(sx, s=s):
                    pb.wait(sx, "c_acev")
                    ins = sx.activation(ea_tm[:, :], acum_all[:, s, :], AF.Exp)
                    pb.inc("c_ea", ins)
                pb.on("scalar", eaa)

                def mmlast(t, s=s):
                    pb.wait(t, "c_flat")
                    pb.wait(t, "c_cst")
                    pb.wait(t, "c_dcev", max(0, pb.val("c_dcev")))
                    arow = bass.AP(acum_flat, 127, [[4096, 1], [128, 32]])
                    pb.inc("c_lbm", t.matmul(psR[:, 512:544], ones_r[:, :], arow,
                                             start=True, stop=True))
                pb.on("tensor", mmlast)

                def dca(v, s=s):
                    pb.wait(v, "c_lbm")
                    pb.inc("c_dc0", v.tensor_tensor(dd_tm[:, :], psR[:, 512:544],
                                                    acum_all[:, s, :], op=ALU.subtract))
                pb.on("vector", dca)

                def dcb(sx, s=s):
                    pb.wait(sx, "c_dc0")
                    arow = bass.AP(acum_flat, 127, [[4096, 1], [128, 32]])
                    sx.activation(decay_tm[:, :], dd_tm[:, :], AF.Exp)
                    sx.drain()
                    pb.inc("c_dc1", sx.activation(dxr[:, :], arow, AF.Exp))
                pb.on("scalar", dcb)

                def mmdbc(t, s=s):
                    pb.wait(t, "c_dc1")
                    pb.inc("c_dbm", t.matmul(psR[:, 544:576], ones_r[:, :], dxr[:, :],
                                             start=True, stop=True))
                pb.on("tensor", mmdbc)

                def dcc(v, s=s):
                    pb.wait(v, "c_dbm")
                    pb.inc("c_dcev", v.tensor_copy(dbc[:, :], psR[:, 544:576]))
                pb.on("vector", dcc)

                for g2 in range(2):
                    def ttx(t, s=s, buf=buf, ccol=ccol, g2=g2):
                        pb.wait(t, "c_cv2")
                        if s == 1 and io.get("_dbg0"):
                            pb.wait(t, "c_d0")
                        pb.wait(t, "c_btev")
                        pb.wait(t, "c_xsev", max(0, pb.val("c_xsev") - 1))
                        for ic in range(8):
                            ins = t.transpose(tps[:, g2, ic * 128:(ic + 1) * 128],
                                              xbcp[:, g2 * 8 + ic, ccol:ccol + 128],
                                              identb[:, :])
                        pb.inc("c_xsm", ins)
                    pb.on("tensor", ttx)

                    def evxs(v, s=s, g2=g2):
                        pb.wait(v, "c_xsm")
                        pb.inc("c_xsev", v.tensor_copy(
                            xs_tm[:, g2 * 1024:(g2 + 1) * 1024], tps[:, g2, :]))
                    pb.on("vector", evxs)

                def mkX(v, s=s):
                    pb.wait(v, "c_xsev")
                    pb.wait(v, "c_dt2")
                    pb.wait(v, "c_dcev")
                    dtap = bass.AP(dt_tm, 0, [[32, 128], [1, 32], [0, 64]])
                    v.tensor_tensor(X1[:, :].rearrange("p (h q) -> p h q", h=32), xs_tm[:, :].rearrange("p (h q) -> p h q", h=32), dtap, op=ALU.mult)
                    v.drain()
                    dcap = bass.AP(decay_tm, 0, [[32, 128], [1, 32], [0, 64]])
                    pb.inc("c_X", v.tensor_tensor(X2[:, :].rearrange("p (h q) -> p h q", h=32), X1[:, :].rearrange("p (h q) -> p h q", h=32), dcap, op=ALU.mult))
                pb.on("vector", mkX)

                def evc32(v, s=s, buf=buf, ccol=ccol):
                    pb.wait(v, "c_cv2")
                    v.tensor_copy(Call[:, s, :], xbcp[:, 17, ccol:ccol + 128])
                    v.drain()
                    pb.inc("c_c32", v.tensor_copy(C32f[:, :], xbcp[:, 17, ccol:ccol + 128]))
                pb.on("vector", evc32)

                def mmg(t, s=s, buf=buf, ccol=ccol):
                    pb.wait(t, "c_cv2")
                    pb.wait(t, "c_gev", max(0, pb.val("c_gev")))
                    pb.inc("c_gm", t.matmul(psR[:, 0:128], xbcp[:, 16, ccol:ccol + 128],
                                            xbcp[:, 17, ccol:ccol + 128],
                                            start=True, stop=True))
                pb.on("tensor", mmg)

                def evg(v, s=s):
                    pb.wait(v, "c_gm")
                    pb.inc("c_gev", v.tensor_tensor(Gm[:, :], psR[:, 0:128],
                                                    io["triu01_sb"][:, :], op=ALU.mult))
                pb.on("vector", evg)

                def mmbt(t, s=s, buf=buf, ccol=ccol):
                    pb.wait(t, "c_cv2")
                    pb.wait(t, "c_btev", max(0, pb.val("c_btev")))
                    pb.inc("c_btm", t.transpose(tps[:, 0, 0:128],
                                                xbcp[:, 16, ccol:ccol + 128], identb[:, :]))
                pb.on("tensor", mmbt)

                def evbt(v, s=s):
                    pb.wait(v, "c_btm")
                    pb.inc("c_btev", v.tensor_copy(Btm[:, :], tps[:, 0, 0:128]))
                pb.on("vector", evbt)

                if s == 8:
                    def rho8(v):
                        pb.wait(v, "c_supd")
                        pb.wait(v, "c_snap")
                        pb.inc("c_supd", v.tensor_scalar(S[:, :], S[:, :], crho[:, :],
                                                         None, op0=ALU.mult))
                    pb.on("vector", rho8)

                for q in range(4):
                    def mmrow(t, s=s, q=q):
                        pb.wait(t, "c_flat")
                        pb.wait(t, "c_gev")
                        pb.wait(t, "c_dcev")
                        pb.wait(t, "c_D2")
                        for j in range(2):
                            ins = t.matmul(psR[:, j * 512:(j + 1) * 512], ones_r[:, :],
                                           acum_flat[:, q * 1024 + j * 512:q * 1024 + (j + 1) * 512],
                                           start=True, stop=True)
                        pb.inc("c_row", ins)
                    pb.on("tensor", mmrow)

                    def mkE(v, s=s, q=q):
                        pb.wait(v, "c_row")
                        acap = bass.AP(acum_all, s * 32 + q * 8, [[NT * 32, 128], [1, 8], [0, 128]])
                        v.tensor_tensor(D1[:, :].rearrange("p (h l) -> p h l", h=8),
                                        psR[:, 0:1024].rearrange("p (h l) -> p h l", h=8),
                                        acap, op=ALU.subtract)
                        v.drain()
                        pb.inc("c_D2", v.tensor_scalar(D1[:, :], D1[:, :], 0.0, None, op0=ALU.min))
                    pb.on("vector", mkE)

                    def expE(sx, s=s, q=q):
                        pb.wait(sx, "c_D2")
                        pb.inc("c_E", sx.activation(Ebuf[:, :], D1[:, :], AF.Exp))
                    pb.on("scalar", expE)

                    def mkM(v, s=s, q=q):
                        pb.wait(v, "c_E")
                        pb.wait(v, "c_gev")
                        gap = bass.AP(Gm, 0, [[128, 128], [0, 8], [1, 128]])
                        pb.inc("c_M", v.tensor_tensor(
                            Mq[:, :].rearrange("p (h l) -> p h l", h=8),
                            Ebuf[:, :].rearrange("p (h l) -> p h l", h=8), gap, op=ALU.mult))
                    pb.on("vector", mkM)

                    def mmyd(t, s=s, q=q):
                        pb.wait(t, "c_M")
                        pb.wait(t, "c_X")
                        pb.wait(t, "c_ypsfree", max(0, pb.val("c_ypsfree")))
                        for hh in range(8):
                            ins = t.matmul(psY[:, hh * 64:(hh + 1) * 64],
                                           Mq[:, hh * 128:(hh + 1) * 128],
                                           X1[:, (q * 8 + hh) * 64:(q * 8 + hh + 1) * 64],
                                           start=(hh == 0), stop=(hh == 7))
                        pb.inc("c_ydm", ins)
                    pb.on("tensor", mmyd)

                    def mmyo(t, s=s, q=q):
                        pb.wait(t, "c_c32")
                        pb.wait(t, "c_s0")
                        pb.wait(t, "c_supd")
                        pb.wait(t, "c_ypsfree", max(0, pb.val("c_ypsfree")))
                        pb.inc("c_yom", t.matmul(psYo[:, :], C32f[:, :],
                                                 S[:, q * 512:(q + 1) * 512],
                                                 start=True, stop=True))
                    pb.on("tensor", mmyo)

                    def mmu(t, s=s, q=q):
                        pb.wait(t, "c_btev")
                        pb.wait(t, "c_X")
                        pb.wait(t, "c_supd", max(0, pb.val("c_supd")))
                        pb.inc("c_um", t.matmul(psU[:, :], Btm[:, :],
                                                X2[:, q * 512:(q + 1) * 512],
                                                start=True, stop=True))
                    pb.on("tensor", mmu)

                    def yasm(v, s=s, q=q):
                        pb.wait(v, "c_ydm")
                        pb.wait(v, "c_yom")
                        pb.wait(v, "c_ea")
                        eap = bass.AP(ea_tm, q * 8, [[32, 128], [1, 8], [0, 64]])
                        v.tensor_tensor(ytmp[:, :].rearrange("p (h w) -> p h w", h=8),
                                        psYo[:, :].rearrange("p (h w) -> p h w", h=8),
                                        eap, op=ALU.mult)
                        v.drain()
                        pb.inc("c_ypsfree", v.tensor_tensor(ytmp[:, :], ytmp[:, :], psY[:, :], op=ALU.add))
                        v.drain()
                        dap = bass.AP(Dbc, q * 8, [[32, 128], [1, 8], [0, 64]])
                        v.tensor_tensor(ytmp2[:, :].rearrange("p (h w) -> p h w", h=8),
                                        xs_tm[:, q * 512:(q + 1) * 512].rearrange("p (h w) -> p h w", h=8),
                                        dap, op=ALU.mult)
                        v.drain()
                        pb.inc("c_yas", v.tensor_tensor(ytile[:, q * 512:(q + 1) * 512],
                                                        ytmp[:, :], ytmp2[:, :], op=ALU.add))
                    pb.on("vector", yasm)

                    def supd(v, s=s, q=q):
                        pb.wait(v, "c_um")
                        pb.wait(v, "c_yom")
                        v.tensor_tensor(S[:, q * 512:(q + 1) * 512].rearrange("p (h w) -> p h w", h=8),
                                        S[:, q * 512:(q + 1) * 512].rearrange("p (h w) -> p h w", h=8),
                                        bass.AP(dbc, q * 8, [[32, 128], [1, 8], [0, 64]]), op=ALU.mult)
                        v.drain()
                        pb.inc("c_supd", v.tensor_tensor(
                            S[:, q * 512:(q + 1) * 512], S[:, q * 512:(q + 1) * 512],
                            psU[:, :], op=ALU.add))
                    pb.on("vector", supd)

                def sty(sync, s=s):
                    pb.wait(sync, "c_yas")
                    pb.inc("c_yst", sync.dma_start(y_dram[s * 128:(s + 1) * 128, :], ytile[:, :]), 16)
                pb.on("sync", sty)

                if s == 0 and io.get("_dbg0"):
                    def d0(sync):
                        pb.wait(sync, "c_yas")
                        pb.inc("c_d0", sync.dma_start(io["_dbg0"]["x1"], X1[:, :]), 16)
                        pb.inc("c_d0", sync.dma_start(io["_dbg0"]["xstm"], xs_tm[:, :]), 16)
                        pb.inc("c_d0", sync.dma_start(io["_dbg0"]["gm"], Gm[:, :]), 16)
                        pb.inc("c_d0", sync.dma_start(io["_dbg0"]["acum"], acum_all[:, 0, :]), 16)
                        pb.inc("c_d0", sync.dma_start(io["_dbg0"]["dt"], dt_tm[:, :]), 16)
                        pb.inc("c_d0", sync.dma_start(io["_dbg0"]["xbcp"], xbcp[:, :, :]), 16)
                    pb.on("sync", d0)

                if s == 7 or s == 15:
                    def snap(sync, s=s):
                        pb.wait(sync, "c_supd")
                        row = 0 if s == 7 else 128
                        pb.inc("c_snap", sync.dma_start(st_in[row:row + 128, :], S[:, :]), 16)
                    pb.on("sync", snap)

        def sprep(v):
            pb.wait(v, "c_rsm")
            v.drain()
            v.memset(acum_flat[0:1, 0:2048], 0.0)
            v.drain()
            pb.inc("c_sprep", v.tensor_copy(acum_flat[0:1, 0:32], rsum[:, :]))
        pb.on("vector", sprep)

        def sput(sync):
            pb.wait(sync, "c_snap")
            pb.wait(sync, "c_sprep")
            pb.inc("c_sst", sync.dma_start(st_in[256:257, :], acum_flat[0:1, 0:2048]), 16)
        pb.on("sync", sput)

        def scc(g):
            pb.wait(g, "c_sst")
            g.collective_compute("AllGather", ALU.bypass,
                                 replica_groups=[[0, 1], [2, 3], [4, 5], [6, 7]],
                                 ins=[st_in], outs=[st_out]).then_inc(pb.semh("c_scc")[0], 1)
            pb.semh("c_scc")[1] += 1
        pb.on("gpsimd", scc)

        def sget(sync):
            pb.wait(sync, "c_scc")
            pb.wait(sync, "c_yst")
            pb.inc("c_sld", sync.dma_start(SiniA[:, :], st_out[0:128, :]), 16)
            pb.inc("c_sld", sync.dma_start(SiniB[:, :], st_out[385:513, :]), 16)
            pb.inc("c_sld", sync.dma_start(ytile[:, :], st_out[0:128, :]), 16)
            pb.inc("c_sld", sync.dma_start(rrow[:, :], st_out[513:514, 0:32]), 16)
        pb.on("sync", sget)

        def sblend1(sx):
            pb.wait(sx, "c_sld")
            pb.inc("c_sb0", sx.activation(dxr[:, :], rrow[:, :], AF.Exp))
        pb.on("scalar", sblend1)

        def sblend2(v):
            pb.wait(v, "c_sb0")
            pb.wait(v, "c_wld")
            pb.inc("c_sb1", v.tensor_scalar(dxr[:, :], dxr[:, :], cdsel[:, :], cdseli[:, :],
                                            op0=ALU.mult, op1=ALU.add))
        pb.on("vector", sblend2)

        def mmds(t):
            pb.wait(t, "c_sb1")
            pb.inc("c_sb2", t.matmul(psR[:, 544:576], ones_r[:, :], dxr[:, :],
                                     start=True, stop=True))
        pb.on("tensor", mmds)

        def sblend3(v):
            pb.wait(v, "c_sb2")
            v.tensor_copy(dbc[:, :], psR[:, 544:576])
            v.drain()
            v.tensor_scalar(SiniA[:, :], SiniA[:, :], cselA[:, :], None, op0=ALU.mult)
            v.drain()
            v.tensor_scalar(SiniB[:, :], SiniB[:, :], cw1[:, :], None, op0=ALU.mult)
            v.drain()
            v.tensor_tensor(ytile[:, :].rearrange("p (h w) -> p h w", h=32),
                            ytile[:, :].rearrange("p (h w) -> p h w", h=32),
                            bass.AP(dbc, 0, [[32, 128], [1, 32], [0, 64]]), op=ALU.mult)
            v.drain()
            pb.inc("c_sini", v.tensor_tensor(SiniB[:, :], SiniB[:, :], ytile[:, :], op=ALU.add))
        pb.on("vector", sblend3)

    with contextlib.ExitStack() as e4:
        opw = _sb(nc, e4, "c_opw", [128, 16, 1024], BF16)
        rmsw = _sb(nc, e4, "c_rmsw_s", [128, 2048], F32)
        ysl = _sb(nc, e4, "c_ysl", [128, 2048], F32)
        zsl = _sb(nc, e4, "c_zsl", [128, 2048], BF16)
        zss = _sb(nc, e4, "c_zss", [128, 2048], BF16)
        y3 = _sb(nc, e4, "c_y3", [128, 2048], BF16)
        y3T = _sb(nc, e4, "c_y3T", [128, 16, 128], BF16)
        Scor = _sb(nc, e4, "c_Scor", [128, 2048], F32)
        cd_bc = _sb(nc, e4, "c_cdbc2", [128, 32], F32)
        ea2 = _sb(nc, e4, "c_ea2", [128, 32], F32)
        ycq = _sb(nc, e4, "c_ycq", [128, 512], F32)
        C32g = _sb(nc, e4, "c_C32g", [128, 128], F32)
        rstd = _sb(nc, e4, "c_rstd", [128, 1], F32)
        rt4 = _sb(nc, e4, "c_rt4", [128, 1024], F32)

        def ldop(g):
            pb.wait(g, "c_sini")
            pb.inc("c_wld2", g.dma_start(opw[:, :, :], opr[:, :, :]), 16)
            pb.inc("c_wld2", g.dma_start(rmsw[:, :], io["c_rmsw"][:, :]), 16)
        pb.on("gpsimd", ldop)

        for s in range(NT):
            Sini = SiniA if s < 8 else SiniB

            def ldy(sync, s=s):
                pb.wait(sync, "c_yst")
                pb.wait(sync, "c_y3", pb.val("c_y3"))
                pb.inc("c_yld", sync.dma_start(ysl[:, :], y_dram[s * 128:(s + 1) * 128, :]), 16)
                pb.inc("c_yld", sync.dma_start(zsl[:, :], z_dram[s * 128:(s + 1) * 128, :]), 16)
            pb.on("sync", ldy)

            def cexp(sx, s=s):
                pb.wait(sx, "c_sini")
                pb.wait(sx, "c_cd1", max(0, pb.val("c_cd1")))
                sx.activation(ea2[:, :], acum_all[:, s, :], AF.Exp)
                sx.drain()
                pb.inc("c_cd0", sx.activation(rrow[:, :], cdrow_all[:, s, :], AF.Exp))
            pb.on("scalar", cexp)

            def mmcd(t, s=s):
                pb.wait(t, "c_cd0")
                pb.wait(t, "c_opev")
                pb.inc("c_cdm", t.matmul(psR[:, 544:576], ones_r[:, :], rrow[:, :],
                                         start=True, stop=True))
            pb.on("tensor", mmcd)

            def mkscor(v, s=s, Sini=Sini):
                pb.wait(v, "c_cdm")
                pb.wait(v, "c_ycm", pb.val("c_ycm"))
                v.tensor_copy(cd_bc[:, :], psR[:, 544:576])
                v.drain()
                v.tensor_copy(C32g[:, :], Call[:, s, :])
                v.drain()
                pb.inc("c_cd1", v.tensor_tensor(
                    Scor[:, :].rearrange("p (h w) -> p h w", h=32),
                    Sini[:, :].rearrange("p (h w) -> p h w", h=32),
                    bass.AP(cd_bc, 0, [[32, 128], [1, 32], [0, 64]]), op=ALU.mult))
            pb.on("vector", mkscor)

            for q in range(4):
                def mmyc(t, s=s, q=q):
                    pb.wait(t, "c_cd1")
                    pb.wait(t, "c_ycev", max(0, pb.val("c_ycev")))
                    pb.inc("c_ycm", t.matmul(psYo[:, :], C32g[:, :],
                                             Scor[:, q * 512:(q + 1) * 512],
                                             start=True, stop=True))
                pb.on("tensor", mmyc)

                def evyc(v, s=s, q=q):
                    pb.wait(v, "c_ycm")
                    pb.wait(v, "c_yld")
                    eap = bass.AP(ea2, q * 8, [[32, 128], [1, 8], [0, 64]])
                    pb.inc("c_ycev", v.tensor_tensor(
                        ycq[:, :].rearrange("p (h w) -> p h w", h=8),
                        psYo[:, :].rearrange("p (h w) -> p h w", h=8),
                        eap, op=ALU.mult))
                    v.drain()
                    pb.inc("c_yc2", v.tensor_tensor(
                        ysl[:, q * 512:(q + 1) * 512],
                        ysl[:, q * 512:(q + 1) * 512], ycq[:, :], op=ALU.add))
                pb.on("vector", evyc)

            def zsi(sx, s=s):
                pb.wait(sx, "c_yc2", pb.val("c_yc2"))
                pb.inc("c_zs", sx.activation(zss[:, :], zsl[:, :], AF.Silu))
            pb.on("scalar", zsi)

            def ymul(v, s=s):
                pb.wait(v, "c_zs")
                pb.inc("c_y2", v.tensor_tensor(ysl[:, :], ysl[:, :], zss[:, :], op=ALU.mult))
            pb.on("vector", ymul)

            def rms1(sx, s=s):
                pb.wait(sx, "c_y2")
                pb.inc("c_rms0", sx.activation(sc["junk"][:, 0:2048], ysl[:, :], AF.Square,
                                               accum_out=rstd[:, :]))
            pb.on("scalar", rms1)

            def rms2(v, s=s):
                pb.wait(v, "c_rms0")
                pb.inc("c_rms1", v.tensor_scalar(rstd[:, :], rstd[:, :], 1.0 / 2048, float(EPS),
                                                 op0=ALU.mult, op1=ALU.add))
            pb.on("vector", rms2)

            def rms3(sx, s=s):
                pb.wait(sx, "c_rms1")
                pb.inc("c_rms2", sx.activation(rstd[:, :], rstd[:, :], AF.Sqrt))
            pb.on("scalar", rms3)

            def rms4(v, s=s):
                pb.wait(v, "c_rms2")
                pb.wait(v, "c_wld2")
                v.reciprocal(sc["c1"][:, :], rstd[:, :])
                v.drain()
                v.tensor_scalar(ysl[:, :], ysl[:, :], sc["c1"][:, :], None, op0=ALU.mult)
                v.drain()
                pb.inc("c_y3", v.tensor_tensor(y3[:, :], ysl[:, :], rmsw[:, :], op=ALU.mult))
            pb.on("vector", rms4)

            for g2 in range(2):
                def tty(t, s=s, g2=g2):
                    pb.wait(t, "c_y3")
                    pb.wait(t, "c_ytev", max(0, pb.val("c_ytev") - 1))
                    for ic in range(8):
                        ins = t.transpose(tps[:, g2, ic * 128:(ic + 1) * 128],
                                          y3[:, (g2 * 8 + ic) * 128:(g2 * 8 + ic + 1) * 128],
                                          identb[:, :])
                    pb.inc("c_ytm", ins)
                pb.on("tensor", tty)

                def evyt(v, s=s, g2=g2):
                    pb.wait(v, "c_ytm")
                    pb.inc("c_ytev", v.tensor_copy(
                        y3T[:, g2 * 8:(g2 + 1) * 8, :],
                        tps[:, g2, :].rearrange("p (a b) -> p a b", a=8)))
                pb.on("vector", evyt)

            for hf in range(2):
                def mmop(t, s=s, hf=hf):
                    pb.wait(t, "c_ytev")
                    pb.wait(t, "c_wld2")
                    pb.wait(t, "c_opev", max(0, pb.val("c_opev") - 1))
                    for ic in range(16):
                        ins = t.matmul(psR[:, (hf % 2) * 512:(hf % 2) * 512 + 512],
                                       y3T[:, ic, :], opw[:, ic, hf * 512:(hf + 1) * 512],
                                       start=(ic == 0), stop=(ic == 15))
                    pb.inc("c_opm", ins)
                pb.on("tensor", mmop)

                def evop(v, s=s, hf=hf):
                    pb.wait(v, "c_opm")
                    pb.wait(v, "ac")
                    pb.inc("c_opev", v.scalar_tensor_tensor(
                        rt4[:, hf * 512:(hf + 1) * 512], stream[:, s, hf * 512:(hf + 1) * 512],
                        2.0, psR[:, (hf % 2) * 512:(hf % 2) * 512 + 512],
                        op0=ALU.mult, op1=ALU.add))
                pb.on("vector", evop)

            emit_ln(pb, sc, rt4[:, :], stream[:, s, :], ready=("c_opev", pb.val("c_opev")))


def stage_mamba(inp):
    d = {}
    d["c_ipT"] = _t(inp["L1_attn_in_proj_w"])
    d["c_opT"] = _t(inp["L1_attn_out_proj_w"])
    d["c_normw"] = np.ascontiguousarray(np.broadcast_to(np.asarray(inp["L1_attn_norm_w"], np.float32), (128, 1024)))
    d["c_normb"] = np.ascontiguousarray(np.broadcast_to(np.asarray(inp["L1_attn_norm_b"], np.float32), (128, 1024)))
    d["c_convw"] = np.ascontiguousarray(np.asarray(inp["L1_attn_conv_w"], np.float32).reshape(18, 128, 4).transpose(1, 0, 2))
    d["c_convb"] = np.ascontiguousarray(np.asarray(inp["L1_attn_conv_b"], np.float32).reshape(18, 128).T)
    d["c_dtb"] = np.ascontiguousarray(np.broadcast_to(np.asarray(inp["L1_attn_dt_bias"], np.float32), (128, 32)))
    d["c_Alog"] = np.ascontiguousarray(np.broadcast_to(np.asarray(inp["L1_attn_A_log"], np.float32), (128, 32)))
    d["c_D"] = np.ascontiguousarray(np.broadcast_to(np.asarray(inp["L1_attn_D"], np.float32), (128, 32)))
    d["c_rmsw"] = np.ascontiguousarray(np.broadcast_to(np.asarray(inp["L1_attn_rms_w"], np.float32), (128, 2048)))
    d["utri_incl"] = np.triu(np.ones((128, 128), np.float32), 0)
    d["triu01"] = np.triu(np.ones((128, 128), np.float32), 0)
    d["ident"] = np.eye(128, dtype=np.float32)
    return d


def stage_mamba_core(core):
    half = core % 2
    e = (half == 0)
    d = {}
    d["c_rho"] = np.full((128, 1), 0.0 if e else 1.0, np.float32)
    d["c_rho2"] = np.full((1, 1), 0.0 if e else 1.0, np.float32)
    d["c_w1"] = np.full((128, 1), 1.0 if e else 0.0, np.float32)
    d["c_dsel"] = np.full((1, 1), 1.0 if e else 0.0, np.float32)
    d["c_dseli"] = np.full((1, 1), 0.0 if e else 1.0, np.float32)
    d["c_selA"] = np.full((128, 1), 0.0 if e else 1.0, np.float32)
    d["c_selE"] = np.full((128, 1), 1.0 if e else 0.0, np.float32)
    d["c_selO"] = np.full((128, 1), 0.0 if e else 1.0, np.float32)
    return d


MAMBA_DEBUG = False

def build_mamba_launch(staged):
    nc = _new_nc()
    hs = {k: _dram_in(nc, k, v) for k, v in staged.items()}
    for k in ("c_rho", "c_w1", "c_selA", "c_selE", "c_selO"):
        hs[k] = nc.dram_tensor(k, [128, 1], F32, kind="ExternalInput").ap()
    for k in ("c_rho2", "c_dsel", "c_dseli"):
        hs[k] = nc.dram_tensor(k, [1, 1], F32, kind="ExternalInput").ap()
    x_h = nc.dram_tensor("x_in", [TOK, D], F32, kind="ExternalInput").ap()
    y_h = nc.dram_tensor("y_out", [TOK, D], F32, kind="ExternalOutput").ap()
    es = contextlib.ExitStack()
    with es:
        pb = PB(nc, es)
        stream = _sb(nc, es, "stream", [128, NT, 1024], F32)
        identb = _sb(nc, es, "identb", [128, 128], BF16)
        utri_sb = _sb(nc, es, "utri_sb", [128, 128], F32)
        triu_sb = _sb(nc, es, "triu_sb", [128, 128], F32)
        csts = emit_const_pool(pb, nc, es, [EPS, 1.0])
        lnsc = mk_ln_scratch(nc, es, "ln_", csts[EPS])
        lnsc["one"] = csts[1.0]

        def ldi(g):
            pb.inc("wld", g.dma_start(identb[:, :], hs["ident"][:, :]), 16)
            pb.inc("wld", g.dma_start(utri_sb[:, :], hs["utri_incl"][:, :]), 16)
            pb.inc("wld", g.dma_start(triu_sb[:, :], hs["triu01"][:, :]), 16)
        pb.on("gpsimd", ldi)

        def wldw(t):
            pb.wait(t, "wld")
        pb.on("tensor", wldw)

        emit_stream_load(pb, stream, x_h)
        io = dict(hs)
        io.update(stream=stream, identb=identb, lnsc=lnsc,
                  utri_incl_sb=utri_sb, triu01_sb=triu_sb)
        if MAMBA_DEBUG:
            io["_dbg0"] = {
                "x1": nc.dram_tensor("dbg_x1", [128, 2048], BF16, kind="ExternalOutput").ap(),
                "xstm": nc.dram_tensor("dbg_xstm", [128, 2048], BF16, kind="ExternalOutput").ap(),
                "gm": nc.dram_tensor("dbg_gm", [128, 128], BF16, kind="ExternalOutput").ap(),
                "acum": nc.dram_tensor("dbg_acum", [128, 32], F32, kind="ExternalOutput").ap(),
                "dt": nc.dram_tensor("dbg_dt", [128, 32], F32, kind="ExternalOutput").ap(),
                "xbcp": nc.dram_tensor("dbg_xbcp", [128, 18, 512], BF16, kind="ExternalOutput").ap(),
            }
        emit_mamba(pb, es, io)
        if MAMBA_DEBUG:
            dbgy = nc.dram_tensor("dbg_y", [TOK, 2048], F32, kind="ExternalOutput").ap()
            dbgz = nc.dram_tensor("dbg_z", [TOK, 2048], BF16, kind="ExternalOutput").ap()
            dbgx = nc.dram_tensor("dbg_xbcr", [128, 18, 2, 1024], BF16, kind="ExternalOutput").ap()
            ydr = io["_dbg"]["y"]
            zdr = io["_dbg"]["z"]
            xdr = io["_dbg"]["x"]
            def dbg(sync):
                pb.wait(sync, "c_yst")
                pb.inc("dbgo", sync.dma_start(dbgy, ydr), 16)
                pb.inc("dbgo", sync.dma_start(dbgz, zdr), 16)
                pb.inc("dbgo", sync.dma_start(dbgx, xdr), 16)
                pb.wait(sync, "dbgo")
            pb.on("sync", dbg)
        emit_stream_store(pb, stream, y_h, ["lnw"])
    return nc


def _t(a):
    return np.ascontiguousarray(np.asarray(a).T.astype(np.float32))


def stage_ffn(inp):
    d = {}
    d["w1t"] = _t(inp["L1_ffn_w1_w"])
    d["w2t"] = _t(inp["L1_ffn_w2_w"])
    d["h1t"] = _t(inp["L1_ffn_h1_w"])
    d["h2t"] = _t(inp["L1_ffn_h2_w"])
    d["w3t"] = _t(inp["L1_ffn_w3_w"])
    d["s1c"] = np.ascontiguousarray(np.asarray(inp["L1_ffn_w1_s"], np.float32).reshape(32, 128).T)
    d["s2c"] = np.ascontiguousarray(np.asarray(inp["L1_ffn_w2_s"], np.float32).reshape(32, 128).T)
    d["sh1c"] = np.ascontiguousarray(np.asarray(inp["L1_ffn_h1_s"], np.float32).reshape(32, 128).T)
    d["sh2c"] = np.ascontiguousarray(np.asarray(inp["L1_ffn_h2_s"], np.float32).reshape(32, 128).T)
    d["s3bc"] = np.ascontiguousarray(np.broadcast_to(np.asarray(inp["L1_ffn_w3_s"], np.float32), (128, 1024)))
    d["ident"] = np.eye(128, dtype=np.float32)
    return d


def emit_phase_barrier(pb, sem, target):
    for eng in ("sync", "gpsimd", "tensor", "vector", "scalar"):
        def w(e, sem=sem, target=target):
            pb.wait(e, sem, target)
        pb.on(eng, w)


def build_full_launch(inp_shapes):
    nc = _new_nc()
    hs = {}
    for name, (shape, dt) in inp_shapes.items():
        hs[name] = nc.dram_tensor(name, list(shape), dt, kind="ExternalInput").ap()
    y_h = nc.dram_tensor("y_out", [TOK, D], F32, kind="ExternalOutput").ap()
    es = contextlib.ExitStack()
    with es:
        pb = PB(nc, es)
        stream = _sb(nc, es, "stream", [128, NT, 1024], F32)
        identb = _sb(nc, es, "identb", [128, 128], BF16)
        csts = emit_const_pool(pb, nc, es, [EPS, 1.0])
        lnsc = mk_ln_scratch(nc, es, "ln_", csts[EPS])
        lnsc["one"] = csts[1.0]

        def ldi(g):
            pb.inc("wld", g.dma_start(identb[:, :], hs["ident"][:, :]), 16)
        pb.on("gpsimd", ldi)

        io = dict(hs)
        io.update(stream=stream, identb=identb, lnsc=lnsc)

        with contextlib.ExitStack() as esA:
            emit_attn(pb, esA, io)
        emit_phase_barrier(pb, "lnw", 16)

        with contextlib.ExitStack() as esB:
            emit_moe(pb, esB, io)
        emit_phase_barrier(pb, "lnw", 32)

        with contextlib.ExitStack() as esC:
            utri_sb = _sb(nc, esC, "utri_sb", [128, 128], F32)
            triu_sb = _sb(nc, esC, "triu_sb", [128, 128], F32)

            def ldu(g):
                pb.inc("c_wld", g.dma_start(utri_sb[:, :], hs["utri_incl"][:, :]), 16)
                pb.inc("c_wld", g.dma_start(triu_sb[:, :], hs["triu01"][:, :]), 16)
            pb.on("gpsimd", ldu)
            io2 = dict(io)
            io2.update(utri_incl_sb=utri_sb, triu01_sb=triu_sb)
            emit_mamba(pb, esC, io2)
        emit_phase_barrier(pb, "lnw", 64)

        with contextlib.ExitStack() as esD:
            emit_ffn(pb, esD, io)
        emit_stream_store(pb, stream, y_h, ["lnw"])
    return nc


def stage_all(inp):
    shared = {}
    shared.update(stage_attn(inp))
    shared.update(stage_moe(inp))
    shared.update(stage_mamba(inp))
    shared.update(stage_ffn(inp))
    maps = []
    for core in range(NCORE):
        d = dict(shared)
        d.update(stage_attn_core(inp, core, shared))
        d.update(stage_moe_core(core))
        d.update(stage_mamba_core(core))
        maps.append(d)
    return maps


_DT_MAP = {"float32": F32, "bfloat16": BF16}


def kernel_fused(**inputs):
    inp = {k: np.asarray(v) for k, v in inputs.items()}
    maps = stage_all(inp)
    inp_shapes = {k: (v.shape, _DT_MAP[str(v.dtype)]) for k, v in maps[0].items()}
    nc = build_full_launch(inp_shapes)
    res = run_bass_kernel_spmd(nc, maps, core_ids=list(range(NCORE)))
    return _reassemble([r["y_out"] for r in res.results])


def _reassemble(per_core):
    out = np.zeros((B, T, D), np.float32)
    for core in range(NCORE):
        b, half = core // 2, core % 2
        out[b][_own_tokens(half)] = per_core[core]
    return out


def kernel(**inputs):
    return kernel_fused(**inputs)


def kernel_multilaunch(**inputs):
    inp = {k: np.asarray(v) for k, v in inputs.items()}

    sh = stage_attn(inp)
    nc_a = build_attn_launch(sh)
    maps = []
    for core in range(NCORE):
        d = dict(sh)
        d.update(stage_attn_core(inp, core, sh))
        maps.append(d)
    res = run_bass_kernel_spmd(nc_a, maps, core_ids=list(range(NCORE)))
    x1 = [r["y_out"] for r in res.results]

    sm = stage_moe(inp)
    nc_b = build_moe_launch(sm)
    maps = []
    for core in range(NCORE):
        d = dict(sm)
        d.update(stage_moe_core(core))
        d["x_in"] = np.ascontiguousarray(x1[core])
        maps.append(d)
    res = run_bass_kernel_spmd(nc_b, maps, core_ids=list(range(NCORE)))
    x2 = [r["y_out"] for r in res.results]

    sc_ = stage_mamba(inp)
    nc_c = build_mamba_launch(sc_)
    maps = []
    for core in range(NCORE):
        d = dict(sc_)
        d.update(stage_mamba_core(core))
        d["x_in"] = np.ascontiguousarray(x2[core])
        maps.append(d)
    res = run_bass_kernel_spmd(nc_c, maps, core_ids=list(range(NCORE)))
    x3 = [r["y_out"] for r in res.results]

    sf = stage_ffn(inp)
    nc_d = build_ffn_launch(sf)
    maps = []
    for core in range(NCORE):
        d = dict(sf)
        d["x_in"] = np.ascontiguousarray(x3[core])
        maps.append(d)
    res = run_bass_kernel_spmd(nc_d, maps, core_ids=list(range(NCORE)))
    x4 = [r["y_out"] for r in res.results]

    return _reassemble(x4)


# version 04
# speedup vs baseline: 1.0270x; 1.0270x over previous
import contextlib
import numpy as np
import concourse.bass as bass
import concourse.mybir as mybir
from concourse.bass_utils import run_bass_kernel_spmd

F32 = mybir.dt.float32
BF16 = mybir.dt.bfloat16
AF = mybir.ActivationFunctionType
ALU = mybir.AluOpType
AX = mybir.AxisListType.X

NCORE = 8
B = 4
T = 4096
D = 1024
NT = 16
TOK = NT * 128
EPS = 1e-5

DEPTH_PLAN = [3, 3, 3, 3, 4, 4, 4, 4, 7, 7, 7, 7, 8, 8, 8, 8]

PERM = np.concatenate([np.arange(0, D, 2), np.arange(1, D, 2)])


def _chunks_of(half):
    if half == 0:
        return list(range(0, 8)) + list(range(24, 32))
    return list(range(8, 24))


def _own_tokens(half):
    idx = []
    for c in _chunks_of(half):
        idx.extend(range(c * 128, (c + 1) * 128))
    return np.array(idx)


_SEM_VEC = {"a_epi", "a_evO", "a_kev", "a_mgen", "a_msk", "a_qev", "a_rope",
            "a_tre2", "a_vev", "c_D2", "c_M", "c_X", "c_ac1", "c_acev",
            "c_btev", "c_c32", "c_cd1", "c_cv1", "c_cvh", "c_dc0", "c_dcev",
            "c_dev", "c_dt0", "c_dt2", "c_fmev", "c_gev", "c_halo", "c_lnx",
            "c_opev", "c_rms1", "c_rsm", "c_sb1", "c_sini", "c_sprep",
            "c_supd", "c_trc", "c_tre", "c_xev", "c_xsev", "c_y2", "c_y3",
            "c_yas", "c_yc2", "c_ycev", "c_ypsfree", "c_ytev", "c_zev", "ev",
            "evA", "evD", "lnc2free", "m_cum", "m_gev", "m_hev", "m_oev",
            "m_offev", "m_rbev", "m_res", "m_rowf", "m_rt1", "m_rt2",
            "m_trel", "m_wcap", "m_wgl", "trc", "tre"}
_SEM_PE = {"a_kmm", "a_omm", "a_pvmm", "a_qkmm", "a_qmm", "a_trm2", "a_vmm",
           "c_acm", "c_btm", "c_cdm", "c_dbm", "c_gm", "c_imm", "c_lbm",
           "c_opm", "c_row", "c_sb2", "c_trf", "c_trm", "c_um", "c_xsm",
           "c_ycm", "c_ydm", "c_yom", "c_ytm", "m_gmm", "m_hmm", "m_offm",
           "m_omm", "m_posm", "m_rbm", "m_summ", "m_trml", "mmA", "mmB0",
           "mmB1", "mmD", "trm"}
_SEM_ACT = {"a_exp", "ac", "acA", "acB0", "acB1", "c_E", "c_ac0", "c_cd0",
            "c_cv2", "c_cvdone", "c_dc1", "c_dt1", "c_ea", "c_rms0",
            "c_rms2", "c_sb0", "c_zs", "m_hact", "m_sg"}
_SEM_GPSE = {"a_cst", "c_cst", "c_s0", "cst", "m_capc", "m_cst"}


def _sem_pool(name):
    if name in _SEM_VEC:
        return "_vec"
    if name in _SEM_PE:
        return "_pe"
    if name in _SEM_ACT:
        return "_act"
    if name in _SEM_GPSE:
        return "_gpse"
    return name


class PB:

    def __init__(self, nc, es):
        self.nc = nc
        self.es = es
        self.blk = es.enter_context(nc.Block())
        self.sems = {}

    def semh(self, name):
        key = _sem_pool(name)
        if key not in self.sems:
            self.sems[key] = [self.es.enter_context(self.nc.semaphore(key)), 0]
        return self.sems[key]

    def inc(self, name, instr, n=1):
        h = self.semh(name)
        instr.then_inc(h[0], n)
        h[1] += n
        return h[1]

    def val(self, name):
        return self.semh(name)[1]

    def wait(self, eng, name, val=None):
        h = self.semh(name)
        v = h[1] if val is None else val
        if v > 0:
            eng.wait_ge(h[0], v)

    def on(self, engname, f):
        getattr(self.blk, engname)(f)


def _sb(nc, es, name, shape, dtype):
    return es.enter_context(nc.sbuf_tensor(name, list(shape), dtype))


def _ps(nc, es, name, shape, dtype=F32):
    return es.enter_context(nc.psum_tensor(name, list(shape), dtype))


def emit_const_pool(pb, nc, es, values):
    cp = _sb(nc, es, "cpool", [128, max(1, len(values))], F32)
    aps = {}
    def g0(g):
        for i, v in enumerate(values):
            ins = g.memset(cp[:, i:i + 1], float(v))
        pb.inc("cst", ins)
    pb.on("gpsimd", g0)
    for i, v in enumerate(values):
        aps[v] = cp[:, i:i + 1]
    for eng in ("vector", "scalar", "tensor"):
        def w(e):
            pb.wait(e, "cst")
        pb.on(eng, w)
    return aps


def mk_ln_scratch(nc, es, pfx, eps_ap=None):
    return {"eps": eps_ap,
            "junk": _sb(nc, es, pfx + "junk", [128, 2048], BF16),
            "lnt": _sb(nc, es, pfx + "lnt", [128, 1024], F32),
            "c1": _sb(nc, es, pfx + "c1", [128, 1], F32),
            "c2": _sb(nc, es, pfx + "c2", [128, 1], F32),
            "c3": _sb(nc, es, pfx + "c3", [128, 1], F32),
            "c4": _sb(nc, es, pfx + "c4", [128, 1], F32)}


def emit_ln(pb, sc, rtile, out_tile, ready=None, gain_bc=None, bias_bc=None, width=1024):
    c1, c2, c3, c4 = sc["c1"], sc["c2"], sc["c3"], sc["c4"]
    junk = sc["junk"]

    def v1(v):
        if ready is not None:
            pb.wait(v, ready[0], ready[1])
        v.drain()
        v.tensor_reduce(c1[:, :], rtile, axis=AX, op=ALU.add)
        v.drain()
        v.tensor_scalar(c1[:, :], c1[:, :], 1.0 / width, None, op0=ALU.mult)
        v.drain()
        pb.wait(v, "ac")
        v.scalar_tensor_tensor(junk[:, :width], rtile, 1.0, rtile,
                               op0=ALU.mult, op1=ALU.mult, accum_out=c2[:, :])
        v.drain()
        v.tensor_scalar(c2[:, :], c2[:, :], 1.0 / width, float(EPS),
                        op0=ALU.mult, op1=ALU.add)
        v.drain()
        v.tensor_tensor(c3[:, :], c1[:, :], c1[:, :], op=ALU.mult)
        v.drain()
        pb.inc("ev", v.tensor_tensor(c2[:, :], c2[:, :], c3[:, :], op=ALU.subtract))
    pb.on("vector", v1)

    def s2(s):
        pb.wait(s, "ev")
        s.activation(c3[:, :], c2[:, :], AF.Ln)
        s.drain()
        pb.inc("ac", s.activation(c4[:, :], c3[:, :], AF.Exp, scale=-0.5))
    pb.on("scalar", s2)

    def v3(v):
        pb.wait(v, "ac")
        if gain_bc is None:
            pb.inc("lnw", v.tensor_scalar(out_tile, rtile, c1[:, :], c4[:, :],
                                          op0=ALU.subtract, op1=ALU.mult))
        else:
            v.tensor_scalar(sc["lnt"][:, :width], rtile, c1[:, :], c4[:, :],
                            op0=ALU.subtract, op1=ALU.mult)
            v.drain()
            v.tensor_tensor(sc["lnt"][:, :width], sc["lnt"][:, :width], gain_bc, op=ALU.mult)
            v.drain()
            pb.inc("lnw", v.tensor_tensor(out_tile, sc["lnt"][:, :width], bias_bc, op=ALU.add))
    pb.on("vector", v3)


def emit_cast_transpose(pb, stream_tiles, xbf, ident_bf, xT_dst, tps, tag):
    for k, (src, pos) in enumerate(stream_tiles):
        def vc(v, src=src):
            pb.wait(v, "trm")
            pb.inc("trc", v.tensor_copy(xbf[:, :], src))
        pb.on("vector", vc)

        def tt(t, k=k):
            pb.wait(t, "wld")
            pb.wait(t, "trc")
            pb.wait(t, "tre", max(0, pb.val("tre") - 1))
            for dc in range(7):
                t.transpose(tps[:, k % 2, dc * 128:(dc + 1) * 128],
                            xbf[:, dc * 128:(dc + 1) * 128], ident_bf[:, :])
            pb.inc("trm", t.transpose(tps[:, k % 2, 7 * 128:8 * 128],
                                      xbf[:, 7 * 128:8 * 128], ident_bf[:, :]))
        pb.on("tensor", tt)

        def ve(v, k=k, pos=pos):
            pb.wait(v, "trm")
            pb.inc("tre", v.tensor_copy(
                xT_dst[:, :, pos * 128:(pos + 1) * 128],
                tps[:, k % 2, :].rearrange("p (a b) -> p a b", a=8)))
        pb.on("vector", ve)


FFN_PARTS = 99

def emit_ffn(pb, es, io):
    nc = pb.nc
    stream = io["stream"]
    sc = io["lnsc"]
    identb = io["identb"]
    SLAB = 4
    NSLAB = NT // SLAB
    ST = SLAB * 128

    xT = _sb(nc, es, "fd_xT", [128, 8, ST], BF16)
    t1a = _sb(nc, es, "fd_t1a", [128, 2, ST], BF16)
    t1 = _sb(nc, es, "fd_t1", [128, 32, ST], BF16)
    t2 = _sb(nc, es, "fd_t2", [128, 32, ST], BF16)
    xbf = _sb(nc, es, "fd_xbf", [128, 1024], BF16)
    w1b = _sb(nc, es, "fd_w1b", [128, 2, 8, 128], BF16)
    w2b = _sb(nc, es, "fd_w2b", [128, 2, 8, 128], BF16)
    hb = _sb(nc, es, "fd_hb", [128, 2, 32, 128], BF16)
    s1c = _sb(nc, es, "fd_s1c", [128, 32], F32)
    s2c = _sb(nc, es, "fd_s2c", [128, 32], F32)
    sh1c = _sb(nc, es, "fd_sh1c", [128, 32], F32)
    sh2c = _sb(nc, es, "fd_sh2c", [128, 32], F32)
    s3bc = _sb(nc, es, "fd_s3bc", [128, 1024], F32)
    rt = _sb(nc, es, "fd_rt", [128, SLAB, 1024], F32)
    tps = _ps(nc, es, "fd_tps", [128, 2, 1024], BF16)
    psA = _ps(nc, es, "fd_psA", [128, 2, 512], F32)
    psB = _ps(nc, es, "fd_psB", [128, 2, 512], F32)
    psO = _ps(nc, es, "fd_psO", [128, 2, 512], F32)

    w1r = io["w1ft"]
    w2r = io["w2ft"]
    h1r = io["h1ft"]
    h2r = io["h2ft"]
    w3r = io["w3ft"]

    def ld0(g):
        pb.inc("wld", g.dma_start(s1c[:, :], io["s1c"][:, :]), 16)
        pb.inc("wld", g.dma_start(s2c[:, :], io["s2c"][:, :]), 16)
        pb.inc("wld", g.dma_start(sh1c[:, :], io["sh1c"][:, :]), 16)
        pb.inc("wld", g.dma_start(sh2c[:, :], io["sh2c"][:, :]), 16)
        pb.inc("wld", g.dma_start(s3bc[:, :], io["s3bc"][:, :]), 16)
    pb.on("gpsimd", ld0)

    for sl in range(NSLAB if FFN_PARTS >= 99 else 1):
        tiles = list(range(sl * SLAB, (sl + 1) * SLAB))
        if FFN_PARTS < 1:
            break
        emit_cast_transpose(pb, [(stream[:, i, :], pos) for pos, i in enumerate(tiles)],
                            xbf, identb, xT, tps, f"f{sl}")

        mmA_after, evA_after, acA_after = {}, {}, {}
        for fc in range(32 if FFN_PARTS >= 2 else 0):
            bf = fc % 2
            def lw(g, fc=fc, bf=bf):
                pb.wait(g, "wld")
                if fc >= 2:
                    pb.wait(g, "mmA", mmA_after[fc - 2])
                else:
                    pb.wait(g, "mmA")
                pb.inc("wldA", g.dma_start(w1b[:, bf], w1r[fc]), 16)
                pb.inc("wldA", g.dma_start(w2b[:, bf], w2r[fc]), 16)
            pb.on("gpsimd", lw)

            def mm(t, fc=fc, bf=bf):
                pb.wait(t, "wldA")
                pb.wait(t, "tre")
                if fc >= 2:
                    pb.wait(t, "acA", acA_after[fc - 2])
                    pb.wait(t, "evA", evA_after[fc - 2])
                else:
                    pb.wait(t, "acB1")
                for dc in range(8):
                    t.matmul(psA[:, bf], w1b[:, bf, dc, :], xT[:, dc, :],
                             start=(dc == 0), stop=(dc == 7))
                for dc in range(8):
                    ins = t.matmul(psB[:, bf], w2b[:, bf, dc, :], xT[:, dc, :],
                                   start=(dc == 0), stop=(dc == 7))
                mmA_after[fc] = pb.inc("mmA", ins)
            pb.on("tensor", mm)

            def ev_a(s, fc=fc, bf=bf):
                pb.wait(s, "mmA", mmA_after[fc])
                if fc >= 2:
                    pb.wait(s, "evA", evA_after[fc - 2])
                else:
                    pb.wait(s, "evA")
                acA_after[fc] = pb.inc("acA", s.activation(
                    t1a[:, bf, :], psA[:, bf], AF.Silu, scale=s1c[:, fc:fc + 1]))
            pb.on("scalar", ev_a)

            def ev_b(v, fc=fc, bf=bf):
                pb.wait(v, "acA", acA_after[fc])
                evA_after[fc] = pb.inc("evA", v.scalar_tensor_tensor(
                    t1[:, fc, :], psB[:, bf], s2c[:, fc:fc + 1], t1a[:, bf, :],
                    op0=ALU.mult, op1=ALU.mult))
            pb.on("vector", ev_b)

        for step, (hr, scol, src, dst) in enumerate(
                ([(h1r, sh1c, t1, t2), (h2r, sh2c, t2, t1)][:max(0, FFN_PARTS - 2)])):
            mmB_after, acB_after = {}, {}
            sem_m, sem_a, sem_w = f"mmB{step}", f"acB{step}", f"wldB{step}"
            for fc in range(32):
                bf = fc % 2
                def lh(g, fc=fc, bf=bf):
                    if fc >= 2:
                        pb.wait(g, sem_m, mmB_after[fc - 2])
                    else:
                        pb.wait(g, "mmB0")
                        pb.wait(g, "mmB1")
                    pb.inc(sem_w, g.dma_start(hb[:, bf], hr[fc]), 16)
                pb.on("gpsimd", lh)

                def mm(t, fc=fc, bf=bf):
                    pb.wait(t, sem_w)
                    if step == 0:
                        pb.wait(t, "evA")
                    else:
                        pb.wait(t, "acB0")
                        pb.wait(t, "evA")
                    if fc >= 2:
                        pb.wait(t, sem_a, acB_after[fc - 2])
                    for kc in range(32):
                        ins = t.matmul(psA[:, bf], hb[:, bf, kc, :], src[:, kc, :],
                                       start=(kc == 0), stop=(kc == 31))
                    mmB_after[fc] = pb.inc(sem_m, ins)
                pb.on("tensor", mm)

                def ev(s, fc=fc, bf=bf):
                    pb.wait(s, sem_m, mmB_after[fc])
                    acB_after[fc] = pb.inc(sem_a, s.activation(
                        dst[:, fc, :], psA[:, bf], AF.Gelu, scale=scol[:, fc:fc + 1]))
                pb.on("scalar", ev)

        evD_after = {}
        for half in range(2 if FFN_PARTS >= 5 else 0):
            def lw3(g, half=half):
                pb.wait(g, "mmB1")
                pb.wait(g, "mmD")
                pb.inc("wldD", g.dma_start(t2[:, :, :], w3r[half]), 16)
            pb.on("gpsimd", lw3)
            for ti in range(SLAB):
                def mm(t, half=half, ti=ti):
                    pb.wait(t, "wldD")
                    pb.wait(t, "acB1")
                    seq = half * SLAB + ti
                    if seq >= 2:
                        pb.wait(t, "evD", evD_after[seq - 2])
                    else:
                        pb.wait(t, "evD")
                    for kc in range(32):
                        ins = t.matmul(psO[:, ti % 2], t1[:, kc, ti * 128:(ti + 1) * 128],
                                       t2[:, kc, :], start=(kc == 0), stop=(kc == 31))
                    pb.inc("mmD", ins)
                pb.on("tensor", mm)

                def ev(v, half=half, ti=ti):
                    pb.wait(v, "mmD")
                    if half == 0:
                        pb.wait(v, "ac")
                    v.tensor_tensor(rt[:, ti, half * 512:(half + 1) * 512], psO[:, ti % 2],
                                    s3bc[:, half * 512:(half + 1) * 512], op=ALU.mult)
                    v.drain()
                    evD_after[half * SLAB + ti] = pb.inc("evD", v.tensor_tensor(
                        rt[:, ti, half * 512:(half + 1) * 512],
                        rt[:, ti, half * 512:(half + 1) * 512],
                        stream[:, tiles[ti], half * 512:(half + 1) * 512], op=ALU.add))
                pb.on("vector", ev)
        for ti in range(SLAB if FFN_PARTS >= 6 else 0):
            emit_ln(pb, sc, rt[:, ti, :], stream[:, tiles[ti], :],
                    ready=("evD", evD_after[SLAB + ti]))


def emit_stream_load(pb, stream, x_h):
    def ld(sync):
        for i in range(NT):
            pb.inc("sld", sync.dma_start(stream[:, i, :],
                                         x_h[i * 128:(i + 1) * 128, :]), 16)
    pb.on("sync", ld)
    for eng in ("vector", "scalar", "tensor", "gpsimd"):
        def w(e):
            pb.wait(e, "sld")
        pb.on(eng, w)


def emit_stream_store(pb, stream, y_h, wait_sems):
    def st(sync):
        for s in wait_sems:
            pb.wait(sync, s)
        for i in range(NT):
            pb.inc("ost", sync.dma_start(y_h[i * 128:(i + 1) * 128, :],
                                         stream[:, i, :]), 16)
        pb.wait(sync, "ost")
    pb.on("sync", st)


def _new_nc():
    return bass.Bass("TRN2", target_bir_lowering=False, detect_race_conditions=False)


def _dram_in(nc, name, arr):
    return nc.dram_tensor(name, list(arr.shape), F32, kind="ExternalInput").ap()


def build_ffn_launch(staged):
    nc = _new_nc()
    hs = {k: _dram_in(nc, k, v) for k, v in staged.items()}
    x_h = nc.dram_tensor("x_in", [TOK, D], F32, kind="ExternalInput").ap()
    y_h = nc.dram_tensor("y_out", [TOK, D], F32, kind="ExternalOutput").ap()
    es = contextlib.ExitStack()
    with es:
        pb = PB(nc, es)
        stream = _sb(nc, es, "stream", [128, NT, 1024], F32)
        identb = _sb(nc, es, "identb", [128, 128], BF16)
        csts = emit_const_pool(pb, nc, es, [EPS, 1.0])
        lnsc = mk_ln_scratch(nc, es, "ln_", csts[EPS])

        def ldi(g):
            pb.inc("wld", g.dma_start(identb[:, :], hs["ident"][:, :]), 16)
        pb.on("gpsimd", ldi)

        emit_stream_load(pb, stream, x_h)
        io = dict(hs)
        io.update(stream=stream, identb=identb, lnsc=lnsc)
        emit_ffn(pb, es, io)
        emit_stream_store(pb, stream, y_h, ["lnw"] if FFN_PARTS >= 6 else [])
    return nc




POS_E = [0, 0, 0, 0, 1, 1, 1, 1, 6, 6, 6, 6, 7, 7, 7, 7]
POS_O = [2, 2, 2, 2, 3, 3, 3, 3, 4, 4, 4, 4, 5, 5, 5, 5]


def emit_rope(pb, rx, xfm, ctab, stab, tmp1, tmp2, tag):
    def body(v):
        for i in range(4):
            xE = xfm[:, i, :]
            xO = xfm[:, 4 + i, :]
            c = ctab[:, i, :]
            sn = stab[:, i, :]
            v.tensor_tensor(tmp1[:, :], xE, c, op=ALU.mult)
            v.tensor_tensor(tmp2[:, :], xO, sn, op=ALU.mult)
            v.drain()
            v.tensor_tensor(rx[:, i, :], tmp1[:, :], tmp2[:, :], op=ALU.subtract)
            v.tensor_tensor(tmp1[:, :], xO, c, op=ALU.mult)
            v.tensor_tensor(tmp2[:, :], xE, sn, op=ALU.mult)
            v.drain()
            ins = v.tensor_tensor(rx[:, 4 + i, :], tmp1[:, :], tmp2[:, :], op=ALU.add)
        pb.inc("a_rope", ins)
    pb.on("vector", body)


def emit_attn(pb, es, io):
    nc = pb.nc
    stream = io["stream"]
    sc = io["lnsc"]
    identb = io["identb"]

    k_dram = nc.dram_tensor("a_kdram", [1024, 4096], BF16, kind="Internal").ap()
    v_dram = nc.dram_tensor("a_vdram", [32, 128, 16, 65], BF16, kind="Internal").ap()
    ao_dram = nc.dram_tensor("a_aodram", [TOK, 1024], BF16, kind="Internal").ap()
    q_dram = nc.dram_tensor("a_qdram", [1024, TOK], BF16, kind="Internal").ap()

    qslot = _sb(nc, es, "a_qslot", [128, 2, 8, 128], BF16)
    kslab = _sb(nc, es, "a_kslab", [128, 2, 8, 512], BF16)
    vslab = _sb(nc, es, "a_vslab", [128, 2, 4, 16, 65], BF16)
    ptsb = _sb(nc, es, "a_ptsb", [128, 2, 512], BF16)
    maskE = _sb(nc, es, "a_maskE", [128, 512], BF16)
    maskO = _sb(nc, es, "a_maskO", [128, 512], BF16)
    kabs = _sb(nc, es, "a_kabs", [128, 512], F32)
    visb = _sb(nc, es, "a_visb", [128, 16, 8], F32)
    dgapE = _sb(nc, es, "a_dgapE", [128, 16], F32)
    dgapO = _sb(nc, es, "a_dgapO", [128, 16], F32)
    attn_sb = _sb(nc, es, "a_attnsb", [128, 1024], BF16)
    rcp = _sb(nc, es, "a_rcp", [128, 1], F32)
    wob = _sb(nc, es, "a_wob", [128, 8, 1024], BF16)
    wqb = wob

    es1 = contextlib.ExitStack()
    xfm = _sb(nc, es1, "a_xfm", [128, 8, 512], BF16)
    rx = _sb(nc, es1, "a_rx", [128, 8, 512], BF16)
    ctab = _sb(nc, es1, "a_ctab", [128, 4, 512], BF16)
    stab = _sb(nc, es1, "a_stab", [128, 4, 512], BF16)
    tmp1 = _sb(nc, es1, "a_tmp1", [128, 512], BF16)
    tmp2 = _sb(nc, es1, "a_tmp2", [128, 512], BF16)
    wvb = _sb(nc, es1, "a_wvb", [128, 8, 1024], BF16)
    wkb = _sb(nc, es1, "a_wkb", [128, 8, 1024], BF16)

    vstage = _sb(nc, es1, "a_vstage", [128, 16, 65], BF16)
    kstage = _sb(nc, es1, "a_kstage", [128, 8, 512], BF16)

    psV = _ps(nc, es, "a_psV", [128, 2, 512], F32)

    xk_r = io["x_fmp"].rearrange("(dc p) t -> p dc t", p=128)
    xq_r = io["x_fmq"].rearrange("(dc p) t -> p dc t", p=128)
    cosk_r = io["cos_k"].rearrange("(c p) t -> p c t", p=128)
    sink_r = io["sin_k"].rearrange("(c p) t -> p c t", p=128)
    cosq_r = io["cos_q"].rearrange("(c p) t -> p c t", p=128)
    sinq_r = io["sin_q"].rearrange("(c p) t -> p c t", p=128)
    wv_r = io["wvT"].rearrange("(dc p) f -> p dc f", p=128)
    wk_r = io["wkT"].rearrange("(dc p) f -> p dc f", p=128)
    wq_r = io["wqT"].rearrange("(dc p) f -> p dc f", p=128)
    wo_r = io["woT"].rearrange("(dc p) f -> p dc f", p=128)
    k_dram_r = k_dram.rearrange("(fc p) t -> p fc t", p=128)
    q_dram_r = q_dram.rearrange("(fc p) t -> p fc t", p=128)
    v_dram_r = v_dram.rearrange("kc p h c -> p kc (h c)")

    def ld0(g):
        pb.inc("a_wld", g.dma_start(wvb[:, :, :], wv_r[:, :, :]), 16)
        pb.inc("a_wld", g.dma_start(wkb[:, :, :], wk_r[:, :, :]), 16)
        pb.inc("a_wld", g.dma_start(wqb[:, :, :], wq_r[:, :, :]), 16)
        pb.inc("a_wld", g.dma_start(kabs[:, :], io["kabs"][:, :]), 16)
        pb.inc("a_wld", g.dma_start(visb[:, :, :], io["visb"][:, :, :]), 16)
        pb.inc("a_wld", g.dma_start(dgapE[:, :], io["dgapE"][:, :]), 16)
        pb.inc("a_wld", g.dma_start(dgapO[:, :], io["dgapO"][:, :]), 16)
        pb.inc("a_cst", g.memset(vstage[:, :, 64:65], 1.0))
    pb.on("gpsimd", ld0)

    vev_after = {}
    kev_after = {}
    qev_after = {}
    for ts in range(8):
        def ldx(sync, ts=ts):
            pb.wait(sync, "a_p1done")
            pb.inc("a_xld", sync.dma_start(ctab[:, :, :], cosk_r[:, :, ts * 512:(ts + 1) * 512]), 16)
            pb.inc("a_xld", sync.dma_start(stab[:, :, :], sink_r[:, :, ts * 512:(ts + 1) * 512]), 16)
        pb.on("sync", ldx)

        def ldx2(g, ts=ts):
            pb.wait(g, "a_p1done")
            pb.inc("a_xldg", g.dma_start(xfm[:, :, :], xk_r[:, :, ts * 512:(ts + 1) * 512]), 16)
        pb.on("gpsimd", ldx2)

        for tt in range(4):
            def mmv(t, ts=ts, tt=tt):
                pb.wait(t, "a_xld")
                pb.wait(t, "a_xldg")
                pb.wait(t, "a_wld")
                pb.wait(t, "a_vev")
                pb.wait(t, "a_kev")
                for hf in range(2):
                    for dc in range(8):
                        ins = t.matmul(psV[:, hf, :], xfm[:, dc, tt * 128:(tt + 1) * 128],
                                       wvb[:, dc, hf * 512:(hf + 1) * 512],
                                       start=(dc == 0), stop=(dc == 7))
                pb.inc("a_vmm", ins)
            pb.on("tensor", mmv)

            def evv(v, ts=ts, tt=tt):
                pb.wait(v, "a_vmm")
                pb.wait(v, "a_cst")
                pb.wait(v, "a_vst")
                v.tensor_copy(vstage[:, 0:8, 0:64],
                              psV[:, 0, :].rearrange("p (h c) -> p h c", h=8))
                v.drain()
                pb.inc("a_vev", v.tensor_copy(
                    vstage[:, 8:16, 0:64], psV[:, 1, :].rearrange("p (h c) -> p h c", h=8)))
            pb.on("vector", evv)

            def stv(sync, ts=ts, tt=tt):
                pb.wait(sync, "a_vev")
                pb.inc("a_vst", sync.dma_start(v_dram[ts * 4 + tt, :, :, :], vstage[:, :, :]), 16)
            pb.on("sync", stv)

        def wrope(v):
            pb.wait(v, "a_xld")
            pb.wait(v, "a_xldg")
            v.drain()
        pb.on("vector", wrope)
        emit_rope(pb, rx, xfm, ctab, stab, tmp1, tmp2, f"rk{ts}")

        for fc in range(8):
            def mmk(t, ts=ts, fc=fc):
                pb.wait(t, "a_rope")
                if fc < 2:
                    pb.wait(t, "a_vev")
                    pb.wait(t, "a_kev")
                else:
                    pb.wait(t, "a_kev", kev_after[(ts, fc - 2)])
                for dc in range(8):
                    ins = t.matmul(psV[:, fc % 2, :], wkb[:, dc, fc * 128:(fc + 1) * 128],
                                   rx[:, dc, :], start=(dc == 0), stop=(dc == 7))
                pb.inc("a_kmm", ins)
            pb.on("tensor", mmk)

            def evk(v, ts=ts, fc=fc):
                pb.wait(v, "a_kmm")
                kev_after[(ts, fc)] = pb.inc("a_kev", v.tensor_copy(kstage[:, fc, :], psV[:, fc % 2, :]))
            pb.on("vector", evk)

        def stk(sync, ts=ts):
            pb.wait(sync, "a_kev")
            pb.inc("a_p1done", sync.dma_start(
                k_dram_r[:, :, ts * 512:(ts + 1) * 512], kstage[:, :, :]), 16)
        pb.on("sync", stk)

    for qs in range(4):
        def ldxq(g, qs=qs):
            pb.wait(g, "a_p1done")
            if qs >= 1:
                pb.wait(g, "a_qev", qev_after[(qs - 1, 7)])
            pb.inc("a_xldg", g.dma_start(xfm[:, :, :], xq_r[:, :, qs * 512:(qs + 1) * 512]), 16)
        pb.on("gpsimd", ldxq)

        def ldxq2(sync, qs=qs):
            pb.wait(sync, "a_p1done")
            if qs >= 1:
                pb.wait(sync, "a_qev", qev_after[(qs - 1, 7)])
            pb.inc("a_xld", sync.dma_start(ctab[:, :, :], cosq_r[:, :, qs * 512:(qs + 1) * 512]), 16)
            pb.inc("a_xld", sync.dma_start(stab[:, :, :], sinq_r[:, :, qs * 512:(qs + 1) * 512]), 16)
        pb.on("sync", ldxq2)

        def wrope2(v):
            pb.wait(v, "a_xld")
            pb.wait(v, "a_xldg")
            v.drain()
        pb.on("vector", wrope2)
        emit_rope(pb, rx, xfm, ctab, stab, tmp1, tmp2, f"rq{qs}")

        for fc in range(8):
            def mmq(t, qs=qs, fc=fc):
                pb.wait(t, "a_rope")
                if qs == 0 and fc < 2:
                    pb.wait(t, "a_kev")
                elif fc < 2:
                    pb.wait(t, "a_qev", qev_after[(qs - 1, 6 + fc)])
                else:
                    pb.wait(t, "a_qev", qev_after[(qs, fc - 2)])
                for dc in range(8):
                    ins = t.matmul(psV[:, fc % 2, :], wqb[:, dc, fc * 128:(fc + 1) * 128],
                                   rx[:, dc, :], start=(dc == 0), stop=(dc == 7))
                pb.inc("a_qmm", ins)
            pb.on("tensor", mmq)

            def evq(v, qs=qs, fc=fc):
                pb.wait(v, "a_qmm")
                if fc == 0:
                    pb.wait(v, "a_qst")
                qev_after[(qs, fc)] = pb.inc("a_qev", v.tensor_copy(
                    kstage[:, fc, :], psV[:, fc % 2, :]))
            pb.on("vector", evq)

        def stq(sync, qs=qs):
            pb.wait(sync, "a_qev", qev_after[(qs, 7)])
            pb.inc("a_qst", sync.dma_start(
                q_dram_r[:, :, qs * 512:(qs + 1) * 512], kstage[:, :, :]), 16)
        pb.on("sync", stq)

    es1.close()

    with contextlib.ExitStack() as es2:
        psSP = _ps(nc, es2, "a_psSP", [128, 2, 512], F32)
        psPO = _ps(nc, es2, "a_psPO", [128, 2, 1024], F32)

        qk_slot_end = {}
        for s in range(NT):
            Ds = DEPTH_PLAN[s]

            def ldq(sync, s=s):
                pb.wait(sync, "a_qst")
                if s >= 2:
                    pb.wait(sync, "a_qkmm", qk_slot_end[s - 2])
                pb.inc("a_qld", sync.dma_start(qslot[:, s % 2, :, :],
                                               q_dram_r[:, :, s * 128:(s + 1) * 128]), 16)
            pb.on("sync", ldq)

            def genmask(v, s=s):
                v.drain()
                v.tensor_scalar(maskE[:, :], kabs[:, :], dgapE[:, s:s + 1], None, op0=ALU.is_le)
                v.drain()
                pb.inc("a_mgen", v.tensor_scalar(maskO[:, :], kabs[:, :], dgapO[:, s:s + 1], None, op0=ALU.is_le))
            pb.on("vector", genmask)

            exp_after = {}
            msk_after = {}
            pvh_after = {}
            qk_after = {}
            for kt in range(Ds):
                bf = kt % 2
                def ldkv(g, s=s, kt=kt, bf=bf):
                    pb.wait(g, "a_p1done")
                    pb.wait(g, "a_vst")
                    if kt >= 2:
                        pb.wait(g, "a_qkmm", qk_after[(kt - 2, 15)])
                        pb.wait(g, "a_pvmm", pvh_after[(kt - 2, 15)])
                    else:
                        pb.wait(g, "a_pvmm")
                    pb.inc("a_kvld", g.dma_start(kslab[:, bf, :, :],
                                                 k_dram_r[:, :, kt * 512:(kt + 1) * 512]), 16)
                    pb.inc("a_kvld", g.dma_start(
                        vslab[:, bf, :, :, :].rearrange("p a h c -> p a (h c)"),
                        v_dram_r[:, kt * 4:(kt + 1) * 4, :]), 16)
                pb.on("gpsimd", ldkv)

                for hh in range(16):
                    sp = hh % 2
                    base = 64 * (hh % 2)
                    fc = hh // 2
                    def mmqk(t, s=s, kt=kt, hh=hh, sp=sp, base=base, fc=fc, bf=bf):
                        pb.wait(t, "a_kvld")
                        pb.wait(t, "a_qld")
                        pb.wait(t, "a_exp")
                        for kc in range(4):
                            ins = t.matmul(psSP[:, sp, kc * 128:(kc + 1) * 128],
                                           kslab[base:base + 64, bf, fc, kc * 128:(kc + 1) * 128],
                                           qslot[base:base + 64, s % 2, fc, :],
                                           start=(kc == 0), stop=(kc == 3))
                        qk_after[(kt, hh)] = pb.inc("a_qkmm", ins)
                    pb.on("tensor", mmqk)

                    def expp(sx, s=s, kt=kt, hh=hh, sp=sp):
                        pb.wait(sx, "a_qkmm", qk_after[(kt, hh)])
                        pb.wait(sx, "a_pvmm")
                        pb.wait(sx, "a_msk")
                        exp_after[(kt, hh)] = pb.inc("a_exp", sx.activation(
                            ptsb[:, hh % 2, :], psSP[:, sp, :], AF.Exp, scale=0.125,
                            bias=visb[:, s, kt:kt + 1]))
                    pb.on("scalar", expp)

                    do_masks = (kt == POS_E[s]) or (kt == POS_O[s])
                    if do_masks:
                        def mskv(v, s=s, kt=kt, hh=hh):
                            pb.wait(v, "a_exp", exp_after[(kt, hh)])
                            pb.wait(v, "a_mgen")
                            ins = None
                            if kt == POS_E[s]:
                                ins = v.tensor_tensor(ptsb[:, hh % 2, :], ptsb[:, hh % 2, :],
                                                      maskE[:, :], op=ALU.mult)
                                v.drain()
                            if kt == POS_O[s]:
                                ins = v.tensor_tensor(ptsb[:, hh % 2, :], ptsb[:, hh % 2, :],
                                                      maskO[:, :], op=ALU.mult)
                                v.drain()
                            msk_after[(kt, hh)] = pb.inc("a_msk", ins)
                        pb.on("vector", mskv)

                    def mmpv(t, s=s, kt=kt, hh=hh, do_masks=do_masks, bf=bf, Ds=Ds):
                        if do_masks:
                            pb.wait(t, "a_msk", msk_after[(kt, hh)])
                        else:
                            pb.wait(t, "a_exp", exp_after[(kt, hh)])
                        if kt == 0 and hh == 0:
                            pb.wait(t, "a_epi")
                        bank = hh // 8
                        col = (hh % 8) * 128
                        first = (kt == 0 and hh % 4 == 0)
                        last = (kt == Ds - 1 and hh % 4 == 3)
                        for kc in range(4):
                            ins = t.matmul(psPO[:, bank, col:col + 65],
                                           ptsb[:, hh % 2, kc * 128:(kc + 1) * 128],
                                           vslab[:, bf, kc, hh, :],
                                           start=(first and kc == 0), stop=(last and kc == 3))
                        pvh_after[(kt, hh)] = pb.inc("a_pvmm", ins)
                    pb.on("tensor", mmpv)

            for hh in range(16):
                def epi(v, s=s, hh=hh):
                    pb.wait(v, "a_pvmm")
                    if hh == 0 and s > 0:
                        pb.wait(v, "a_aost")
                    v.drain()
                    v.reciprocal(rcp[:, :], psPO[:, hh // 8, (hh % 8) * 128 + 64:(hh % 8) * 128 + 65])
                    v.drain()
                    pb.inc("a_epi", v.tensor_scalar(
                        attn_sb[:, hh * 64:(hh + 1) * 64],
                        psPO[:, hh // 8, (hh % 8) * 128:(hh % 8) * 128 + 64],
                        rcp[:, :], None, op0=ALU.mult))
                pb.on("vector", epi)

            def sta(sync, s=s):
                pb.wait(sync, "a_epi")
                pb.inc("a_aost", sync.dma_start(ao_dram[s * 128:(s + 1) * 128, :], attn_sb[:, :]), 16)
            pb.on("sync", sta)
            qk_slot_end[s] = pb.val("a_qkmm")

    tps = _ps(nc, es, "a_tps", [128, 2, 1024], BF16)
    abf = _sb(nc, es, "a_abf", [128, 1024], BF16)
    aT = _sb(nc, es, "a_aT", [128, 8, 128], BF16)
    bobc = _sb(nc, es, "a_bobc", [128, 1024], F32)
    rt = _sb(nc, es, "a_rt", [128, 1024], F32)

    def ldwo(g):
        pb.wait(g, "a_qmm")
        pb.wait(g, "a_qst")
        pb.wait(g, "a_vst")
        pb.inc("a_wld2", g.dma_start(wob[:, :, :], wo_r[:, :, :]), 16)
        pb.inc("a_wld2", g.dma_start(bobc[:, :], io["bo_bc"][:, :]), 16)
    pb.on("gpsimd", ldwo)

    def ldstream(sync):
        for i in range(NT):
            pb.inc("a_sld", sync.dma_start(stream[:, i, :],
                                           io["x_tm"][i * 128:(i + 1) * 128, :]), 16)
    pb.on("sync", ldstream)

    evO_after = {}
    tre2_after = {}
    for i in range(NT):
        def lda(sync, i=i):
            pb.wait(sync, "a_aost")
            pb.wait(sync, "a_trm2")
            pb.inc("a_ald", sync.dma_start(abf[:, :], ao_dram[i * 128:(i + 1) * 128, :]), 16)
        pb.on("sync", lda)

        def trp(t, i=i):
            pb.wait(t, "a_ald")
            if i >= 2:
                pb.wait(t, "a_tre2", tre2_after[i - 2])
            for dc in range(7):
                t.transpose(tps[:, i % 2, dc * 128:(dc + 1) * 128],
                            abf[:, dc * 128:(dc + 1) * 128], identb[:, :])
            pb.inc("a_trm2", t.transpose(tps[:, i % 2, 7 * 128:8 * 128],
                                         abf[:, 7 * 128:8 * 128], identb[:, :]))
        pb.on("tensor", trp)

        def tev(v, i=i):
            pb.wait(v, "a_trm2")
            pb.wait(v, "a_omm")
            tre2_after[i] = pb.inc("a_tre2", v.tensor_copy(
                aT[:, :, :], tps[:, i % 2, :].rearrange("p (a b) -> p a b", a=8)))
        pb.on("vector", tev)

        def mmo(t, i=i):
            pb.wait(t, "a_wld2")
            pb.wait(t, "a_tre2", tre2_after[i])
            pb.wait(t, "a_evO", evO_after.get(i - 1, 0))
            for hf in range(2):
                for dc in range(8):
                    ins = t.matmul(psV[:, hf, :], aT[:, dc, :],
                                   wob[:, dc, hf * 512:(hf + 1) * 512],
                                   start=(dc == 0), stop=(dc == 7))
            pb.inc("a_omm", ins)
        pb.on("tensor", mmo)

        def evo(v, i=i):
            pb.wait(v, "a_omm")
            pb.wait(v, "a_sld")
            pb.wait(v, "ac")
            for hf in range(2):
                v.tensor_tensor(rt[:, hf * 512:(hf + 1) * 512], psV[:, hf, :],
                                bobc[:, hf * 512:(hf + 1) * 512], op=ALU.add)
            v.drain()
            for hf in range(2):
                ins = v.tensor_tensor(rt[:, hf * 512:(hf + 1) * 512],
                                      rt[:, hf * 512:(hf + 1) * 512],
                                      stream[:, i, hf * 512:(hf + 1) * 512], op=ALU.add)
            evO_after[i] = pb.inc("a_evO", ins)
        pb.on("vector", evo)

        emit_ln(pb, sc, rt[:, :], stream[:, i, :], ready=("a_evO", evO_after[i]))


def stage_attn(inp):
    d = {}
    perm = PERM
    Wq = np.asarray(inp["L0_attn_Wq"], np.float32)
    Wk = np.asarray(inp["L0_attn_Wk"], np.float32)
    Wv = np.asarray(inp["L0_attn_Wv"], np.float32)
    Wo = np.asarray(inp["L0_attn_Wo"], np.float32)
    bo = np.asarray(inp["L0_attn_bo"], np.float32)
    d["wqT"] = np.ascontiguousarray(Wq[:, perm].T)
    d["wkT"] = np.ascontiguousarray(Wk[:, perm].T)
    d["wvT"] = np.ascontiguousarray(Wv[:, perm].T)
    d["woT"] = np.ascontiguousarray(Wo.T)
    d["bo_bc"] = np.ascontiguousarray(np.broadcast_to(bo, (128, 1024)))
    import ml_dtypes
    theta = (10000.0 ** (-np.arange(0, D, 2, dtype=np.float64) / D))
    ang = np.arange(T, dtype=np.float64)[:, None] * theta[None, :]
    d["cos_k"] = np.ascontiguousarray(np.cos(ang).T.astype(ml_dtypes.bfloat16))
    d["sin_k"] = np.ascontiguousarray(np.sin(ang).T.astype(ml_dtypes.bfloat16))
    i_ = np.arange(128)[:, None, None]
    c_ = np.arange(4)[None, :, None]
    j_ = np.arange(128)[None, None, :]
    d["kabs"] = np.ascontiguousarray((c_ * 128 + i_ - j_).reshape(128, 512).astype(np.float32))
    d["ident"] = np.eye(128, dtype=np.float32)
    return d


def stage_attn_core(inp, core, shared):
    import ml_dtypes
    b, half = core // 2, core % 2
    x = np.asarray(inp["L0_attn_input"], np.float32)[b]
    own = _own_tokens(half)
    d = {}
    d["x_tm"] = np.ascontiguousarray(x[own])
    d["x_fmp"] = np.ascontiguousarray(x[:, PERM].T)
    d["x_fmq"] = np.ascontiguousarray(x[own][:, PERM].T)
    theta = (10000.0 ** (-np.arange(0, D, 2, dtype=np.float64) / D))
    angq = own[:, None].astype(np.float64) * theta[None, :]
    d["cos_q"] = np.ascontiguousarray(np.cos(angq).T.astype(ml_dtypes.bfloat16))
    d["sin_q"] = np.ascontiguousarray(np.sin(angq).T.astype(ml_dtypes.bfloat16))
    chunks = _chunks_of(half)
    qa = np.array([c * 128 for c in chunks])
    dtrue = np.array([(c * 128 + 128 + 511) // 512 for c in chunks])
    visb = np.zeros((128, NT, 8), np.float32)
    for s in range(NT):
        visb[:, s, dtrue[s]:] = -1e30
    d["visb"] = visb
    dgE = np.zeros((128, NT), np.float32)
    dgO = np.zeros((128, NT), np.float32)
    for s in range(NT):
        pe, po = POS_E[s], POS_O[s]
        dgE[:, s] = (qa[s] - pe * 512) if pe == dtrue[s] - 1 else 1e9
        dgO[:, s] = (qa[s] - po * 512) if po == dtrue[s] - 1 else 1e9
    d["dgapE"] = dgE
    d["dgapO"] = dgO
    return d


def build_attn_launch(shared):
    nc = _new_nc()
    hs = {k: nc.dram_tensor(k, list(v.shape), BF16 if v.dtype.kind == "V" or str(v.dtype) == "bfloat16" else F32,
                            kind="ExternalInput").ap()
          for k, v in shared.items()}
    percore_shapes = {"x_tm": (TOK, D), "x_fmp": (D, T), "x_fmq": (D, TOK),
                      "cos_q": (512, TOK), "sin_q": (512, TOK),
                      "visb": (128, NT, 8), "dgapE": (128, NT), "dgapO": (128, NT)}
    for k, shp in percore_shapes.items():
        dt = BF16 if k in ("cos_q", "sin_q") else F32
        hs[k] = nc.dram_tensor(k, list(shp), dt, kind="ExternalInput").ap()
    y_h = nc.dram_tensor("y_out", [TOK, D], F32, kind="ExternalOutput").ap()
    es = contextlib.ExitStack()
    with es:
        pb = PB(nc, es)
        stream = _sb(nc, es, "stream", [128, NT, 1024], F32)
        identb = _sb(nc, es, "identb", [128, 128], BF16)
        csts = emit_const_pool(pb, nc, es, [EPS, 1.0])
        lnsc = mk_ln_scratch(nc, es, "ln_", csts[EPS])

        def ldi(g):
            pb.inc("wld", g.dma_start(identb[:, :], hs["ident"][:, :]), 16)
        pb.on("gpsimd", ldi)

        io = dict(hs)
        io.update(stream=stream, identb=identb, lnsc=lnsc)
        emit_attn(pb, es, io)
        emit_stream_store(pb, stream, y_h, ["lnw"])
    return nc


def emit_moe(pb, es, io):
    nc = pb.nc
    stream = io["stream"]
    sc = io["lnsc"]
    identb = io["identb"]
    NE = 8

    with contextlib.ExitStack() as e1:
        xT = _sb(nc, e1, "m_xT", [128, 8, TOK], BF16)
        acc = _sb(nc, e1, "m_acc", [128, NT, 1024], BF16)
        hbuf = _sb(nc, e1, "m_hbuf", [128, 16, 512], BF16)
        xbf = _sb(nc, e1, "m_xbf", [128, 1024], BF16)
        w1bi = _sb(nc, e1, "m_w1bi", [128, 2, 8, 128], BF16)
        w1bg = _sb(nc, e1, "m_w1bg", [128, 2, 8, 128], BF16)
        w2b = _sb(nc, e1, "m_w2b", [128, 16, 512], BF16)
        wgb = _sb(nc, e1, "m_wgb", [128, 8, 8], BF16)
        wgf = _sb(nc, e1, "m_wgf", [128, 8, 8], F32)
        wglo = _sb(nc, e1, "m_wglo", [128, 8, 8], BF16)
        xlo = _sb(nc, e1, "m_xlo", [128, 1024], BF16)
        xloT = _sb(nc, e1, "m_xloT", [128, 8, 128], BF16)
        logit = _sb(nc, e1, "m_logit", [128, NT, 8], F32)
        wcoef = _sb(nc, e1, "m_wcoef", [128, NT, 8], F32)
        mk1s = _sb(nc, e1, "m_mk1s", [128, NT, 8], F32)
        wcB = _sb(nc, e1, "m_wcB", [128, NT, 8], F32)
        pcnt = _sb(nc, e1, "m_pcnt", [128, NT, 8], F32)
        utri = _sb(nc, e1, "m_utri", [128, 128], F32)
        worder = _sb(nc, e1, "m_worder", [16, 2], F32)
        ones_c = _sb(nc, e1, "m_onesc", [128, 1], F32)
        ones_r = _sb(nc, e1, "m_onesr", [1, 128], F32)
        sums_sb = _sb(nc, e1, "m_sums", [1, 16, 8], F32)
        cum_sb = _sb(nc, e1, "m_cum", [1, 16, 8], F32)
        tot2 = _sb(nc, e1, "m_tot2", [1, 2, 8], F32)
        gath = _sb(nc, e1, "m_gath", [16, 8], F32)
        offs2 = _sb(nc, e1, "m_offs2", [2, 8], F32)
        offsr = _sb(nc, e1, "m_offsr", [1, 16], F32)
        rowfull = _sb(nc, e1, "m_rowfull", [1, 16, 8], F32)
        rowb = _sb(nc, e1, "m_rowb", [128, 16, 8], F32)
        posb = _sb(nc, e1, "m_posb", [128, 8], F32)
        kp = _sb(nc, e1, "m_kp", [128, 8], F32)
        ccin = nc.dram_tensor("m_ccin", [2, 8], F32, kind="Internal").ap()
        ccout = nc.dram_tensor("m_ccout", [16, 8], F32, kind="Internal").ap()
        t2a = _sb(nc, e1, "m_t2a", [128, 2, 512], BF16)
        v1c = _sb(nc, e1, "m_v1", [128, 1], F32)
        v2c = _sb(nc, e1, "m_v2", [128, 1], F32)
        g1c = _sb(nc, e1, "m_g1", [128, 1], F32)
        g2c = _sb(nc, e1, "m_g2", [128, 1], F32)
        mk1 = _sb(nc, e1, "m_mk1", [128, 8], F32)
        mk2 = _sb(nc, e1, "m_mk2", [128, 8], F32)
        rt = _sb(nc, e1, "m_rt", [128, 1024], F32)
        tps = _ps(nc, e1, "m_tps", [128, 2, 1024], BF16)
        psA = _ps(nc, e1, "m_psA", [128, 2, 512], F32)
        psB = _ps(nc, e1, "m_psB", [128, 2, 512], F32)
        psO = _ps(nc, e1, "m_psO", [128, 2, 512], F32)

        w1r = io["moe_w1ft"]
        w2r = io["moe_w2ft"]
        wgr = io["moe_wg"].rearrange("(dc p) f -> p dc f", p=128)

        def ld0(g):
            pb.inc("m_wld", g.dma_start(wgb[:, :, :], wgr[:, :, :]), 16)
            pb.inc("m_wld", g.dma_start(wgf[:, :, :], wgr[:, :, :]), 16)
            pb.inc("m_cst", g.memset(acc[:, :, :], 0.0))
        pb.on("gpsimd", ld0)

        def wglo_mk(v):
            pb.wait(v, "m_wld")
            pb.inc("m_wgl", v.tensor_tensor(wglo[:, :, :], wgf[:, :, :], wgb[:, :, :],
                                            op=ALU.subtract))
        pb.on("vector", wglo_mk)

        gev_after = {}
        for i in range(NT):
            def vc(v, i=i):
                pb.wait(v, "trm")
                v.tensor_copy(xbf[:, :], stream[:, i, :])
                v.drain()
                pb.inc("trc", v.tensor_tensor(xlo[:, :], stream[:, i, :], xbf[:, :],
                                              op=ALU.subtract))
            pb.on("vector", vc)

            def tt(t, i=i):
                pb.wait(t, "trc")
                pb.wait(t, "m_trel")
                pb.wait(t, "tre", max(0, pb.val("tre") - 1))
                for dc in range(8):
                    t.transpose(tps[:, i % 2, dc * 128:(dc + 1) * 128],
                                xbf[:, dc * 128:(dc + 1) * 128], identb[:, :])
                pb.inc("trm", t.transpose(tps[:, (i + 1) % 2, 0:128],
                                          xlo[:, 0:128], identb[:, :]))
            pb.on("tensor", tt)

            def ve(v, i=i):
                pb.wait(v, "trm")
                pb.inc("tre", v.tensor_copy(
                    xT[:, :, i * 128:(i + 1) * 128],
                    tps[:, i % 2, :].rearrange("p (a b) -> p a b", a=8)))
            pb.on("vector", ve)

            def tt2(t, i=i):
                pb.wait(t, "tre")
                for dc in range(1, 8):
                    t.transpose(tps[:, (i + 1) % 2, dc * 128:(dc + 1) * 128],
                                xlo[:, dc * 128:(dc + 1) * 128], identb[:, :])
                pb.inc("m_trml", t.transpose(tps[:, (i + 1) % 2, 0:128],
                                             xlo[:, 0:128], identb[:, :]))
            pb.on("tensor", tt2)

            def ve2(v, i=i):
                pb.wait(v, "m_trml")
                pb.wait(v, "m_gmm")
                pb.inc("m_trel", v.tensor_copy(
                    xloT[:, :, :], tps[:, (i + 1) % 2, :].rearrange("p (a b) -> p a b", a=8)))
            pb.on("vector", ve2)

            def mmg(t, i=i):
                pb.wait(t, "m_wld")
                pb.wait(t, "m_wgl")
                pb.wait(t, "m_trel")
                if i >= 1:
                    pb.wait(t, "m_gev", gev_after[i - 1])
                for dc in range(8):
                    t.matmul(psA[:, i % 2, 0:8], xT[:, dc, i * 128:(i + 1) * 128],
                             wgb[:, dc, :], start=(dc == 0), stop=False)
                for dc in range(8):
                    t.matmul(psA[:, i % 2, 0:8], xT[:, dc, i * 128:(i + 1) * 128],
                             wglo[:, dc, :], start=False, stop=False)
                for dc in range(8):
                    ins = t.matmul(psA[:, i % 2, 0:8], xloT[:, dc, :],
                                   wgb[:, dc, :], start=False, stop=(dc == 7))
                pb.inc("m_gmm", ins)
            pb.on("tensor", mmg)

            def evg(v, i=i):
                pb.wait(v, "m_gmm")
                gev_after[i] = pb.inc("m_gev", v.tensor_copy(logit[:, i, :], psA[:, i % 2, 0:8]))
            pb.on("vector", evg)

        for i in range(NT):
            def route(v, i=i):
                pb.wait(v, "m_gev")
                lg = logit[:, i, :]
                v.drain()
                v.tensor_reduce(v1c[:, :], lg, axis=AX, op=ALU.max)
                v.drain()
                v.tensor_scalar(mk1s[:, i, :], lg, v1c[:, :], None, op0=ALU.is_ge)
                v.drain()
                v.scalar_tensor_tensor(mk2[:, :], mk1s[:, i, :], -1e30, lg, op0=ALU.mult, op1=ALU.add)
                v.drain()
                v.tensor_reduce(v2c[:, :], mk2[:, :], axis=AX, op=ALU.max)
                v.drain()
                v.tensor_scalar(mk2[:, :], mk2[:, :], v2c[:, :], None, op0=ALU.is_ge)
                v.drain()
                pb.inc("m_rt1", v.tensor_tensor(v1c[:, :], v1c[:, :], v2c[:, :], op=ALU.subtract))
            pb.on("vector", route)

            def sg(sx, i=i):
                pb.wait(sx, "m_rt1")
                pb.inc("m_sg", sx.activation(g1c[:, :], v1c[:, :], AF.Exp, scale=-1.0))
            pb.on("scalar", sg)

            def route2(v, i=i):
                pb.wait(v, "m_sg")
                v.tensor_scalar(g1c[:, :], g1c[:, :], 1.0, None, op0=ALU.add)
                v.drain()
                v.reciprocal(g1c[:, :], g1c[:, :])
                v.drain()
                v.tensor_scalar(g2c[:, :], g1c[:, :], -1.0, 1.0, op0=ALU.mult, op1=ALU.add)
                v.drain()
                v.tensor_scalar(wcoef[:, i, :], mk1s[:, i, :], g1c[:, :], None, op0=ALU.mult)
                v.tensor_scalar(wcB[:, i, :], mk2[:, :], g2c[:, :], None, op0=ALU.mult)
                v.drain()
                pb.inc("m_rt2", v.tensor_tensor(pcnt[:, i, :], mk1s[:, i, :], mk2[:, :], op=ALU.add))
            pb.on("vector", route2)

        CAP = 5120.0

        def ldcap(g):
            pb.inc("m_capld", g.dma_start(utri[:, :], io["utri"][:, :]), 16)
            pb.inc("m_capld", g.dma_start(worder[:, :], io["worder"][:, :]), 16)
            g.memset(ones_c[:, :], 1.0)
            pb.inc("m_capc", g.memset(ones_r[:, :], 1.0))
        pb.on("gpsimd", ldcap)

        def mmsum(t):
            pb.wait(t, "m_rt2")
            pb.wait(t, "m_capc")
            pb.wait(t, "m_capld")
            for si in range(NT):
                ins = t.matmul(psB[0:1, 0, si * 8:(si + 1) * 8], ones_c[:, :], pcnt[:, si, :],
                               start=(si == 0), stop=(si == NT - 1))
            pb.inc("m_summ", ins)
        pb.on("tensor", mmsum)

        def evsum(v):
            pb.wait(v, "m_summ")
            v.tensor_copy(sums_sb[0:1, :, :], psB[0:1, 0, 0:128].rearrange("p (a b) -> p a b", a=16))
            v.drain()
            v.memset(cum_sb[0:1, 0, :], 0.0)
            v.memset(cum_sb[0:1, 8, :], 0.0)
            v.drain()
            for u in range(2):
                for si in range(1, 8):
                    sl = u * 8 + si
                    v.tensor_tensor(cum_sb[0:1, sl, :], cum_sb[0:1, sl - 1, :],
                                    sums_sb[0:1, sl - 1, :], op=ALU.add)
                    v.drain()
            pb.inc("m_cum", v.tensor_tensor(
                tot2[0:1, :, :], cum_sb[0:1, 7::8, :], sums_sb[0:1, 7::8, :], op=ALU.add))
        pb.on("vector", evsum)

        def ccst(sync):
            pb.wait(sync, "m_cum")
            pb.inc("m_ccst", sync.dma_start(ccin, tot2[0:1, :, :].rearrange("p a b -> p (a b)")), 16)
        pb.on("sync", ccst)

        def ccgo(g):
            pb.wait(g, "m_ccst")
            g.collective_compute("AllGather", ALU.bypass,
                                 replica_groups=[list(range(NCORE))],
                                 ins=[ccin], outs=[ccout]).then_inc(pb.semh("m_cc")[0], 1)
            pb.semh("m_cc")[1] += 1
        pb.on("gpsimd", ccgo)

        def ccld(sync):
            pb.wait(sync, "m_cc")
            pb.inc("m_ccld", sync.dma_start(gath[:, :], ccout), 16)
        pb.on("sync", ccld)

        def mmoff(t):
            pb.wait(t, "m_ccld")
            pb.inc("m_offm", t.matmul(psA[0:2, 0, 0:8], worder[:, :], gath[:, :],
                                      start=True, stop=True))
        pb.on("tensor", mmoff)

        def evoff(v):
            pb.wait(v, "m_offm")
            pb.inc("m_offev", v.tensor_copy(offs2[:, :], psA[0:2, 0, 0:8]))
        pb.on("vector", evoff)

        def offdma(sync):
            pb.wait(sync, "m_offev")
            pb.inc("m_offfl", sync.dma_start(offsr[:, :], offs2[:, :]), 16)
        pb.on("sync", offdma)

        def mkrow(v):
            pb.wait(v, "m_offfl")
            orow = bass.AP(offsr, 0, [[16, 1], [8, 2], [0, 8], [1, 8]])
            pb.inc("m_rowf", v.tensor_tensor(
                rowfull[0:1, :, :].rearrange("p (u s) e -> p u s e", u=2),
                cum_sb[0:1, :, :].rearrange("p (u s) e -> p u s e", u=2),
                orow, op=ALU.add))
        pb.on("vector", mkrow)

        def mmrow(t):
            pb.wait(t, "m_rowf")
            pb.inc("m_rbm", t.matmul(psA[:, 1, 0:128], ones_r[:, :],
                                     rowfull[0:1, :, :].rearrange("p a b -> p (a b)"),
                                     start=True, stop=True))
        pb.on("tensor", mmrow)

        def evrow(v):
            pb.wait(v, "m_rbm")
            pb.inc("m_rbev", v.tensor_copy(rowb[:, :, :],
                                           psA[:, 1, 0:128].rearrange("p (a b) -> p a b", a=16)))
        pb.on("vector", evrow)

        pos_after = {}
        for si in range(NT):
            def mmpos(t, si=si):
                pb.wait(t, "m_rbev")
                if si >= 2:
                    pb.wait(t, "m_wcap", pos_after[si - 2])
                ins = t.matmul(psB[:, si % 2, 0:8], utri[:, :], pcnt[:, si, :],
                               start=True, stop=True)
                pb.inc("m_posm", ins)
            pb.on("tensor", mmpos)

            def evpos(v, si=si):
                pb.wait(v, "m_posm")
                v.tensor_tensor(posb[:, :], psB[:, si % 2, 0:8], rowb[:, si, :], op=ALU.add)
                v.drain()
                v.tensor_scalar(kp[:, :], posb[:, :], CAP, None, op0=ALU.is_lt)
                v.drain()
                v.tensor_tensor(wcoef[:, si, :], wcoef[:, si, :], kp[:, :], op=ALU.mult)
                v.drain()
                v.tensor_tensor(posb[:, :], posb[:, :], mk1s[:, si, :], op=ALU.add)
                v.drain()
                v.tensor_scalar(kp[:, :], posb[:, :], CAP, None, op0=ALU.is_lt)
                v.drain()
                v.tensor_tensor(kp[:, :], wcB[:, si, :], kp[:, :], op=ALU.mult)
                v.drain()
                pos_after[si] = pb.inc("m_wcap", v.tensor_tensor(
                    wcoef[:, si, :], wcoef[:, si, :], kp[:, :], op=ALU.add))
            pb.on("vector", evpos)

        NSL = 4
        for e in range(NE):
            for sl in range(NSL):
                mmh_after, evh_after = {}, {}
                for fi in range(16):
                    bf = fi % 2
                    def ldw1(g, e=e, sl=sl, fi=fi, bf=bf):
                        if fi >= 2:
                            pb.wait(g, "m_hmm", mmh_after[fi - 2])
                        else:
                            pb.wait(g, "m_hmm")
                        pb.inc("m_w1ld", g.dma_start(w1bi[:, bf], w1r[e, fi]), 16)
                        pb.inc("m_w1ld", g.dma_start(w1bg[:, bf], w1r[e, 16 + fi]), 16)
                    pb.on("gpsimd", ldw1)

                    def mmh(t, e=e, sl=sl, fi=fi, bf=bf):
                        pb.wait(t, "m_w1ld")
                        pb.wait(t, "tre")
                        if fi >= 2:
                            pb.wait(t, "m_hev", evh_after[fi - 2])
                        else:
                            pb.wait(t, "m_hev")
                            pb.wait(t, "m_gev")
                            pb.wait(t, "m_wcap", pb.val("m_wcap"))
                        for dc in range(8):
                            t.matmul(psA[:, bf], w1bi[:, bf, dc, :], xT[:, dc, sl * 512:(sl + 1) * 512],
                                     start=(dc == 0), stop=(dc == 7))
                        for dc in range(8):
                            ins = t.matmul(psB[:, bf], w1bg[:, bf, dc, :], xT[:, dc, sl * 512:(sl + 1) * 512],
                                           start=(dc == 0), stop=(dc == 7))
                        mmh_after[fi] = pb.inc("m_hmm", ins)
                    pb.on("tensor", mmh)

                    def evh1(sx, e=e, fi=fi, bf=bf):
                        pb.wait(sx, "m_hmm", mmh_after[fi])
                        if fi >= 2:
                            pb.wait(sx, "m_hev", evh_after[fi - 2])
                        pb.inc("m_hact", sx.activation(t2a[:, bf, :], psA[:, bf], AF.Gelu))
                    pb.on("scalar", evh1)

                    def evh2(v, e=e, fi=fi, bf=bf):
                        pb.wait(v, "m_hact")
                        pb.wait(v, "m_omm")
                        evh_after[fi] = pb.inc("m_hev", v.tensor_tensor(
                            hbuf[:, fi, :], t2a[:, bf, :], psB[:, bf], op=ALU.mult))
                    pb.on("vector", evh2)

                for hf in range(2):
                    def ldw2(g, e=e, sl=sl, hf=hf):
                        pb.wait(g, "m_omm")
                        pb.inc("m_w2ld", g.dma_start(w2b[:, :, :], w2r[e, hf]), 16)
                    pb.on("gpsimd", ldw2)
                    for tt in range(4):
                        def mmo(t, e=e, sl=sl, tt=tt, hf=hf):
                            pb.wait(t, "m_w2ld")
                            pb.wait(t, "m_hev")
                            pb.wait(t, "m_oev", max(0, pb.val("m_oev") - 1))
                            for ic in range(16):
                                ins = t.matmul(psO[:, tt % 2],
                                               hbuf[:, ic, tt * 128:(tt + 1) * 128],
                                               w2b[:, ic, :],
                                               start=(ic == 0), stop=(ic == 15))
                            pb.inc("m_omm", ins)
                        pb.on("tensor", mmo)

                        def evo(v, e=e, sl=sl, tt=tt, hf=hf):
                            pb.wait(v, "m_omm")
                            pb.wait(v, "m_wcap")
                            pb.wait(v, "m_cst")
                            ti = sl * 4 + tt
                            pb.inc("m_oev", v.scalar_tensor_tensor(
                                acc[:, ti, hf * 512:(hf + 1) * 512], psO[:, tt % 2],
                                wcoef[:, ti, e:e + 1],
                                acc[:, ti, hf * 512:(hf + 1) * 512], op0=ALU.mult, op1=ALU.add))
                        pb.on("vector", evo)

        for i in range(NT):
            def evr(v, i=i):
                pb.wait(v, "m_oev")
                pb.wait(v, "ac")
                pb.inc("m_res", v.tensor_tensor(rt[:, :], acc[:, i, :], stream[:, i, :], op=ALU.add))
            pb.on("vector", evr)
            emit_ln(pb, sc, rt[:, :], stream[:, i, :], ready=("m_res", pb.val("m_res")))


def stage_moe(inp):
    d = {}
    d["moe_wg"] = _t(inp["L0_ffn_Wg"])
    w1 = np.asarray(inp["L0_ffn_W1"], np.float32)
    d["moe_w1ft"] = np.ascontiguousarray(
        w1.reshape(8, 8, 128, 32, 128).transpose(0, 3, 2, 1, 4))
    w2 = np.asarray(inp["L0_ffn_W2"], np.float32)
    d["moe_w2ft"] = np.ascontiguousarray(
        w2.reshape(8, 16, 128, 2, 512).transpose(0, 3, 2, 1, 4))
    d["ident"] = np.eye(128, dtype=np.float32)
    d["utri"] = np.triu(np.ones((128, 128), np.float32), 1)
    return d


def _run_seq_pos(core, run):
    b, half = core // 2, core % 2
    if half == 0:
        return 4 * b + (0 if run == 0 else 3)
    return 4 * b + (1 if run == 0 else 2)


def stage_moe_core(core):
    w = np.zeros((16, 2), np.float32)
    for rc in range(NCORE):
        for ru in range(2):
            j = rc * 2 + ru
            for myu in range(2):
                if _run_seq_pos(rc, ru) < _run_seq_pos(core, myu):
                    w[j, myu] = 1.0
    return {"worder": w}


def build_moe_launch(staged):
    nc = _new_nc()
    hs = {k: _dram_in(nc, k, v) for k, v in staged.items()}
    hs["worder"] = nc.dram_tensor("worder", [16, 2], F32, kind="ExternalInput").ap()
    x_h = nc.dram_tensor("x_in", [TOK, D], F32, kind="ExternalInput").ap()
    y_h = nc.dram_tensor("y_out", [TOK, D], F32, kind="ExternalOutput").ap()
    es = contextlib.ExitStack()
    with es:
        pb = PB(nc, es)
        stream = _sb(nc, es, "stream", [128, NT, 1024], F32)
        identb = _sb(nc, es, "identb", [128, 128], BF16)
        csts = emit_const_pool(pb, nc, es, [EPS, 1.0])
        lnsc = mk_ln_scratch(nc, es, "ln_", csts[EPS])

        def ldi(g):
            pb.inc("wld", g.dma_start(identb[:, :], hs["ident"][:, :]), 16)
        pb.on("gpsimd", ldi)

        emit_stream_load(pb, stream, x_h)
        io = dict(hs)
        io.update(stream=stream, identb=identb, lnsc=lnsc)
        emit_moe(pb, es, io)
        emit_stream_store(pb, stream, y_h, ["lnw"])
    return nc


def emit_mamba(pb, es, io):
    nc = pb.nc
    stream = io["stream"]
    sc = io["lnsc"]
    identb = io["identb"]

    z_dram = nc.dram_tensor("c_zdram", [TOK, 2048], BF16, kind="Internal").ap()
    xbcr_dram = nc.dram_tensor("c_xbcrdram", [128, 18, 2, 1024], BF16, kind="Internal").ap()
    y_dram = nc.dram_tensor("c_ydram", [TOK, 2048], F32, kind="Internal").ap()
    hl_in = nc.dram_tensor("c_hlin", [128, 108], BF16, kind="Internal").ap()
    hl_out = nc.dram_tensor("c_hlout", [256, 108], BF16, kind="Internal").ap()
    st_in = nc.dram_tensor("c_stin", [257, 2048], F32, kind="Internal").ap()
    st_out = nc.dram_tensor("c_stout", [514, 2048], F32, kind="Internal").ap()
    io["_dbg"] = {"y": y_dram, "z": z_dram, "x": xbcr_dram}

    ipr = io["c_ipT"].rearrange("(dc p) f -> p dc f", p=128)
    opr = io["c_opT"].rearrange("(ic p) f -> p ic f", p=128)

    dtr = _sb(nc, es, "c_dtr", [128, NT, 32], F32)
    acum_all = _sb(nc, es, "c_acumall", [128, NT, 32], F32)
    cdrow_all = _sb(nc, es, "c_cdrow", [1, NT, 32], F32)
    Call = _sb(nc, es, "c_Call", [128, NT, 128], BF16)
    SiniA = _sb(nc, es, "c_SiniA", [128, 2048], F32)
    SiniB = _sb(nc, es, "c_SiniB", [128, 2048], F32)
    identf = _sb(nc, es, "c_identf", [128, 128], F32)
    ones_r = _sb(nc, es, "c_onesr", [1, 128], F32)
    rsum = _sb(nc, es, "c_rsum", [1, 32], F32)
    rrow = _sb(nc, es, "c_rrow", [1, 32], F32)
    crho = _sb(nc, es, "c_crho", [128, 1], F32)
    crho2 = _sb(nc, es, "c_crho2", [1, 1], F32)
    cw1 = _sb(nc, es, "c_cw1", [128, 1], F32)
    cdsel = _sb(nc, es, "c_cdsel", [1, 1], F32)
    cdseli = _sb(nc, es, "c_cdseli", [1, 1], F32)
    cselA = _sb(nc, es, "c_cselA", [128, 1], F32)
    cselE = _sb(nc, es, "c_cselE", [128, 1], F32)
    cselO = _sb(nc, es, "c_cselO", [128, 1], F32)
    haloA = _sb(nc, es, "c_haloA", [128, 18, 3], BF16)
    haloB = _sb(nc, es, "c_haloB", [128, 18, 3], BF16)
    tps = _ps(nc, es, "c_tps", [128, 2, 1024], BF16)
    psR = _ps(nc, es, "c_psR", [128, 1024], F32)
    psY = _ps(nc, es, "c_psY", [128, 512], F32)
    psYo = _ps(nc, es, "c_psYo", [128, 512], F32)
    psU = _ps(nc, es, "c_psU", [128, 512], F32)

    def ld0(g):
        pb.inc("c_wld", g.dma_start(identf[:, :], io["ident"][:, :]), 16)
        pb.inc("c_wld", g.dma_start(crho[:, :], io["c_rho"][:, :]), 16)
        pb.inc("c_wld", g.dma_start(crho2[:, :], io["c_rho2"][:, :]), 16)
        pb.inc("c_wld", g.dma_start(cw1[:, :], io["c_w1"][:, :]), 16)
        pb.inc("c_wld", g.dma_start(cdsel[:, :], io["c_dsel"][:, :]), 16)
        pb.inc("c_wld", g.dma_start(cdseli[:, :], io["c_dseli"][:, :]), 16)
        pb.inc("c_wld", g.dma_start(cselA[:, :], io["c_selA"][:, :]), 16)
        pb.inc("c_wld", g.dma_start(cselE[:, :], io["c_selE"][:, :]), 16)
        pb.inc("c_wld", g.dma_start(cselO[:, :], io["c_selO"][:, :]), 16)
        pb.inc("c_cst", g.memset(ones_r[:, :], 1.0))
    pb.on("gpsimd", ld0)

    with contextlib.ExitStack() as e1:
        xn = _sb(nc, e1, "c_xn", [128, NT, 1024], BF16)
        nwb = _sb(nc, e1, "c_nwb", [128, 1024], F32)
        nbb = _sb(nc, e1, "c_nbb", [128, 1024], F32)
        ipb = _sb(nc, e1, "c_ipb", [128, 8, 2048], BF16)
        xnT = _sb(nc, e1, "c_xnT", [128, 8, 512], BF16)
        xbf1 = _sb(nc, e1, "c_xbf1", [128, 1024], BF16)
        zst = _sb(nc, e1, "c_zst", [128, 2048], BF16)
        xbcst = _sb(nc, e1, "c_xbcst", [128, 18, 512], BF16)

        def ldnw(g):
            pb.inc("c_wld", g.dma_start(nwb[:, :], io["c_normw"][:, :]), 16)
            pb.inc("c_wld", g.dma_start(nbb[:, :], io["c_normb"][:, :]), 16)
        pb.on("gpsimd", ldnw)

        for i in range(NT):
            def wv(v, i=i):
                pb.wait(v, "c_wld")
                pb.inc("c_lnx", v.tensor_copy(sc["lnt"][:, 0:1], stream[:, i, 0:1]))
            pb.on("vector", wv)
            emit_ln(pb, sc, stream[:, i, :], xn[:, i, :],
                    ready=("c_lnx", pb.val("c_lnx")), gain_bc=nwb[:, :], bias_bc=nbb[:, :])

        for sl in range(4):
            tiles = list(range(sl * 4, sl * 4 + 4))
            for pos, i in enumerate(tiles):
                def vc(v, i=i):
                    pb.wait(v, "lnw")
                    pb.wait(v, "c_trm")
                    pb.inc("c_trc", v.tensor_copy(xbf1[:, :], xn[:, i, :]))
                pb.on("vector", vc)

                def tt(t, pos=pos):
                    pb.wait(t, "c_trc")
                    pb.wait(t, "c_tre", max(0, pb.val("c_tre") - 1))
                    for dc in range(7):
                        t.transpose(tps[:, pos % 2, dc * 128:(dc + 1) * 128],
                                    xbf1[:, dc * 128:(dc + 1) * 128], identb[:, :])
                    pb.inc("c_trm", t.transpose(tps[:, pos % 2, 896:1024],
                                                xbf1[:, 896:1024], identb[:, :]))
                pb.on("tensor", tt)

                def ve(v, pos=pos):
                    pb.wait(v, "c_trm")
                    pb.inc("c_tre", v.tensor_copy(
                        xnT[:, :, pos * 128:(pos + 1) * 128],
                        tps[:, pos % 2, :].rearrange("p (a b) -> p a b", a=8)))
                pb.on("vector", ve)

            def ldz(g, sl=sl):
                pb.wait(g, "c_imm")
                pb.inc("c_ipld", g.dma_start(ipb[:, :, 0:2048], ipr[:, :, 0:2048]), 16)
            pb.on("gpsimd", ldz)
            for tt_ in range(4):
                for cg in range(4):
                    def mmz(t, sl=sl, tt_=tt_, cg=cg):
                        pb.wait(t, "c_ipld")
                        pb.wait(t, "c_tre")
                        pb.wait(t, "c_zev", max(0, pb.val("c_zev") - 1))
                        for dc in range(8):
                            ins = t.matmul(psR[:, (cg % 2) * 512:(cg % 2) * 512 + 512],
                                           xnT[:, dc, tt_ * 128:(tt_ + 1) * 128],
                                           ipb[:, dc, cg * 512:(cg + 1) * 512],
                                           start=(dc == 0), stop=(dc == 7))
                        pb.inc("c_imm", ins)
                    pb.on("tensor", mmz)

                    def evz(v, sl=sl, tt_=tt_, cg=cg):
                        pb.wait(v, "c_imm")
                        if cg == 0:
                            pb.wait(v, "c_zst")
                        pb.inc("c_zev", v.tensor_copy(zst[:, cg * 512:(cg + 1) * 512],
                                                      psR[:, (cg % 2) * 512:(cg % 2) * 512 + 512]))
                    pb.on("vector", evz)

                def stz(sync, sl=sl, tt_=tt_):
                    pb.wait(sync, "c_zev")
                    i = sl * 4 + tt_
                    pb.inc("c_zst", sync.dma_start(z_dram[i * 128:(i + 1) * 128, :], zst[:, :]), 16)
                pb.on("sync", stz)

            def ldxbc(g, sl=sl):
                pb.wait(g, "c_imm")
                pb.inc("c_ipld", g.dma_start(ipb[:, :, 0:2048], ipr[:, :, 2048:4096]), 16)
            pb.on("gpsimd", ldxbc)
            for fc in range(16):
                def mmx(t, sl=sl, fc=fc):
                    pb.wait(t, "c_ipld")
                    pb.wait(t, "c_xev", max(0, pb.val("c_xev") - 1))
                    for dc in range(8):
                        ins = t.matmul(psR[:, (fc % 2) * 512:(fc % 2) * 512 + 512],
                                       ipb[:, dc, fc * 128:(fc + 1) * 128],
                                       xnT[:, dc, :], start=(dc == 0), stop=(dc == 7))
                    pb.inc("c_imm", ins)
                pb.on("tensor", mmx)

                def evx(v, sl=sl, fc=fc):
                    pb.wait(v, "c_imm")
                    if fc == 0:
                        pb.wait(v, "c_xst")
                    pb.inc("c_xev", v.tensor_copy(xbcst[:, fc, :],
                                                  psR[:, (fc % 2) * 512:(fc % 2) * 512 + 512]))
                pb.on("vector", evx)

            def ldxbc2(g, sl=sl):
                pb.wait(g, "c_imm")
                pb.inc("c_ipld", g.dma_start(ipb[:, :, 0:256], ipr[:, :, 4096:4352]), 16)
            pb.on("gpsimd", ldxbc2)
            for fc in range(16, 18):
                def mmx(t, sl=sl, fc=fc):
                    pb.wait(t, "c_ipld")
                    pb.wait(t, "c_xev", max(0, pb.val("c_xev") - 1))
                    for dc in range(8):
                        ins = t.matmul(psR[:, (fc % 2) * 512:(fc % 2) * 512 + 512],
                                       ipb[:, dc, (fc - 16) * 128:(fc - 15) * 128],
                                       xnT[:, dc, :], start=(dc == 0), stop=(dc == 7))
                    pb.inc("c_imm", ins)
                pb.on("tensor", mmx)

                def evx(v, sl=sl, fc=fc):
                    pb.wait(v, "c_imm")
                    pb.inc("c_xev", v.tensor_copy(xbcst[:, fc, :],
                                                  psR[:, (fc % 2) * 512:(fc % 2) * 512 + 512]))
                pb.on("vector", evx)

            def stx(sync, sl=sl):
                pb.wait(sync, "c_xev")
                pb.inc("c_xst", sync.dma_start(
                    xbcr_dram[:, :, sl // 2, (sl % 2) * 512:(sl % 2) * 512 + 512],
                    xbcst[:, :, :]), 16)
            pb.on("sync", stx)

            def lddt(g, sl=sl):
                pb.wait(g, "c_imm")
                pb.inc("c_ipld", g.dma_start(ipb[:, :, 0:32], ipr[:, :, 4352:4384]), 16)
            pb.on("gpsimd", lddt)
            for tt_ in range(4):
                def mmd(t, sl=sl, tt_=tt_):
                    pb.wait(t, "c_ipld")
                    pb.wait(t, "c_dev", max(0, pb.val("c_dev") - 1))
                    for dc in range(8):
                        ins = t.matmul(psR[:, (tt_ % 2) * 512:(tt_ % 2) * 512 + 32],
                                       xnT[:, dc, tt_ * 128:(tt_ + 1) * 128],
                                       ipb[:, dc, 0:32], start=(dc == 0), stop=(dc == 7))
                    pb.inc("c_imm", ins)
                pb.on("tensor", mmd)

                def evd(v, sl=sl, tt_=tt_):
                    pb.wait(v, "c_imm")
                    pb.inc("c_dev", v.tensor_copy(dtr[:, sl * 4 + tt_, :],
                                                  psR[:, (tt_ % 2) * 512:(tt_ % 2) * 512 + 32]))
                pb.on("vector", evd)

    def hput(sync):
        pb.wait(sync, "c_xst")
        pb.inc("c_hst", sync.dma_start(
            hl_in.rearrange("p (k f) -> p k f", k=2)[:, 0, :].rearrange("p (fc c) -> p fc c", fc=18),
            xbcr_dram[:, :, 0, 1021:1024]), 16)
        pb.inc("c_hst", sync.dma_start(
            hl_in.rearrange("p (k f) -> p k f", k=2)[:, 1, :].rearrange("p (fc c) -> p fc c", fc=18),
            xbcr_dram[:, :, 1, 1021:1024]), 16)
    pb.on("sync", hput)

    def hcc(g):
        pb.wait(g, "c_hst")
        g.collective_compute("AllGather", ALU.bypass,
                             replica_groups=[[0, 1], [2, 3], [4, 5], [6, 7]],
                             ins=[hl_in], outs=[hl_out]).then_inc(pb.semh("c_hcc")[0], 1)
        pb.semh("c_hcc")[1] += 1
    pb.on("gpsimd", hcc)

    with contextlib.ExitStack() as e2:
        heven = _sb(nc, e2, "c_heven", [128, 2, 18, 3], BF16)
        hodd = _sb(nc, e2, "c_hodd", [128, 2, 18, 3], BF16)

        def hget(sync):
            pb.wait(sync, "c_hcc")
            pb.inc("c_hld", sync.dma_start(
                heven[:, :, :, :].rearrange("p k fc c -> p (k fc c)"), hl_out[0:128, :]), 16)
            pb.inc("c_hld", sync.dma_start(
                hodd[:, :, :, :].rearrange("p k fc c -> p (k fc c)"), hl_out[128:256, :]), 16)
        pb.on("sync", hget)

        def hblend(v):
            pb.wait(v, "c_hld")
            pb.wait(v, "c_wld")
            v.tensor_scalar(haloA[:, :, :], heven[:, 0], cselA[:, :], None, op0=ALU.mult)
            v.drain()
            v.tensor_scalar(haloB[:, :, :], hodd[:, 1], cselE[:, :], None, op0=ALU.mult)
            v.drain()
            pb.inc("c_halo", v.scalar_tensor_tensor(
                haloB[:, :, :].rearrange("p a b -> p (a b)"),
                hodd[:, 0].rearrange("p a b -> p (a b)"),
                cselO[:, :], haloB[:, :, :].rearrange("p a b -> p (a b)"),
                op0=ALU.mult, op1=ALU.add))
        pb.on("vector", hblend)

    with contextlib.ExitStack() as e3:
        xconv = _sb(nc, e3, "c_xconv", [128, 18, 515], BF16)
        xbcp = _sb(nc, e3, "c_xbcp", [128, 18, 512], BF16)
        convw = _sb(nc, e3, "c_convw_s", [128, 18, 4], F32)
        convb = _sb(nc, e3, "c_convb_s", [128, 18], F32)
        acc1 = _sb(nc, e3, "c_acc1", [128, 512], F32)
        acc2 = _sb(nc, e3, "c_acc2", [128, 512], F32)
        dtb = _sb(nc, e3, "c_dtb_s", [128, 32], F32)
        Albc = _sb(nc, e3, "c_Albc", [128, 32], F32)
        Abc = _sb(nc, e3, "c_Abc", [128, 32], F32)
        Dbc = _sb(nc, e3, "c_Dbc", [128, 32], F32)
        dt_tm = _sb(nc, e3, "c_dttm", [128, 32], F32)
        sp1 = _sb(nc, e3, "c_sp1", [128, 32], F32)
        sp2 = _sb(nc, e3, "c_sp2", [128, 32], F32)
        a_tm = _sb(nc, e3, "c_atm", [128, 32], F32)
        acum_fm = _sb(nc, e3, "c_acumfm", [32, 128], F32)
        acum_flat = _sb(nc, e3, "c_acumflat", [1, 4096], F32)
        ea_tm = _sb(nc, e3, "c_eatm", [128, 32], F32)
        dd_tm = _sb(nc, e3, "c_ddtm", [128, 32], F32)
        decay_tm = _sb(nc, e3, "c_decaytm", [128, 32], F32)
        dxr = _sb(nc, e3, "c_dxr", [1, 32], F32)
        dbc = _sb(nc, e3, "c_dbc", [128, 32], F32)
        cdxr = _sb(nc, e3, "c_cdxr", [1, 32], F32)
        cdbc = _sb(nc, e3, "c_cdbc", [128, 32], F32)
        S = _sb(nc, e3, "c_S", [128, 2048], F32)
        Gm = _sb(nc, e3, "c_Gm", [128, 128], BF16)
        Btm = _sb(nc, e3, "c_Btm", [128, 128], BF16)
        xs_tm = _sb(nc, e3, "c_xstm", [128, 2048], BF16)
        X1 = _sb(nc, e3, "c_X1", [128, 2048], BF16)
        X2 = _sb(nc, e3, "c_X2", [128, 2048], BF16)
        D1 = _sb(nc, e3, "c_D1", [128, 1024], F32)
        Ebuf = _sb(nc, e3, "c_Ebuf", [128, 1024], BF16)
        Mq = _sb(nc, e3, "c_Mq", [128, 1024], BF16)
        ytile = _sb(nc, e3, "c_ytile", [128, 2048], F32)
        C32f = _sb(nc, e3, "c_C32f", [128, 128], F32)
        ytmp = _sb(nc, e3, "c_ytmp", [128, 512], F32)
        ytmp2 = _sb(nc, e3, "c_ytmp2", [128, 512], F32)

        def ldc(g):
            pb.inc("c_s0", g.memset(S[:, :], 0.0))
            pb.inc("c_wld", g.dma_start(convw[:, :, :], io["c_convw"][:, :, :]), 16)
            pb.inc("c_wld", g.dma_start(convb[:, :], io["c_convb"][:, :]), 16)
            pb.inc("c_wld", g.dma_start(dtb[:, :], io["c_dtb"][:, :]), 16)
            pb.inc("c_wld", g.dma_start(Albc[:, :], io["c_Alog"][:, :]), 16)
            pb.inc("c_wld", g.dma_start(Dbc[:, :], io["c_D"][:, :]), 16)
        pb.on("gpsimd", ldc)

        def mkA(sx):
            pb.wait(sx, "c_wld")
            pb.inc("c_ac0", sx.activation(Abc[:, :], Albc[:, :], AF.Exp, scale=1.0))
        pb.on("scalar", mkA)

        def mkA2(v):
            pb.wait(v, "c_ac0")
            pb.inc("c_ac1", v.tensor_scalar(Abc[:, :], Abc[:, :], -1.0, None, op0=ALU.mult))
        pb.on("vector", mkA2)

        for sl in range(4):
            buf = sl % 2
            runstart = (sl % 2 == 0)
            run = sl // 2
            def ldcv(sync, sl=sl, run=run, runstart=runstart):
                pb.wait(sync, "c_xst")
                pb.wait(sync, "c_cv1")
                if runstart:
                    pb.inc("c_cvld", sync.dma_start(xconv[:, :, 3:515],
                                                    xbcr_dram[:, :, run, 0:512]), 16)
                else:
                    pb.inc("c_cvld", sync.dma_start(xconv[:, :, 0:515],
                                                    xbcr_dram[:, :, run, 509:1024]), 16)
            pb.on("sync", ldcv)

            if runstart:
                def cph(v, sl=sl, run=run):
                    pb.wait(v, "c_cvld")
                    pb.wait(v, "c_halo")
                    src = haloA if run == 0 else haloB
                    pb.inc("c_cvh", v.tensor_copy(xconv[:, :, 0:3], src[:, :, :]))
                pb.on("vector", cph)

            for fc in range(18):
                def cvx(v, sl=sl, fc=fc, buf=buf):
                    pb.wait(v, "c_cvld")
                    pb.wait(v, "c_cvh")
                    pb.wait(v, "c_wld")
                    pb.wait(v, "c_xsm")
                    pb.wait(v, "c_gm")
                    pb.wait(v, "c_btm")
                    pb.wait(v, "c_c32")
                    pb.wait(v, "c_cv2")
                    v.tensor_scalar(acc1[:, :], xconv[:, fc, 0:512], convw[:, fc, 0:1],
                                    None, op0=ALU.mult)
                    v.drain()
                    v.scalar_tensor_tensor(acc2[:, :], xconv[:, fc, 1:513], convw[:, fc, 1:2],
                                           acc1[:, :], op0=ALU.mult, op1=ALU.add)
                    v.drain()
                    v.scalar_tensor_tensor(acc1[:, :], xconv[:, fc, 2:514], convw[:, fc, 2:3],
                                           acc2[:, :], op0=ALU.mult, op1=ALU.add)
                    v.drain()
                    pb.inc("c_cv1", v.scalar_tensor_tensor(
                        acc2[:, :], xconv[:, fc, 3:515], convw[:, fc, 3:4],
                        acc1[:, :], op0=ALU.mult, op1=ALU.add))
                pb.on("vector", cvx)

                def cvs(sx, sl=sl, fc=fc, buf=buf):
                    pb.wait(sx, "c_cv1")
                    ins = sx.activation(xbcp[:, fc, :], acc2[:, :], AF.Silu,
                                        bias=convb[:, fc:fc + 1])
                    pb.inc("c_cv2", ins)
                    if fc == 17:
                        pb.inc("c_cvdone", sx.activation(
                            xbcp[:, 17, 0:1], acc2[:, 0:1], AF.Silu,
                            bias=convb[:, 17:18]))
                pb.on("scalar", cvs)

            for s4 in range(4):
                s = sl * 4 + s4
                ccol = s4 * 128
                def dta(v, s=s):
                    pb.wait(v, "c_cv2")
                    pb.wait(v, "c_ac1")
                    if s == 1 and io.get("_dbg0"):
                        pb.wait(v, "c_d0")
                    pb.inc("c_dt0", v.tensor_tensor(dt_tm[:, :], dtr[:, s, :], dtb[:, :], op=ALU.add))
                pb.on("vector", dta)

                def dtb_(sx, s=s):
                    pb.wait(sx, "c_dt0")
                    sx.activation(sp1[:, :], dt_tm[:, :], AF.Abs)
                    sx.drain()
                    sx.activation(sp1[:, :], sp1[:, :], AF.Exp, scale=-1.0)
                    sx.drain()
                    sx.activation(sp1[:, :], sp1[:, :], AF.Ln, bias=sc["one"])
                    sx.drain()
                    pb.inc("c_dt1", sx.activation(sp2[:, :], dt_tm[:, :], AF.Relu))
                pb.on("scalar", dtb_)

                def dtc(v, s=s):
                    pb.wait(v, "c_dt1")
                    v.tensor_tensor(dt_tm[:, :], sp1[:, :], sp2[:, :], op=ALU.add)
                    v.drain()
                    pb.inc("c_dt2", v.tensor_tensor(a_tm[:, :], dt_tm[:, :], Abc[:, :], op=ALU.mult))
                pb.on("vector", dtc)

                def mmac(t, s=s):
                    pb.wait(t, "c_dt2")
                    pb.wait(t, "c_wld")
                    pb.wait(t, "c_D2")
                    pb.wait(t, "c_acev", max(0, pb.val("c_acev")))
                    pb.inc("c_acm", t.matmul(psR[:, 0:32], io["utri_incl_sb"][:, :],
                                             a_tm[:, :], start=True, stop=True))
                pb.on("tensor", mmac)

                def evac_(v, s=s):
                    pb.wait(v, "c_acm")
                    pb.inc("c_acev", v.tensor_copy(acum_all[:, s, :], psR[:, 0:32]))
                pb.on("vector", evac_)

                def mmtr(t, s=s):
                    pb.wait(t, "c_acev")
                    pb.wait(t, "c_ypsfree", max(0, pb.val("c_ypsfree")))
                    pb.wait(t, "c_fmev", max(0, pb.val("c_fmev")))
                    pb.inc("c_trf", t.transpose(psYo[0:32, 0:128], acum_all[:, s, :], identf[:, :]))
                pb.on("tensor", mmtr)

                def evfm(v, s=s):
                    pb.wait(v, "c_trf")
                    pb.inc("c_fmev", v.tensor_copy(acum_fm[:, :], psYo[0:32, 0:128]))
                pb.on("vector", evfm)

                def flat(sync, s=s):
                    pb.wait(sync, "c_fmev")
                    pb.inc("c_flat", sync.dma_start(acum_flat[:, :], acum_fm[:, :]), 16)
                pb.on("sync", flat)

                def rsm(v, s=s):
                    pb.wait(v, "c_flat")
                    if s == 0:
                        v.memset(rsum[:, :], 0.0)
                        v.drain()
                    if s == 8:
                        v.tensor_scalar(rsum[:, :], rsum[:, :], crho2[:, :], None, op0=ALU.mult)
                        v.drain()
                    v.tensor_copy(cdrow_all[:, s, :], rsum[:, :])
                    v.drain()
                    arow = bass.AP(acum_flat, 127, [[4096, 1], [128, 32]])
                    pb.inc("c_rsm", v.tensor_tensor(rsum[:, :], rsum[:, :], arow, op=ALU.add))
                pb.on("vector", rsm)

                def eaa(sx, s=s):
                    pb.wait(sx, "c_acev")
                    ins = sx.activation(ea_tm[:, :], acum_all[:, s, :], AF.Exp)
                    pb.inc("c_ea", ins)
                pb.on("scalar", eaa)

                def mmlast(t, s=s):
                    pb.wait(t, "c_flat")
                    pb.wait(t, "c_cst")
                    pb.wait(t, "c_dcev", max(0, pb.val("c_dcev")))
                    arow = bass.AP(acum_flat, 127, [[4096, 1], [128, 32]])
                    pb.inc("c_lbm", t.matmul(psR[:, 512:544], ones_r[:, :], arow,
                                             start=True, stop=True))
                pb.on("tensor", mmlast)

                def dca(v, s=s):
                    pb.wait(v, "c_lbm")
                    pb.inc("c_dc0", v.tensor_tensor(dd_tm[:, :], psR[:, 512:544],
                                                    acum_all[:, s, :], op=ALU.subtract))
                pb.on("vector", dca)

                def dcb(sx, s=s):
                    pb.wait(sx, "c_dc0")
                    arow = bass.AP(acum_flat, 127, [[4096, 1], [128, 32]])
                    sx.activation(decay_tm[:, :], dd_tm[:, :], AF.Exp)
                    sx.drain()
                    pb.inc("c_dc1", sx.activation(dxr[:, :], arow, AF.Exp))
                pb.on("scalar", dcb)

                def mmdbc(t, s=s):
                    pb.wait(t, "c_dc1")
                    pb.inc("c_dbm", t.matmul(psR[:, 544:576], ones_r[:, :], dxr[:, :],
                                             start=True, stop=True))
                pb.on("tensor", mmdbc)

                def dcc(v, s=s):
                    pb.wait(v, "c_dbm")
                    pb.inc("c_dcev", v.tensor_copy(dbc[:, :], psR[:, 544:576]))
                pb.on("vector", dcc)

                for g2 in range(2):
                    def ttx(t, s=s, buf=buf, ccol=ccol, g2=g2):
                        pb.wait(t, "c_cv2")
                        if s == 1 and io.get("_dbg0"):
                            pb.wait(t, "c_d0")
                        pb.wait(t, "c_btev")
                        pb.wait(t, "c_xsev", max(0, pb.val("c_xsev") - 1))
                        for ic in range(8):
                            ins = t.transpose(tps[:, g2, ic * 128:(ic + 1) * 128],
                                              xbcp[:, g2 * 8 + ic, ccol:ccol + 128],
                                              identb[:, :])
                        pb.inc("c_xsm", ins)
                    pb.on("tensor", ttx)

                    def evxs(v, s=s, g2=g2):
                        pb.wait(v, "c_xsm")
                        pb.inc("c_xsev", v.tensor_copy(
                            xs_tm[:, g2 * 1024:(g2 + 1) * 1024], tps[:, g2, :]))
                    pb.on("vector", evxs)

                def mkX(v, s=s):
                    pb.wait(v, "c_xsev")
                    pb.wait(v, "c_dt2")
                    pb.wait(v, "c_dcev")
                    dtap = bass.AP(dt_tm, 0, [[32, 128], [1, 32], [0, 64]])
                    v.tensor_tensor(X1[:, :].rearrange("p (h q) -> p h q", h=32), xs_tm[:, :].rearrange("p (h q) -> p h q", h=32), dtap, op=ALU.mult)
                    v.drain()
                    dcap = bass.AP(decay_tm, 0, [[32, 128], [1, 32], [0, 64]])
                    pb.inc("c_X", v.tensor_tensor(X2[:, :].rearrange("p (h q) -> p h q", h=32), X1[:, :].rearrange("p (h q) -> p h q", h=32), dcap, op=ALU.mult))
                pb.on("vector", mkX)

                def evc32(v, s=s, buf=buf, ccol=ccol):
                    pb.wait(v, "c_cv2")
                    v.tensor_copy(Call[:, s, :], xbcp[:, 17, ccol:ccol + 128])
                    v.drain()
                    pb.inc("c_c32", v.tensor_copy(C32f[:, :], xbcp[:, 17, ccol:ccol + 128]))
                pb.on("vector", evc32)

                def mmg(t, s=s, buf=buf, ccol=ccol):
                    pb.wait(t, "c_cv2")
                    pb.wait(t, "c_gev", max(0, pb.val("c_gev")))
                    pb.inc("c_gm", t.matmul(psR[:, 0:128], xbcp[:, 16, ccol:ccol + 128],
                                            xbcp[:, 17, ccol:ccol + 128],
                                            start=True, stop=True))
                pb.on("tensor", mmg)

                def evg(v, s=s):
                    pb.wait(v, "c_gm")
                    pb.inc("c_gev", v.tensor_tensor(Gm[:, :], psR[:, 0:128],
                                                    io["triu01_sb"][:, :], op=ALU.mult))
                pb.on("vector", evg)

                def mmbt(t, s=s, buf=buf, ccol=ccol):
                    pb.wait(t, "c_cv2")
                    pb.wait(t, "c_btev", max(0, pb.val("c_btev")))
                    pb.inc("c_btm", t.transpose(tps[:, 0, 0:128],
                                                xbcp[:, 16, ccol:ccol + 128], identb[:, :]))
                pb.on("tensor", mmbt)

                def evbt(v, s=s):
                    pb.wait(v, "c_btm")
                    pb.inc("c_btev", v.tensor_copy(Btm[:, :], tps[:, 0, 0:128]))
                pb.on("vector", evbt)

                if s == 8:
                    def rho8(v):
                        pb.wait(v, "c_supd")
                        pb.wait(v, "c_snap")
                        pb.inc("c_supd", v.tensor_scalar(S[:, :], S[:, :], crho[:, :],
                                                         None, op0=ALU.mult))
                    pb.on("vector", rho8)

                for q in range(4):
                    def mmrow(t, s=s, q=q):
                        pb.wait(t, "c_flat")
                        pb.wait(t, "c_gev")
                        pb.wait(t, "c_dcev")
                        pb.wait(t, "c_D2")
                        for j in range(2):
                            ins = t.matmul(psR[:, j * 512:(j + 1) * 512], ones_r[:, :],
                                           acum_flat[:, q * 1024 + j * 512:q * 1024 + (j + 1) * 512],
                                           start=True, stop=True)
                        pb.inc("c_row", ins)
                    pb.on("tensor", mmrow)

                    def mkE(v, s=s, q=q):
                        pb.wait(v, "c_row")
                        acap = bass.AP(acum_all, s * 32 + q * 8, [[NT * 32, 128], [1, 8], [0, 128]])
                        v.tensor_tensor(D1[:, :].rearrange("p (h l) -> p h l", h=8),
                                        psR[:, 0:1024].rearrange("p (h l) -> p h l", h=8),
                                        acap, op=ALU.subtract)
                        v.drain()
                        pb.inc("c_D2", v.tensor_scalar(D1[:, :], D1[:, :], 0.0, None, op0=ALU.min))
                    pb.on("vector", mkE)

                    def expE(sx, s=s, q=q):
                        pb.wait(sx, "c_D2")
                        pb.inc("c_E", sx.activation(Ebuf[:, :], D1[:, :], AF.Exp))
                    pb.on("scalar", expE)

                    def mkM(v, s=s, q=q):
                        pb.wait(v, "c_E")
                        pb.wait(v, "c_gev")
                        gap = bass.AP(Gm, 0, [[128, 128], [0, 8], [1, 128]])
                        pb.inc("c_M", v.tensor_tensor(
                            Mq[:, :].rearrange("p (h l) -> p h l", h=8),
                            Ebuf[:, :].rearrange("p (h l) -> p h l", h=8), gap, op=ALU.mult))
                    pb.on("vector", mkM)

                    def mmyd(t, s=s, q=q):
                        pb.wait(t, "c_M")
                        pb.wait(t, "c_X")
                        pb.wait(t, "c_ypsfree", max(0, pb.val("c_ypsfree")))
                        for hh in range(8):
                            ins = t.matmul(psY[:, hh * 64:(hh + 1) * 64],
                                           Mq[:, hh * 128:(hh + 1) * 128],
                                           X1[:, (q * 8 + hh) * 64:(q * 8 + hh + 1) * 64],
                                           start=(hh == 0), stop=(hh == 7))
                        pb.inc("c_ydm", ins)
                    pb.on("tensor", mmyd)

                    def mmyo(t, s=s, q=q):
                        pb.wait(t, "c_c32")
                        pb.wait(t, "c_s0")
                        pb.wait(t, "c_supd")
                        pb.wait(t, "c_ypsfree", max(0, pb.val("c_ypsfree")))
                        pb.inc("c_yom", t.matmul(psYo[:, :], C32f[:, :],
                                                 S[:, q * 512:(q + 1) * 512],
                                                 start=True, stop=True))
                    pb.on("tensor", mmyo)

                    def mmu(t, s=s, q=q):
                        pb.wait(t, "c_btev")
                        pb.wait(t, "c_X")
                        pb.wait(t, "c_supd", max(0, pb.val("c_supd")))
                        pb.inc("c_um", t.matmul(psU[:, :], Btm[:, :],
                                                X2[:, q * 512:(q + 1) * 512],
                                                start=True, stop=True))
                    pb.on("tensor", mmu)

                    def yasm(v, s=s, q=q):
                        pb.wait(v, "c_ydm")
                        pb.wait(v, "c_yom")
                        pb.wait(v, "c_ea")
                        eap = bass.AP(ea_tm, q * 8, [[32, 128], [1, 8], [0, 64]])
                        v.tensor_tensor(ytmp[:, :].rearrange("p (h w) -> p h w", h=8),
                                        psYo[:, :].rearrange("p (h w) -> p h w", h=8),
                                        eap, op=ALU.mult)
                        v.drain()
                        pb.inc("c_ypsfree", v.tensor_tensor(ytmp[:, :], ytmp[:, :], psY[:, :], op=ALU.add))
                        v.drain()
                        dap = bass.AP(Dbc, q * 8, [[32, 128], [1, 8], [0, 64]])
                        v.tensor_tensor(ytmp2[:, :].rearrange("p (h w) -> p h w", h=8),
                                        xs_tm[:, q * 512:(q + 1) * 512].rearrange("p (h w) -> p h w", h=8),
                                        dap, op=ALU.mult)
                        v.drain()
                        pb.inc("c_yas", v.tensor_tensor(ytile[:, q * 512:(q + 1) * 512],
                                                        ytmp[:, :], ytmp2[:, :], op=ALU.add))
                    pb.on("vector", yasm)

                    def supd(v, s=s, q=q):
                        pb.wait(v, "c_um")
                        pb.wait(v, "c_yom")
                        v.tensor_tensor(S[:, q * 512:(q + 1) * 512].rearrange("p (h w) -> p h w", h=8),
                                        S[:, q * 512:(q + 1) * 512].rearrange("p (h w) -> p h w", h=8),
                                        bass.AP(dbc, q * 8, [[32, 128], [1, 8], [0, 64]]), op=ALU.mult)
                        v.drain()
                        pb.inc("c_supd", v.tensor_tensor(
                            S[:, q * 512:(q + 1) * 512], S[:, q * 512:(q + 1) * 512],
                            psU[:, :], op=ALU.add))
                    pb.on("vector", supd)

                def sty(sync, s=s):
                    pb.wait(sync, "c_yas")
                    pb.inc("c_yst", sync.dma_start(y_dram[s * 128:(s + 1) * 128, :], ytile[:, :]), 16)
                pb.on("sync", sty)

                if s == 0 and io.get("_dbg0"):
                    def d0(sync):
                        pb.wait(sync, "c_yas")
                        pb.inc("c_d0", sync.dma_start(io["_dbg0"]["x1"], X1[:, :]), 16)
                        pb.inc("c_d0", sync.dma_start(io["_dbg0"]["xstm"], xs_tm[:, :]), 16)
                        pb.inc("c_d0", sync.dma_start(io["_dbg0"]["gm"], Gm[:, :]), 16)
                        pb.inc("c_d0", sync.dma_start(io["_dbg0"]["acum"], acum_all[:, 0, :]), 16)
                        pb.inc("c_d0", sync.dma_start(io["_dbg0"]["dt"], dt_tm[:, :]), 16)
                        pb.inc("c_d0", sync.dma_start(io["_dbg0"]["xbcp"], xbcp[:, :, :]), 16)
                    pb.on("sync", d0)

                if s == 7 or s == 15:
                    def snap(sync, s=s):
                        pb.wait(sync, "c_supd")
                        row = 0 if s == 7 else 128
                        pb.inc("c_snap", sync.dma_start(st_in[row:row + 128, :], S[:, :]), 16)
                    pb.on("sync", snap)

        def sprep(v):
            pb.wait(v, "c_rsm")
            v.drain()
            v.memset(acum_flat[0:1, 0:2048], 0.0)
            v.drain()
            pb.inc("c_sprep", v.tensor_copy(acum_flat[0:1, 0:32], rsum[:, :]))
        pb.on("vector", sprep)

        def sput(sync):
            pb.wait(sync, "c_snap")
            pb.wait(sync, "c_sprep")
            pb.inc("c_sst", sync.dma_start(st_in[256:257, :], acum_flat[0:1, 0:2048]), 16)
        pb.on("sync", sput)

        def scc(g):
            pb.wait(g, "c_sst")
            g.collective_compute("AllGather", ALU.bypass,
                                 replica_groups=[[0, 1], [2, 3], [4, 5], [6, 7]],
                                 ins=[st_in], outs=[st_out]).then_inc(pb.semh("c_scc")[0], 1)
            pb.semh("c_scc")[1] += 1
        pb.on("gpsimd", scc)

        def sget(sync):
            pb.wait(sync, "c_scc")
            pb.wait(sync, "c_yst")
            pb.inc("c_sld", sync.dma_start(SiniA[:, :], st_out[0:128, :]), 16)
            pb.inc("c_sld", sync.dma_start(SiniB[:, :], st_out[385:513, :]), 16)
            pb.inc("c_sld", sync.dma_start(ytile[:, :], st_out[0:128, :]), 16)
            pb.inc("c_sld", sync.dma_start(rrow[:, :], st_out[513:514, 0:32]), 16)
        pb.on("sync", sget)

        def sblend1(sx):
            pb.wait(sx, "c_sld")
            pb.inc("c_sb0", sx.activation(dxr[:, :], rrow[:, :], AF.Exp))
        pb.on("scalar", sblend1)

        def sblend2(v):
            pb.wait(v, "c_sb0")
            pb.wait(v, "c_wld")
            pb.inc("c_sb1", v.tensor_scalar(dxr[:, :], dxr[:, :], cdsel[:, :], cdseli[:, :],
                                            op0=ALU.mult, op1=ALU.add))
        pb.on("vector", sblend2)

        def mmds(t):
            pb.wait(t, "c_sb1")
            pb.inc("c_sb2", t.matmul(psR[:, 544:576], ones_r[:, :], dxr[:, :],
                                     start=True, stop=True))
        pb.on("tensor", mmds)

        def sblend3(v):
            pb.wait(v, "c_sb2")
            v.tensor_copy(dbc[:, :], psR[:, 544:576])
            v.drain()
            v.tensor_scalar(SiniA[:, :], SiniA[:, :], cselA[:, :], None, op0=ALU.mult)
            v.drain()
            v.tensor_scalar(SiniB[:, :], SiniB[:, :], cw1[:, :], None, op0=ALU.mult)
            v.drain()
            v.tensor_tensor(ytile[:, :].rearrange("p (h w) -> p h w", h=32),
                            ytile[:, :].rearrange("p (h w) -> p h w", h=32),
                            bass.AP(dbc, 0, [[32, 128], [1, 32], [0, 64]]), op=ALU.mult)
            v.drain()
            pb.inc("c_sini", v.tensor_tensor(SiniB[:, :], SiniB[:, :], ytile[:, :], op=ALU.add))
        pb.on("vector", sblend3)

    with contextlib.ExitStack() as e4:
        opw = _sb(nc, e4, "c_opw", [128, 16, 1024], BF16)
        rmsw = _sb(nc, e4, "c_rmsw_s", [128, 2048], F32)
        ysl = _sb(nc, e4, "c_ysl", [128, 2048], F32)
        zsl = _sb(nc, e4, "c_zsl", [128, 2048], BF16)
        zss = _sb(nc, e4, "c_zss", [128, 2048], BF16)
        y3 = _sb(nc, e4, "c_y3", [128, 2048], BF16)
        y3T = _sb(nc, e4, "c_y3T", [128, 16, 128], BF16)
        Scor = _sb(nc, e4, "c_Scor", [128, 2048], F32)
        cd_bc = _sb(nc, e4, "c_cdbc2", [128, 32], F32)
        ea2 = _sb(nc, e4, "c_ea2", [128, 32], F32)
        ycq = _sb(nc, e4, "c_ycq", [128, 512], F32)
        C32g = _sb(nc, e4, "c_C32g", [128, 128], F32)
        rstd = _sb(nc, e4, "c_rstd", [128, 1], F32)
        rt4 = _sb(nc, e4, "c_rt4", [128, 1024], F32)

        def ldop(g):
            pb.wait(g, "c_sini")
            pb.inc("c_wld2", g.dma_start(opw[:, :, :], opr[:, :, :]), 16)
            pb.inc("c_wld2", g.dma_start(rmsw[:, :], io["c_rmsw"][:, :]), 16)
        pb.on("gpsimd", ldop)

        for s in range(NT):
            Sini = SiniA if s < 8 else SiniB

            def ldy(sync, s=s):
                pb.wait(sync, "c_yst")
                pb.wait(sync, "c_y3", pb.val("c_y3"))
                pb.inc("c_yld", sync.dma_start(ysl[:, :], y_dram[s * 128:(s + 1) * 128, :]), 16)
                pb.inc("c_yld", sync.dma_start(zsl[:, :], z_dram[s * 128:(s + 1) * 128, :]), 16)
            pb.on("sync", ldy)

            def cexp(sx, s=s):
                pb.wait(sx, "c_sini")
                pb.wait(sx, "c_cd1", max(0, pb.val("c_cd1")))
                sx.activation(ea2[:, :], acum_all[:, s, :], AF.Exp)
                sx.drain()
                pb.inc("c_cd0", sx.activation(rrow[:, :], cdrow_all[:, s, :], AF.Exp))
            pb.on("scalar", cexp)

            def mmcd(t, s=s):
                pb.wait(t, "c_cd0")
                pb.wait(t, "c_opev")
                pb.inc("c_cdm", t.matmul(psR[:, 544:576], ones_r[:, :], rrow[:, :],
                                         start=True, stop=True))
            pb.on("tensor", mmcd)

            def mkscor(v, s=s, Sini=Sini):
                pb.wait(v, "c_cdm")
                pb.wait(v, "c_ycm", pb.val("c_ycm"))
                v.tensor_copy(cd_bc[:, :], psR[:, 544:576])
                v.drain()
                v.tensor_copy(C32g[:, :], Call[:, s, :])
                v.drain()
                pb.inc("c_cd1", v.tensor_tensor(
                    Scor[:, :].rearrange("p (h w) -> p h w", h=32),
                    Sini[:, :].rearrange("p (h w) -> p h w", h=32),
                    bass.AP(cd_bc, 0, [[32, 128], [1, 32], [0, 64]]), op=ALU.mult))
            pb.on("vector", mkscor)

            for q in range(4):
                def mmyc(t, s=s, q=q):
                    pb.wait(t, "c_cd1")
                    pb.wait(t, "c_ycev", max(0, pb.val("c_ycev")))
                    pb.inc("c_ycm", t.matmul(psYo[:, :], C32g[:, :],
                                             Scor[:, q * 512:(q + 1) * 512],
                                             start=True, stop=True))
                pb.on("tensor", mmyc)

                def evyc(v, s=s, q=q):
                    pb.wait(v, "c_ycm")
                    pb.wait(v, "c_yld")
                    eap = bass.AP(ea2, q * 8, [[32, 128], [1, 8], [0, 64]])
                    pb.inc("c_ycev", v.tensor_tensor(
                        ycq[:, :].rearrange("p (h w) -> p h w", h=8),
                        psYo[:, :].rearrange("p (h w) -> p h w", h=8),
                        eap, op=ALU.mult))
                    v.drain()
                    pb.inc("c_yc2", v.tensor_tensor(
                        ysl[:, q * 512:(q + 1) * 512],
                        ysl[:, q * 512:(q + 1) * 512], ycq[:, :], op=ALU.add))
                pb.on("vector", evyc)

            def zsi(sx, s=s):
                pb.wait(sx, "c_yc2", pb.val("c_yc2"))
                pb.inc("c_zs", sx.activation(zss[:, :], zsl[:, :], AF.Silu))
            pb.on("scalar", zsi)

            def ymul(v, s=s):
                pb.wait(v, "c_zs")
                pb.inc("c_y2", v.tensor_tensor(ysl[:, :], ysl[:, :], zss[:, :], op=ALU.mult))
            pb.on("vector", ymul)

            def rms1(sx, s=s):
                pb.wait(sx, "c_y2")
                pb.inc("c_rms0", sx.activation(sc["junk"][:, 0:2048], ysl[:, :], AF.Square,
                                               accum_out=rstd[:, :]))
            pb.on("scalar", rms1)

            def rms2(v, s=s):
                pb.wait(v, "c_rms0")
                pb.inc("c_rms1", v.tensor_scalar(rstd[:, :], rstd[:, :], 1.0 / 2048, float(EPS),
                                                 op0=ALU.mult, op1=ALU.add))
            pb.on("vector", rms2)

            def rms3(sx, s=s):
                pb.wait(sx, "c_rms1")
                pb.inc("c_rms2", sx.activation(rstd[:, :], rstd[:, :], AF.Sqrt))
            pb.on("scalar", rms3)

            def rms4(v, s=s):
                pb.wait(v, "c_rms2")
                pb.wait(v, "c_wld2")
                v.reciprocal(sc["c1"][:, :], rstd[:, :])
                v.drain()
                v.tensor_scalar(ysl[:, :], ysl[:, :], sc["c1"][:, :], None, op0=ALU.mult)
                v.drain()
                pb.inc("c_y3", v.tensor_tensor(y3[:, :], ysl[:, :], rmsw[:, :], op=ALU.mult))
            pb.on("vector", rms4)

            for g2 in range(2):
                def tty(t, s=s, g2=g2):
                    pb.wait(t, "c_y3")
                    pb.wait(t, "c_ytev", max(0, pb.val("c_ytev") - 1))
                    for ic in range(8):
                        ins = t.transpose(tps[:, g2, ic * 128:(ic + 1) * 128],
                                          y3[:, (g2 * 8 + ic) * 128:(g2 * 8 + ic + 1) * 128],
                                          identb[:, :])
                    pb.inc("c_ytm", ins)
                pb.on("tensor", tty)

                def evyt(v, s=s, g2=g2):
                    pb.wait(v, "c_ytm")
                    pb.inc("c_ytev", v.tensor_copy(
                        y3T[:, g2 * 8:(g2 + 1) * 8, :],
                        tps[:, g2, :].rearrange("p (a b) -> p a b", a=8)))
                pb.on("vector", evyt)

            for hf in range(2):
                def mmop(t, s=s, hf=hf):
                    pb.wait(t, "c_ytev")
                    pb.wait(t, "c_wld2")
                    pb.wait(t, "c_opev", max(0, pb.val("c_opev") - 1))
                    for ic in range(16):
                        ins = t.matmul(psR[:, (hf % 2) * 512:(hf % 2) * 512 + 512],
                                       y3T[:, ic, :], opw[:, ic, hf * 512:(hf + 1) * 512],
                                       start=(ic == 0), stop=(ic == 15))
                    pb.inc("c_opm", ins)
                pb.on("tensor", mmop)

                def evop(v, s=s, hf=hf):
                    pb.wait(v, "c_opm")
                    pb.wait(v, "ac")
                    pb.inc("c_opev", v.scalar_tensor_tensor(
                        rt4[:, hf * 512:(hf + 1) * 512], stream[:, s, hf * 512:(hf + 1) * 512],
                        2.0, psR[:, (hf % 2) * 512:(hf % 2) * 512 + 512],
                        op0=ALU.mult, op1=ALU.add))
                pb.on("vector", evop)

            emit_ln(pb, sc, rt4[:, :], stream[:, s, :], ready=("c_opev", pb.val("c_opev")))


def stage_mamba(inp):
    d = {}
    d["c_ipT"] = _t(inp["L1_attn_in_proj_w"])
    d["c_opT"] = _t(inp["L1_attn_out_proj_w"])
    d["c_normw"] = np.ascontiguousarray(np.broadcast_to(np.asarray(inp["L1_attn_norm_w"], np.float32), (128, 1024)))
    d["c_normb"] = np.ascontiguousarray(np.broadcast_to(np.asarray(inp["L1_attn_norm_b"], np.float32), (128, 1024)))
    d["c_convw"] = np.ascontiguousarray(np.asarray(inp["L1_attn_conv_w"], np.float32).reshape(18, 128, 4).transpose(1, 0, 2))
    d["c_convb"] = np.ascontiguousarray(np.asarray(inp["L1_attn_conv_b"], np.float32).reshape(18, 128).T)
    d["c_dtb"] = np.ascontiguousarray(np.broadcast_to(np.asarray(inp["L1_attn_dt_bias"], np.float32), (128, 32)))
    d["c_Alog"] = np.ascontiguousarray(np.broadcast_to(np.asarray(inp["L1_attn_A_log"], np.float32), (128, 32)))
    d["c_D"] = np.ascontiguousarray(np.broadcast_to(np.asarray(inp["L1_attn_D"], np.float32), (128, 32)))
    d["c_rmsw"] = np.ascontiguousarray(np.broadcast_to(np.asarray(inp["L1_attn_rms_w"], np.float32), (128, 2048)))
    d["utri_incl"] = np.triu(np.ones((128, 128), np.float32), 0)
    d["triu01"] = np.triu(np.ones((128, 128), np.float32), 0)
    d["ident"] = np.eye(128, dtype=np.float32)
    return d


def stage_mamba_core(core):
    half = core % 2
    e = (half == 0)
    d = {}
    d["c_rho"] = np.full((128, 1), 0.0 if e else 1.0, np.float32)
    d["c_rho2"] = np.full((1, 1), 0.0 if e else 1.0, np.float32)
    d["c_w1"] = np.full((128, 1), 1.0 if e else 0.0, np.float32)
    d["c_dsel"] = np.full((1, 1), 1.0 if e else 0.0, np.float32)
    d["c_dseli"] = np.full((1, 1), 0.0 if e else 1.0, np.float32)
    d["c_selA"] = np.full((128, 1), 0.0 if e else 1.0, np.float32)
    d["c_selE"] = np.full((128, 1), 1.0 if e else 0.0, np.float32)
    d["c_selO"] = np.full((128, 1), 0.0 if e else 1.0, np.float32)
    return d


MAMBA_DEBUG = False

def build_mamba_launch(staged):
    nc = _new_nc()
    hs = {k: _dram_in(nc, k, v) for k, v in staged.items()}
    for k in ("c_rho", "c_w1", "c_selA", "c_selE", "c_selO"):
        hs[k] = nc.dram_tensor(k, [128, 1], F32, kind="ExternalInput").ap()
    for k in ("c_rho2", "c_dsel", "c_dseli"):
        hs[k] = nc.dram_tensor(k, [1, 1], F32, kind="ExternalInput").ap()
    x_h = nc.dram_tensor("x_in", [TOK, D], F32, kind="ExternalInput").ap()
    y_h = nc.dram_tensor("y_out", [TOK, D], F32, kind="ExternalOutput").ap()
    es = contextlib.ExitStack()
    with es:
        pb = PB(nc, es)
        stream = _sb(nc, es, "stream", [128, NT, 1024], F32)
        identb = _sb(nc, es, "identb", [128, 128], BF16)
        utri_sb = _sb(nc, es, "utri_sb", [128, 128], F32)
        triu_sb = _sb(nc, es, "triu_sb", [128, 128], F32)
        csts = emit_const_pool(pb, nc, es, [EPS, 1.0])
        lnsc = mk_ln_scratch(nc, es, "ln_", csts[EPS])
        lnsc["one"] = csts[1.0]

        def ldi(g):
            pb.inc("wld", g.dma_start(identb[:, :], hs["ident"][:, :]), 16)
            pb.inc("wld", g.dma_start(utri_sb[:, :], hs["utri_incl"][:, :]), 16)
            pb.inc("wld", g.dma_start(triu_sb[:, :], hs["triu01"][:, :]), 16)
        pb.on("gpsimd", ldi)

        def wldw(t):
            pb.wait(t, "wld")
        pb.on("tensor", wldw)

        emit_stream_load(pb, stream, x_h)
        io = dict(hs)
        io.update(stream=stream, identb=identb, lnsc=lnsc,
                  utri_incl_sb=utri_sb, triu01_sb=triu_sb)
        if MAMBA_DEBUG:
            io["_dbg0"] = {
                "x1": nc.dram_tensor("dbg_x1", [128, 2048], BF16, kind="ExternalOutput").ap(),
                "xstm": nc.dram_tensor("dbg_xstm", [128, 2048], BF16, kind="ExternalOutput").ap(),
                "gm": nc.dram_tensor("dbg_gm", [128, 128], BF16, kind="ExternalOutput").ap(),
                "acum": nc.dram_tensor("dbg_acum", [128, 32], F32, kind="ExternalOutput").ap(),
                "dt": nc.dram_tensor("dbg_dt", [128, 32], F32, kind="ExternalOutput").ap(),
                "xbcp": nc.dram_tensor("dbg_xbcp", [128, 18, 512], BF16, kind="ExternalOutput").ap(),
            }
        emit_mamba(pb, es, io)
        if MAMBA_DEBUG:
            dbgy = nc.dram_tensor("dbg_y", [TOK, 2048], F32, kind="ExternalOutput").ap()
            dbgz = nc.dram_tensor("dbg_z", [TOK, 2048], BF16, kind="ExternalOutput").ap()
            dbgx = nc.dram_tensor("dbg_xbcr", [128, 18, 2, 1024], BF16, kind="ExternalOutput").ap()
            ydr = io["_dbg"]["y"]
            zdr = io["_dbg"]["z"]
            xdr = io["_dbg"]["x"]
            def dbg(sync):
                pb.wait(sync, "c_yst")
                pb.inc("dbgo", sync.dma_start(dbgy, ydr), 16)
                pb.inc("dbgo", sync.dma_start(dbgz, zdr), 16)
                pb.inc("dbgo", sync.dma_start(dbgx, xdr), 16)
                pb.wait(sync, "dbgo")
            pb.on("sync", dbg)
        emit_stream_store(pb, stream, y_h, ["lnw"])
    return nc


def _t(a):
    return np.ascontiguousarray(np.asarray(a).T.astype(np.float32))


def _tile_kf(wT, KC, FC, FW):
    K, F = wT.shape
    return np.ascontiguousarray(
        wT.reshape(KC, 128, FC, FW).transpose(2, 1, 0, 3))


def stage_ffn(inp):
    d = {}
    d["w1ft"] = _tile_kf(_t(inp["L1_ffn_w1_w"]), 8, 32, 128)
    d["w2ft"] = _tile_kf(_t(inp["L1_ffn_w2_w"]), 8, 32, 128)
    d["h1ft"] = _tile_kf(_t(inp["L1_ffn_h1_w"]), 32, 32, 128)
    d["h2ft"] = _tile_kf(_t(inp["L1_ffn_h2_w"]), 32, 32, 128)
    d["w3ft"] = _tile_kf(_t(inp["L1_ffn_w3_w"]), 32, 2, 512)
    d["s1c"] = np.ascontiguousarray(np.asarray(inp["L1_ffn_w1_s"], np.float32).reshape(32, 128).T)
    d["s2c"] = np.ascontiguousarray(np.asarray(inp["L1_ffn_w2_s"], np.float32).reshape(32, 128).T)
    d["sh1c"] = np.ascontiguousarray(np.asarray(inp["L1_ffn_h1_s"], np.float32).reshape(32, 128).T)
    d["sh2c"] = np.ascontiguousarray(np.asarray(inp["L1_ffn_h2_s"], np.float32).reshape(32, 128).T)
    d["s3bc"] = np.ascontiguousarray(np.broadcast_to(np.asarray(inp["L1_ffn_w3_s"], np.float32), (128, 1024)))
    d["ident"] = np.eye(128, dtype=np.float32)
    return d


def emit_phase_barrier(pb, sem, target):
    for eng in ("sync", "gpsimd", "tensor", "vector", "scalar"):
        def w(e, sem=sem, target=target):
            pb.wait(e, sem, target)
        pb.on(eng, w)


def build_full_launch(inp_shapes):
    nc = _new_nc()
    hs = {}
    for name, (shape, dt) in inp_shapes.items():
        hs[name] = nc.dram_tensor(name, list(shape), dt, kind="ExternalInput").ap()
    y_h = nc.dram_tensor("y_out", [TOK, D], F32, kind="ExternalOutput").ap()
    es = contextlib.ExitStack()
    with es:
        pb = PB(nc, es)
        stream = _sb(nc, es, "stream", [128, NT, 1024], F32)
        identb = _sb(nc, es, "identb", [128, 128], BF16)
        csts = emit_const_pool(pb, nc, es, [EPS, 1.0])
        lnsc = mk_ln_scratch(nc, es, "ln_", csts[EPS])
        lnsc["one"] = csts[1.0]

        def ldi(g):
            pb.inc("wld", g.dma_start(identb[:, :], hs["ident"][:, :]), 16)
        pb.on("gpsimd", ldi)

        io = dict(hs)
        io.update(stream=stream, identb=identb, lnsc=lnsc)

        with contextlib.ExitStack() as esA:
            emit_attn(pb, esA, io)
        emit_phase_barrier(pb, "lnw", 16)

        with contextlib.ExitStack() as esB:
            emit_moe(pb, esB, io)
        emit_phase_barrier(pb, "lnw", 32)

        with contextlib.ExitStack() as esC:
            utri_sb = _sb(nc, esC, "utri_sb", [128, 128], F32)
            triu_sb = _sb(nc, esC, "triu_sb", [128, 128], F32)

            def ldu(g):
                pb.inc("c_wld", g.dma_start(utri_sb[:, :], hs["utri_incl"][:, :]), 16)
                pb.inc("c_wld", g.dma_start(triu_sb[:, :], hs["triu01"][:, :]), 16)
            pb.on("gpsimd", ldu)
            io2 = dict(io)
            io2.update(utri_incl_sb=utri_sb, triu01_sb=triu_sb)
            emit_mamba(pb, esC, io2)
        emit_phase_barrier(pb, "lnw", 64)

        with contextlib.ExitStack() as esD:
            emit_ffn(pb, esD, io)
        emit_stream_store(pb, stream, y_h, ["lnw"])
    return nc


def stage_all(inp):
    shared = {}
    shared.update(stage_attn(inp))
    shared.update(stage_moe(inp))
    shared.update(stage_mamba(inp))
    shared.update(stage_ffn(inp))
    maps = []
    for core in range(NCORE):
        d = dict(shared)
        d.update(stage_attn_core(inp, core, shared))
        d.update(stage_moe_core(core))
        d.update(stage_mamba_core(core))
        maps.append(d)
    return maps


_DT_MAP = {"float32": F32, "bfloat16": BF16}


def kernel_fused(**inputs):
    inp = {k: np.asarray(v) for k, v in inputs.items()}
    maps = stage_all(inp)
    inp_shapes = {k: (v.shape, _DT_MAP[str(v.dtype)]) for k, v in maps[0].items()}
    nc = build_full_launch(inp_shapes)
    res = run_bass_kernel_spmd(nc, maps, core_ids=list(range(NCORE)))
    return _reassemble([r["y_out"] for r in res.results])


def _reassemble(per_core):
    out = np.zeros((B, T, D), np.float32)
    for core in range(NCORE):
        b, half = core // 2, core % 2
        out[b][_own_tokens(half)] = per_core[core]
    return out


def kernel(**inputs):
    return kernel_fused(**inputs)


def kernel_multilaunch(**inputs):
    inp = {k: np.asarray(v) for k, v in inputs.items()}

    sh = stage_attn(inp)
    nc_a = build_attn_launch(sh)
    maps = []
    for core in range(NCORE):
        d = dict(sh)
        d.update(stage_attn_core(inp, core, sh))
        maps.append(d)
    res = run_bass_kernel_spmd(nc_a, maps, core_ids=list(range(NCORE)))
    x1 = [r["y_out"] for r in res.results]

    sm = stage_moe(inp)
    nc_b = build_moe_launch(sm)
    maps = []
    for core in range(NCORE):
        d = dict(sm)
        d.update(stage_moe_core(core))
        d["x_in"] = np.ascontiguousarray(x1[core])
        maps.append(d)
    res = run_bass_kernel_spmd(nc_b, maps, core_ids=list(range(NCORE)))
    x2 = [r["y_out"] for r in res.results]

    sc_ = stage_mamba(inp)
    nc_c = build_mamba_launch(sc_)
    maps = []
    for core in range(NCORE):
        d = dict(sc_)
        d.update(stage_mamba_core(core))
        d["x_in"] = np.ascontiguousarray(x2[core])
        maps.append(d)
    res = run_bass_kernel_spmd(nc_c, maps, core_ids=list(range(NCORE)))
    x3 = [r["y_out"] for r in res.results]

    sf = stage_ffn(inp)
    nc_d = build_ffn_launch(sf)
    maps = []
    for core in range(NCORE):
        d = dict(sf)
        d["x_in"] = np.ascontiguousarray(x3[core])
        maps.append(d)
    res = run_bass_kernel_spmd(nc_d, maps, core_ids=list(range(NCORE)))
    x4 = [r["y_out"] for r in res.results]

    return _reassemble(x4)


# version 06
# speedup vs baseline: 1.1287x; 1.0990x over previous
import contextlib
import numpy as np
import concourse.bass as bass
import concourse.mybir as mybir
from concourse.bass_utils import run_bass_kernel_spmd

F32 = mybir.dt.float32
BF16 = mybir.dt.bfloat16
AF = mybir.ActivationFunctionType
ALU = mybir.AluOpType
AX = mybir.AxisListType.X

NCORE = 8
B = 4
T = 4096
D = 1024
NT = 16
TOK = NT * 128
EPS = 1e-5

DEPTH_PLAN = [3, 3, 3, 3, 4, 4, 4, 4, 7, 7, 7, 7, 8, 8, 8, 8]

PERM = np.concatenate([np.arange(0, D, 2), np.arange(1, D, 2)])


def _chunks_of(half):
    if half == 0:
        return list(range(0, 8)) + list(range(24, 32))
    return list(range(8, 24))


def _own_tokens(half):
    idx = []
    for c in _chunks_of(half):
        idx.extend(range(c * 128, (c + 1) * 128))
    return np.array(idx)


_SEM_VEC = {"a_epi", "a_evO", "a_kev", "a_mgen", "a_msk", "a_qev", "a_rope",
            "a_tre2", "a_vev", "c_D2", "c_M", "c_X", "c_ac1", "c_acev",
            "c_btev", "c_c32", "c_cd1", "c_cv1", "c_cvh", "c_dc0", "c_dcev",
            "c_dev", "c_dt0", "c_dt2", "c_fmev", "c_gev", "c_halo", "c_lnx",
            "c_opev", "c_rms1", "c_rsm", "c_sb1", "c_sini", "c_sprep",
            "c_supd", "c_trc", "c_tre", "c_xev", "c_xsev", "c_y2", "c_y3",
            "c_yas", "c_yc2", "c_ycev", "c_ypsfree", "c_ytev", "c_zev", "ev",
            "evA", "evD", "lnc2free", "m_cum", "m_gev", "m_hev", "m_oev",
            "m_offev", "m_rbev", "m_res", "m_rowf", "m_rt1", "m_rt2",
            "m_trel", "m_wcap", "m_wgl", "trc", "tre"}
_SEM_PE = {"a_kmm", "a_omm", "a_pvmm", "a_qkmm", "a_qmm", "a_trm2", "a_vmm",
           "c_acm", "c_btm", "c_cdm", "c_dbm", "c_gm", "c_imm", "c_lbm",
           "c_opm", "c_row", "c_sb2", "c_trf", "c_trm", "c_um", "c_xsm",
           "c_ycm", "c_ydm", "c_yom", "c_ytm", "m_gmm", "m_hmm", "m_offm",
           "m_omm", "m_posm", "m_rbm", "m_summ", "m_trml", "mmA", "mmB0",
           "mmB1", "mmD", "trm"}
_SEM_ACT = {"a_exp", "ac", "acA", "acB0", "acB1", "c_E", "c_ac0", "c_cd0",
            "c_cv2", "c_cvdone", "c_dc1", "c_dt1", "c_ea", "c_rms0",
            "c_rms2", "c_sb0", "c_zs", "m_hact", "m_sg"}
_SEM_GPSE = {"a_cst", "c_cst", "c_s0", "cst", "m_capc", "m_cst"}


def _sem_pool(name):
    if name in _SEM_VEC:
        return "_vec"
    if name in _SEM_PE:
        return "_pe"
    if name in _SEM_ACT:
        return "_act"
    if name in _SEM_GPSE:
        return "_gpse"
    return name


class PB:

    def __init__(self, nc, es):
        self.nc = nc
        self.es = es
        self.blk = es.enter_context(nc.Block())
        self.sems = {}

    def semh(self, name):
        key = _sem_pool(name)
        if key not in self.sems:
            self.sems[key] = [self.es.enter_context(self.nc.semaphore(key)), 0]
        return self.sems[key]

    def inc(self, name, instr, n=1):
        h = self.semh(name)
        instr.then_inc(h[0], n)
        h[1] += n
        return h[1]

    def val(self, name):
        return self.semh(name)[1]

    def wait(self, eng, name, val=None):
        h = self.semh(name)
        v = h[1] if val is None else val
        if v > 0:
            eng.wait_ge(h[0], v)

    def on(self, engname, f):
        getattr(self.blk, engname)(f)


def _sb(nc, es, name, shape, dtype):
    return es.enter_context(nc.sbuf_tensor(name, list(shape), dtype))


def _ps(nc, es, name, shape, dtype=F32):
    return es.enter_context(nc.psum_tensor(name, list(shape), dtype))


def emit_const_pool(pb, nc, es, values):
    cp = _sb(nc, es, "cpool", [128, max(1, len(values))], F32)
    aps = {}
    def g0(g):
        for i, v in enumerate(values):
            ins = g.memset(cp[:, i:i + 1], float(v))
        pb.inc("cst", ins)
    pb.on("gpsimd", g0)
    for i, v in enumerate(values):
        aps[v] = cp[:, i:i + 1]
    for eng in ("vector", "scalar", "tensor"):
        def w(e):
            pb.wait(e, "cst")
        pb.on(eng, w)
    return aps


def mk_ln_scratch(nc, es, pfx, eps_ap=None):
    return {"eps": eps_ap,
            "junk": _sb(nc, es, pfx + "junk", [128, 2048], BF16),
            "lnt": _sb(nc, es, pfx + "lnt", [128, 1024], F32),
            "c1": _sb(nc, es, pfx + "c1", [128, 1], F32),
            "c2": _sb(nc, es, pfx + "c2", [128, 1], F32),
            "c3": _sb(nc, es, pfx + "c3", [128, 1], F32),
            "c4": _sb(nc, es, pfx + "c4", [128, 1], F32)}


def emit_ln(pb, sc, rtile, out_tile, ready=None, gain_bc=None, bias_bc=None, width=1024):
    c1, c2, c3, c4 = sc["c1"], sc["c2"], sc["c3"], sc["c4"]
    junk = sc["junk"]

    def v1(v):
        if ready is not None:
            pb.wait(v, ready[0], ready[1])
        v.drain()
        v.tensor_reduce(c1[:, :], rtile, axis=AX, op=ALU.add)
        v.drain()
        v.tensor_scalar(c1[:, :], c1[:, :], 1.0 / width, None, op0=ALU.mult)
        v.drain()
        pb.wait(v, "ac")
        v.scalar_tensor_tensor(junk[:, :width], rtile, 1.0, rtile,
                               op0=ALU.mult, op1=ALU.mult, accum_out=c2[:, :])
        v.drain()
        v.tensor_scalar(c2[:, :], c2[:, :], 1.0 / width, float(EPS),
                        op0=ALU.mult, op1=ALU.add)
        v.drain()
        v.tensor_tensor(c3[:, :], c1[:, :], c1[:, :], op=ALU.mult)
        v.drain()
        pb.inc("ev", v.tensor_tensor(c2[:, :], c2[:, :], c3[:, :], op=ALU.subtract))
    pb.on("vector", v1)

    def s2(s):
        pb.wait(s, "ev")
        s.activation(c3[:, :], c2[:, :], AF.Ln)
        s.drain()
        pb.inc("ac", s.activation(c4[:, :], c3[:, :], AF.Exp, scale=-0.5))
    pb.on("scalar", s2)

    def v3(v):
        pb.wait(v, "ac")
        if gain_bc is None:
            pb.inc("lnw", v.tensor_scalar(out_tile, rtile, c1[:, :], c4[:, :],
                                          op0=ALU.subtract, op1=ALU.mult))
        else:
            v.tensor_scalar(sc["lnt"][:, :width], rtile, c1[:, :], c4[:, :],
                            op0=ALU.subtract, op1=ALU.mult)
            v.drain()
            v.tensor_tensor(sc["lnt"][:, :width], sc["lnt"][:, :width], gain_bc, op=ALU.mult)
            v.drain()
            pb.inc("lnw", v.tensor_tensor(out_tile, sc["lnt"][:, :width], bias_bc, op=ALU.add))
    pb.on("vector", v3)


def emit_cast_transpose(pb, stream_tiles, xbf, ident_bf, xT_dst, tps, tag):
    for k, (src, pos) in enumerate(stream_tiles):
        def vc(v, src=src):
            pb.wait(v, "trm")
            pb.inc("trc", v.tensor_copy(xbf[:, :], src))
        pb.on("vector", vc)

        def tt(t, k=k):
            pb.wait(t, "wld")
            pb.wait(t, "trc")
            pb.wait(t, "tre", max(0, pb.val("tre") - 1))
            for dc in range(7):
                t.transpose(tps[:, k % 2, dc * 128:(dc + 1) * 128],
                            xbf[:, dc * 128:(dc + 1) * 128], ident_bf[:, :])
            pb.inc("trm", t.transpose(tps[:, k % 2, 7 * 128:8 * 128],
                                      xbf[:, 7 * 128:8 * 128], ident_bf[:, :]))
        pb.on("tensor", tt)

        def ve(v, k=k, pos=pos):
            pb.wait(v, "trm")
            pb.inc("tre", v.tensor_copy(
                xT_dst[:, :, pos * 128:(pos + 1) * 128],
                tps[:, k % 2, :].rearrange("p (a b) -> p a b", a=8)))
        pb.on("vector", ve)


FFN_PARTS = 99

def emit_ffn(pb, es, io):
    nc = pb.nc
    stream = io["stream"]
    sc = io["lnsc"]
    identb = io["identb"]
    SLAB = 4
    NSLAB = NT // SLAB
    ST = SLAB * 128

    xT = _sb(nc, es, "fd_xT", [128, 8, ST], BF16)
    t1a = _sb(nc, es, "fd_t1a", [128, 2, ST], BF16)
    t1 = _sb(nc, es, "fd_t1", [128, 32, ST], BF16)
    t2 = _sb(nc, es, "fd_t2", [128, 32, ST], BF16)
    xbf = _sb(nc, es, "fd_xbf", [128, 1024], BF16)
    w1b = _sb(nc, es, "fd_w1b", [128, 2, 8, 128], BF16)
    w2b = _sb(nc, es, "fd_w2b", [128, 2, 8, 128], BF16)
    hb = _sb(nc, es, "fd_hb", [128, 2, 32, 128], BF16)
    s1c = _sb(nc, es, "fd_s1c", [128, 32], F32)
    s2c = _sb(nc, es, "fd_s2c", [128, 32], F32)
    sh1c = _sb(nc, es, "fd_sh1c", [128, 32], F32)
    sh2c = _sb(nc, es, "fd_sh2c", [128, 32], F32)
    s3bc = _sb(nc, es, "fd_s3bc", [128, 1024], F32)
    rt = _sb(nc, es, "fd_rt", [128, SLAB, 1024], F32)
    tps = _ps(nc, es, "fd_tps", [128, 2, 1024], BF16)
    psA = _ps(nc, es, "fd_psA", [128, 2, 512], F32)
    psB = _ps(nc, es, "fd_psB", [128, 2, 512], F32)
    psO = _ps(nc, es, "fd_psO", [128, 2, 512], F32)

    w1r = io["w1ft"]
    w2r = io["w2ft"]
    h1r = io["h1ft"]
    h2r = io["h2ft"]
    w3r = io["w3ft"]

    def ld0(g):
        pb.inc("wld", g.dma_start(s1c[:, :], io["s1c"][:, :]), 16)
        pb.inc("wld", g.dma_start(s2c[:, :], io["s2c"][:, :]), 16)
        pb.inc("wld", g.dma_start(sh1c[:, :], io["sh1c"][:, :]), 16)
        pb.inc("wld", g.dma_start(sh2c[:, :], io["sh2c"][:, :]), 16)
        pb.inc("wld", g.dma_start(s3bc[:, :], io["s3bc"][:, :]), 16)
    pb.on("gpsimd", ld0)

    for sl in range(NSLAB if FFN_PARTS >= 99 else 1):
        tiles = list(range(sl * SLAB, (sl + 1) * SLAB))
        if FFN_PARTS < 1:
            break
        emit_cast_transpose(pb, [(stream[:, i, :], pos) for pos, i in enumerate(tiles)],
                            xbf, identb, xT, tps, f"f{sl}")

        mmA_after, evA_after, acA_after = {}, {}, {}
        for fc in range(32 if FFN_PARTS >= 2 else 0):
            bf = fc % 2
            def lw(g, fc=fc, bf=bf):
                pb.wait(g, "wld")
                if fc >= 2:
                    pb.wait(g, "mmA", mmA_after[fc - 2])
                else:
                    pb.wait(g, "mmA")
                pb.inc("wldA", g.dma_start(w1b[:, bf], w1r[fc]), 16)
                pb.inc("wldA", g.dma_start(w2b[:, bf], w2r[fc]), 16)
            pb.on("gpsimd", lw)

            def mm(t, fc=fc, bf=bf):
                pb.wait(t, "wldA")
                pb.wait(t, "tre")
                if fc >= 2:
                    pb.wait(t, "acA", acA_after[fc - 2])
                    pb.wait(t, "evA", evA_after[fc - 2])
                else:
                    pb.wait(t, "acB1")
                for dc in range(8):
                    t.matmul(psA[:, bf], w1b[:, bf, dc, :], xT[:, dc, :],
                             start=(dc == 0), stop=(dc == 7))
                for dc in range(8):
                    ins = t.matmul(psB[:, bf], w2b[:, bf, dc, :], xT[:, dc, :],
                                   start=(dc == 0), stop=(dc == 7))
                mmA_after[fc] = pb.inc("mmA", ins)
            pb.on("tensor", mm)

            def ev_a(s, fc=fc, bf=bf):
                pb.wait(s, "mmA", mmA_after[fc])
                if fc >= 2:
                    pb.wait(s, "evA", evA_after[fc - 2])
                else:
                    pb.wait(s, "evA")
                acA_after[fc] = pb.inc("acA", s.activation(
                    t1a[:, bf, :], psA[:, bf], AF.Silu, scale=s1c[:, fc:fc + 1]))
            pb.on("scalar", ev_a)

            def ev_b(v, fc=fc, bf=bf):
                pb.wait(v, "acA", acA_after[fc])
                evA_after[fc] = pb.inc("evA", v.scalar_tensor_tensor(
                    t1[:, fc, :], psB[:, bf], s2c[:, fc:fc + 1], t1a[:, bf, :],
                    op0=ALU.mult, op1=ALU.mult))
            pb.on("vector", ev_b)

        for step, (hr, scol, src, dst) in enumerate(
                ([(h1r, sh1c, t1, t2), (h2r, sh2c, t2, t1)][:max(0, FFN_PARTS - 2)])):
            mmB_after, acB_after = {}, {}
            sem_m, sem_a, sem_w = f"mmB{step}", f"acB{step}", f"wldB{step}"
            for fc in range(32):
                bf = fc % 2
                def lh(g, fc=fc, bf=bf):
                    if fc >= 2:
                        pb.wait(g, sem_m, mmB_after[fc - 2])
                    else:
                        pb.wait(g, "mmB0")
                        pb.wait(g, "mmB1")
                    pb.inc(sem_w, g.dma_start(hb[:, bf], hr[fc]), 16)
                pb.on("gpsimd", lh)

                def mm(t, fc=fc, bf=bf):
                    pb.wait(t, sem_w)
                    if step == 0:
                        pb.wait(t, "evA")
                    else:
                        pb.wait(t, "acB0")
                        pb.wait(t, "evA")
                    if fc >= 2:
                        pb.wait(t, sem_a, acB_after[fc - 2])
                    for kc in range(32):
                        ins = t.matmul(psA[:, bf], hb[:, bf, kc, :], src[:, kc, :],
                                       start=(kc == 0), stop=(kc == 31))
                    mmB_after[fc] = pb.inc(sem_m, ins)
                pb.on("tensor", mm)

                def ev(s, fc=fc, bf=bf):
                    pb.wait(s, sem_m, mmB_after[fc])
                    acB_after[fc] = pb.inc(sem_a, s.activation(
                        dst[:, fc, :], psA[:, bf], AF.Gelu, scale=scol[:, fc:fc + 1]))
                pb.on("scalar", ev)

        evD_after = {}
        for half in range(2 if FFN_PARTS >= 5 else 0):
            def lw3(g, half=half):
                pb.wait(g, "mmB1")
                pb.wait(g, "mmD")
                pb.inc("wldD", g.dma_start(t2[:, :, :], w3r[half]), 16)
            pb.on("gpsimd", lw3)
            for ti in range(SLAB):
                def mm(t, half=half, ti=ti):
                    pb.wait(t, "wldD")
                    pb.wait(t, "acB1")
                    seq = half * SLAB + ti
                    if seq >= 2:
                        pb.wait(t, "evD", evD_after[seq - 2])
                    else:
                        pb.wait(t, "evD")
                    for kc in range(32):
                        ins = t.matmul(psO[:, ti % 2], t1[:, kc, ti * 128:(ti + 1) * 128],
                                       t2[:, kc, :], start=(kc == 0), stop=(kc == 31))
                    pb.inc("mmD", ins)
                pb.on("tensor", mm)

                def ev(v, half=half, ti=ti):
                    pb.wait(v, "mmD")
                    if half == 0:
                        pb.wait(v, "ac")
                    v.tensor_tensor(rt[:, ti, half * 512:(half + 1) * 512], psO[:, ti % 2],
                                    s3bc[:, half * 512:(half + 1) * 512], op=ALU.mult)
                    v.drain()
                    evD_after[half * SLAB + ti] = pb.inc("evD", v.tensor_tensor(
                        rt[:, ti, half * 512:(half + 1) * 512],
                        rt[:, ti, half * 512:(half + 1) * 512],
                        stream[:, tiles[ti], half * 512:(half + 1) * 512], op=ALU.add))
                pb.on("vector", ev)
        for ti in range(SLAB if FFN_PARTS >= 6 else 0):
            emit_ln(pb, sc, rt[:, ti, :], stream[:, tiles[ti], :],
                    ready=("evD", evD_after[SLAB + ti]))


def emit_stream_load(pb, stream, x_h):
    def ld(sync):
        for i in range(NT):
            pb.inc("sld", sync.dma_start(stream[:, i, :],
                                         x_h[i * 128:(i + 1) * 128, :]), 16)
    pb.on("sync", ld)
    for eng in ("vector", "scalar", "tensor", "gpsimd"):
        def w(e):
            pb.wait(e, "sld")
        pb.on(eng, w)


def emit_stream_store(pb, stream, y_h, wait_sems):
    def st(sync):
        for s in wait_sems:
            pb.wait(sync, s)
        for i in range(NT):
            pb.inc("ost", sync.dma_start(y_h[i * 128:(i + 1) * 128, :],
                                         stream[:, i, :]), 16)
        pb.wait(sync, "ost")
    pb.on("sync", st)


def _new_nc():
    return bass.Bass("TRN2", target_bir_lowering=False, detect_race_conditions=False)


def _dram_in(nc, name, arr):
    return nc.dram_tensor(name, list(arr.shape), F32, kind="ExternalInput").ap()


def build_ffn_launch(staged):
    nc = _new_nc()
    hs = {k: _dram_in(nc, k, v) for k, v in staged.items()}
    x_h = nc.dram_tensor("x_in", [TOK, D], F32, kind="ExternalInput").ap()
    y_h = nc.dram_tensor("y_out", [TOK, D], F32, kind="ExternalOutput").ap()
    es = contextlib.ExitStack()
    with es:
        pb = PB(nc, es)
        stream = _sb(nc, es, "stream", [128, NT, 1024], F32)
        identb = _sb(nc, es, "identb", [128, 128], BF16)
        csts = emit_const_pool(pb, nc, es, [EPS, 1.0])
        lnsc = mk_ln_scratch(nc, es, "ln_", csts[EPS])

        def ldi(g):
            pb.inc("wld", g.dma_start(identb[:, :], hs["ident"][:, :]), 16)
        pb.on("gpsimd", ldi)

        emit_stream_load(pb, stream, x_h)
        io = dict(hs)
        io.update(stream=stream, identb=identb, lnsc=lnsc)
        emit_ffn(pb, es, io)
        emit_stream_store(pb, stream, y_h, ["lnw"] if FFN_PARTS >= 6 else [])
    return nc




POS_E = [0, 0, 0, 0, 1, 1, 1, 1, 6, 6, 6, 6, 7, 7, 7, 7]
POS_O = [2, 2, 2, 2, 3, 3, 3, 3, 4, 4, 4, 4, 5, 5, 5, 5]


def emit_rope(pb, rx, xfm, ctab, stab, tmp1, tmp2, tag):
    def body(v):
        for i in range(4):
            xE = xfm[:, i, :]
            xO = xfm[:, 4 + i, :]
            c = ctab[:, i, :]
            sn = stab[:, i, :]
            v.tensor_tensor(tmp1[:, :], xE, c, op=ALU.mult)
            v.tensor_tensor(tmp2[:, :], xO, sn, op=ALU.mult)
            v.drain()
            v.tensor_tensor(rx[:, i, :], tmp1[:, :], tmp2[:, :], op=ALU.subtract)
            v.tensor_tensor(tmp1[:, :], xO, c, op=ALU.mult)
            v.tensor_tensor(tmp2[:, :], xE, sn, op=ALU.mult)
            v.drain()
            ins = v.tensor_tensor(rx[:, 4 + i, :], tmp1[:, :], tmp2[:, :], op=ALU.add)
        pb.inc("a_rope", ins)
    pb.on("vector", body)


def emit_attn(pb, es, io):
    nc = pb.nc
    stream = io["stream"]
    sc = io["lnsc"]
    identb = io["identb"]

    k_dram = nc.dram_tensor("a_kdram", [1024, 4096], BF16, kind="Internal").ap()
    v_dram = nc.dram_tensor("a_vdram", [32, 128, 16, 65], BF16, kind="Internal").ap()
    ao_dram = nc.dram_tensor("a_aodram", [TOK, 1024], BF16, kind="Internal").ap()
    q_dram = nc.dram_tensor("a_qdram", [1024, TOK], BF16, kind="Internal").ap()

    qslot = _sb(nc, es, "a_qslot", [128, 2, 8, 128], BF16)
    kslab = _sb(nc, es, "a_kslab", [128, 2, 8, 512], BF16)
    vslab = _sb(nc, es, "a_vslab", [128, 2, 4, 16, 65], BF16)
    ptsb = _sb(nc, es, "a_ptsb", [128, 2, 1024], BF16)
    maskE = _sb(nc, es, "a_maskE", [128, 512], BF16)
    maskO = _sb(nc, es, "a_maskO", [128, 512], BF16)
    kabs = _sb(nc, es, "a_kabs", [128, 512], F32)
    visb = _sb(nc, es, "a_visb", [128, 16, 8], F32)
    dgapE = _sb(nc, es, "a_dgapE", [128, 16], F32)
    dgapO = _sb(nc, es, "a_dgapO", [128, 16], F32)
    attn_sb = _sb(nc, es, "a_attnsb", [128, 1024], BF16)
    rcp = _sb(nc, es, "a_rcp", [128, 1], F32)
    wob = _sb(nc, es, "a_wob", [128, 8, 1024], BF16)
    wqb = wob

    es1 = contextlib.ExitStack()
    xfm = _sb(nc, es1, "a_xfm", [128, 8, 512], BF16)
    rx = _sb(nc, es1, "a_rx", [128, 8, 512], BF16)
    ctab = _sb(nc, es1, "a_ctab", [128, 4, 512], BF16)
    stab = _sb(nc, es1, "a_stab", [128, 4, 512], BF16)
    tmp1 = _sb(nc, es1, "a_tmp1", [128, 512], BF16)
    tmp2 = _sb(nc, es1, "a_tmp2", [128, 512], BF16)
    wvb = _sb(nc, es1, "a_wvb", [128, 8, 1024], BF16)
    wkb = _sb(nc, es1, "a_wkb", [128, 8, 1024], BF16)

    vstage = _sb(nc, es1, "a_vstage", [128, 16, 65], BF16)
    kstage = _sb(nc, es1, "a_kstage", [128, 8, 512], BF16)
    psV = _ps(nc, es1, "a_psV", [128, 2, 512], F32)

    xk_r = io["x_fmp"].rearrange("(dc p) t -> p dc t", p=128)
    xq_r = io["x_fmq"].rearrange("(dc p) t -> p dc t", p=128)
    cosk_r = io["cos_k"].rearrange("(c p) t -> p c t", p=128)
    sink_r = io["sin_k"].rearrange("(c p) t -> p c t", p=128)
    cosq_r = io["cos_q"].rearrange("(c p) t -> p c t", p=128)
    sinq_r = io["sin_q"].rearrange("(c p) t -> p c t", p=128)
    wv_r = io["wvT"].rearrange("(dc p) f -> p dc f", p=128)
    wk_r = io["wkT"].rearrange("(dc p) f -> p dc f", p=128)
    wq_r = io["wqT"].rearrange("(dc p) f -> p dc f", p=128)
    wo_r = io["woT"].rearrange("(dc p) f -> p dc f", p=128)
    k_dram_r = k_dram.rearrange("(fc p) t -> p fc t", p=128)
    q_dram_r = q_dram.rearrange("(fc p) t -> p fc t", p=128)
    v_dram_r = v_dram.rearrange("kc p h c -> p kc (h c)")

    def ld0(g):
        pb.inc("a_wld", g.dma_start(wvb[:, :, :], wv_r[:, :, :]), 16)
        pb.inc("a_wld", g.dma_start(wkb[:, :, :], wk_r[:, :, :]), 16)
        pb.inc("a_wld", g.dma_start(wqb[:, :, :], wq_r[:, :, :]), 16)
        pb.inc("a_wld", g.dma_start(kabs[:, :], io["kabs"][:, :]), 16)
        pb.inc("a_wld", g.dma_start(visb[:, :, :], io["visb"][:, :, :]), 16)
        pb.inc("a_wld", g.dma_start(dgapE[:, :], io["dgapE"][:, :]), 16)
        pb.inc("a_wld", g.dma_start(dgapO[:, :], io["dgapO"][:, :]), 16)
        pb.inc("a_cst", g.memset(vstage[:, :, 64:65], 1.0))
    pb.on("gpsimd", ld0)

    vev_after = {}
    kev_after = {}
    qev_after = {}
    for ts in range(8):
        def ldx(sync, ts=ts):
            pb.wait(sync, "a_p1done")
            pb.inc("a_xld", sync.dma_start(ctab[:, :, :], cosk_r[:, :, ts * 512:(ts + 1) * 512]), 16)
            pb.inc("a_xld", sync.dma_start(stab[:, :, :], sink_r[:, :, ts * 512:(ts + 1) * 512]), 16)
        pb.on("sync", ldx)

        def ldx2(g, ts=ts):
            pb.wait(g, "a_p1done")
            pb.inc("a_xldg", g.dma_start(xfm[:, :, :], xk_r[:, :, ts * 512:(ts + 1) * 512]), 16)
        pb.on("gpsimd", ldx2)

        for tt in range(4):
            def mmv(t, ts=ts, tt=tt):
                pb.wait(t, "a_xld")
                pb.wait(t, "a_xldg")
                pb.wait(t, "a_wld")
                pb.wait(t, "a_vev")
                pb.wait(t, "a_kev")
                for hf in range(2):
                    for dc in range(8):
                        ins = t.matmul(psV[:, hf, :], xfm[:, dc, tt * 128:(tt + 1) * 128],
                                       wvb[:, dc, hf * 512:(hf + 1) * 512],
                                       start=(dc == 0), stop=(dc == 7))
                pb.inc("a_vmm", ins)
            pb.on("tensor", mmv)

            def evv(v, ts=ts, tt=tt):
                pb.wait(v, "a_vmm")
                pb.wait(v, "a_cst")
                pb.wait(v, "a_vst")
                v.tensor_copy(vstage[:, 0:8, 0:64],
                              psV[:, 0, :].rearrange("p (h c) -> p h c", h=8))
                v.drain()
                pb.inc("a_vev", v.tensor_copy(
                    vstage[:, 8:16, 0:64], psV[:, 1, :].rearrange("p (h c) -> p h c", h=8)))
            pb.on("vector", evv)

            def stv(sync, ts=ts, tt=tt):
                pb.wait(sync, "a_vev")
                pb.inc("a_vst", sync.dma_start(v_dram[ts * 4 + tt, :, :, :], vstage[:, :, :]), 16)
            pb.on("sync", stv)

        def wrope(v):
            pb.wait(v, "a_xld")
            pb.wait(v, "a_xldg")
            v.drain()
        pb.on("vector", wrope)
        emit_rope(pb, rx, xfm, ctab, stab, tmp1, tmp2, f"rk{ts}")

        for fc in range(8):
            def mmk(t, ts=ts, fc=fc):
                pb.wait(t, "a_rope")
                if fc < 2:
                    pb.wait(t, "a_vev")
                    pb.wait(t, "a_kev")
                else:
                    pb.wait(t, "a_kev", kev_after[(ts, fc - 2)])
                for dc in range(8):
                    ins = t.matmul(psV[:, fc % 2, :], wkb[:, dc, fc * 128:(fc + 1) * 128],
                                   rx[:, dc, :], start=(dc == 0), stop=(dc == 7))
                pb.inc("a_kmm", ins)
            pb.on("tensor", mmk)

            def evk(v, ts=ts, fc=fc):
                pb.wait(v, "a_kmm")
                kev_after[(ts, fc)] = pb.inc("a_kev", v.tensor_copy(kstage[:, fc, :], psV[:, fc % 2, :]))
            pb.on("vector", evk)

        def stk(sync, ts=ts):
            pb.wait(sync, "a_kev")
            pb.inc("a_p1done", sync.dma_start(
                k_dram_r[:, :, ts * 512:(ts + 1) * 512], kstage[:, :, :]), 16)
        pb.on("sync", stk)

    for qs in range(4):
        def ldxq(g, qs=qs):
            pb.wait(g, "a_p1done")
            if qs >= 1:
                pb.wait(g, "a_qev", qev_after[(qs - 1, 7)])
            pb.inc("a_xldg", g.dma_start(xfm[:, :, :], xq_r[:, :, qs * 512:(qs + 1) * 512]), 16)
        pb.on("gpsimd", ldxq)

        def ldxq2(sync, qs=qs):
            pb.wait(sync, "a_p1done")
            if qs >= 1:
                pb.wait(sync, "a_qev", qev_after[(qs - 1, 7)])
            pb.inc("a_xld", sync.dma_start(ctab[:, :, :], cosq_r[:, :, qs * 512:(qs + 1) * 512]), 16)
            pb.inc("a_xld", sync.dma_start(stab[:, :, :], sinq_r[:, :, qs * 512:(qs + 1) * 512]), 16)
        pb.on("sync", ldxq2)

        def wrope2(v):
            pb.wait(v, "a_xld")
            pb.wait(v, "a_xldg")
            v.drain()
        pb.on("vector", wrope2)
        emit_rope(pb, rx, xfm, ctab, stab, tmp1, tmp2, f"rq{qs}")

        for fc in range(8):
            def mmq(t, qs=qs, fc=fc):
                pb.wait(t, "a_rope")
                if qs == 0 and fc < 2:
                    pb.wait(t, "a_kev")
                elif fc < 2:
                    pb.wait(t, "a_qev", qev_after[(qs - 1, 6 + fc)])
                else:
                    pb.wait(t, "a_qev", qev_after[(qs, fc - 2)])
                for dc in range(8):
                    ins = t.matmul(psV[:, fc % 2, :], wqb[:, dc, fc * 128:(fc + 1) * 128],
                                   rx[:, dc, :], start=(dc == 0), stop=(dc == 7))
                pb.inc("a_qmm", ins)
            pb.on("tensor", mmq)

            def evq(v, qs=qs, fc=fc):
                pb.wait(v, "a_qmm")
                if fc == 0:
                    pb.wait(v, "a_qst")
                qev_after[(qs, fc)] = pb.inc("a_qev", v.tensor_copy(
                    kstage[:, fc, :], psV[:, fc % 2, :]))
            pb.on("vector", evq)

        def stq(sync, qs=qs):
            pb.wait(sync, "a_qev", qev_after[(qs, 7)])
            pb.inc("a_qst", sync.dma_start(
                q_dram_r[:, :, qs * 512:(qs + 1) * 512], kstage[:, :, :]), 16)
        pb.on("sync", stq)

    es1.close()

    with contextlib.ExitStack() as es2:
        psSP = _ps(nc, es2, "a_psSP", [128, 2, 1024], F32)
        psPO = _ps(nc, es2, "a_psPO", [128, 2, 1024], F32)

        qk_slot_end = {}
        qk_hist = []
        exp_hist = []
        pv_hist = []
        kt_last_qk = {}
        kt_last_pv = {}
        for s in range(NT):
            Ds = DEPTH_PLAN[s]

            def ldq(sync, s=s):
                pb.wait(sync, "a_qst")
                if s >= 2:
                    pb.wait(sync, "a_qkmm", qk_slot_end[s - 2])
                pb.inc("a_qld", sync.dma_start(qslot[:, s % 2, :, :],
                                               q_dram_r[:, :, s * 128:(s + 1) * 128]), 16)
            pb.on("sync", ldq)

            def genmask(v, s=s):
                v.drain()
                v.tensor_scalar(maskE[:, :], kabs[:, :], dgapE[:, s:s + 1], None, op0=ALU.is_le)
                v.drain()
                pb.inc("a_mgen", v.tensor_scalar(maskO[:, :], kabs[:, :], dgapO[:, s:s + 1], None, op0=ALU.is_le))
            pb.on("vector", genmask)

            for kt in range(Ds):
                bf = kt % 2
                def ldkv(g, s=s, kt=kt, bf=bf):
                    pb.wait(g, "a_p1done")
                    pb.wait(g, "a_vst")
                    if kt >= 2:
                        pb.wait(g, "a_qkmm", kt_last_qk[kt - 2])
                        pb.wait(g, "a_pvmm", kt_last_pv[kt - 2])
                    else:
                        pb.wait(g, "a_pvmm")
                    pb.inc("a_kvld", g.dma_start(kslab[:, bf, :, :],
                                                 k_dram_r[:, :, kt * 512:(kt + 1) * 512]), 16)
                    pb.inc("a_kvld", g.dma_start(
                        vslab[:, bf, :, :, :].rearrange("p a h c -> p a (h c)"),
                        v_dram_r[:, kt * 4:(kt + 1) * 4, :]), 16)
                pb.on("gpsimd", ldkv)

                for hp in range(8):
                    ps = hp % 2
                    idx = len(qk_hist)
                    def mmqk(t, s=s, kt=kt, hp=hp, ps=ps, bf=bf, idx=idx):
                        pb.wait(t, "a_kvld")
                        pb.wait(t, "a_qld")
                        if idx >= 2:
                            pb.wait(t, "a_exp", exp_hist[idx - 2])
                        for hh2 in range(2):
                            base = 64 * hh2
                            for kc in range(4):
                                ins = t.matmul(
                                    psSP[:, ps, hh2 * 512 + kc * 128:hh2 * 512 + (kc + 1) * 128],
                                    kslab[base:base + 64, bf, hp, kc * 128:(kc + 1) * 128],
                                    qslot[base:base + 64, s % 2, hp, :],
                                    start=(kc == 0), stop=(kc == 3))
                        v = pb.inc("a_qkmm", ins)
                        qk_hist.append(v)
                        kt_last_qk[kt] = v
                    pb.on("tensor", mmqk)

                    def expp(sx, s=s, kt=kt, hp=hp, ps=ps, idx=idx):
                        pb.wait(sx, "a_qkmm", qk_hist[idx])
                        if idx >= 2:
                            pb.wait(sx, "a_pvmm", pv_hist[idx - 2])
                        pb.wait(sx, "a_msk")
                        v = pb.inc("a_exp", sx.activation(
                            ptsb[:, ps, :], psSP[:, ps, :], AF.Exp, scale=0.125,
                            bias=visb[:, s, kt:kt + 1]))
                        exp_hist.append(v)
                    pb.on("scalar", expp)

                    do_masks = (kt == POS_E[s]) or (kt == POS_O[s])
                    if do_masks:
                        def mskv(v, s=s, kt=kt, hp=hp, ps=ps, idx=idx):
                            pb.wait(v, "a_exp", exp_hist[idx])
                            pb.wait(v, "a_mgen")
                            ins = None
                            if kt == POS_E[s]:
                                ins = v.tensor_tensor(
                                    ptsb[:, ps, :].rearrange("p (h x) -> p h x", h=2),
                                    ptsb[:, ps, :].rearrange("p (h x) -> p h x", h=2),
                                    bass.AP(maskE, 0, [[512, 128], [0, 2], [1, 512]]),
                                    op=ALU.mult)
                                v.drain()
                            if kt == POS_O[s]:
                                ins = v.tensor_tensor(
                                    ptsb[:, ps, :].rearrange("p (h x) -> p h x", h=2),
                                    ptsb[:, ps, :].rearrange("p (h x) -> p h x", h=2),
                                    bass.AP(maskO, 0, [[512, 128], [0, 2], [1, 512]]),
                                    op=ALU.mult)
                                v.drain()
                            pb.inc("a_msk", ins)
                        pb.on("vector", mskv)

                    def mmpv(t, s=s, kt=kt, hp=hp, ps=ps, do_masks=do_masks, bf=bf, Ds=Ds, idx=idx):
                        if do_masks:
                            pb.wait(t, "a_msk")
                        else:
                            pb.wait(t, "a_exp", exp_hist[idx])
                        if kt == 0 and hp == 0:
                            pb.wait(t, "a_epi")
                        for hh2 in range(2):
                            hh = hp * 2 + hh2
                            bank = hh // 8
                            col = (hh % 8) * 128
                            first = (kt == 0 and hh % 4 == 0)
                            last = (kt == Ds - 1 and hh % 4 == 3)
                            for kc in range(4):
                                ins = t.matmul(psPO[:, bank, col:col + 65],
                                               ptsb[:, ps, hh2 * 512 + kc * 128:hh2 * 512 + (kc + 1) * 128],
                                               vslab[:, bf, kc, hh, :],
                                               start=(first and kc == 0), stop=(last and kc == 3))
                        v = pb.inc("a_pvmm", ins)
                        pv_hist.append(v)
                        kt_last_pv[kt] = v
                    pb.on("tensor", mmpv)

            for hh in range(16):
                def epi(v, s=s, hh=hh):
                    pb.wait(v, "a_pvmm")
                    if hh == 0 and s > 0:
                        pb.wait(v, "a_aost")
                    v.drain()
                    v.reciprocal(rcp[:, :], psPO[:, hh // 8, (hh % 8) * 128 + 64:(hh % 8) * 128 + 65])
                    v.drain()
                    pb.inc("a_epi", v.tensor_scalar(
                        attn_sb[:, hh * 64:(hh + 1) * 64],
                        psPO[:, hh // 8, (hh % 8) * 128:(hh % 8) * 128 + 64],
                        rcp[:, :], None, op0=ALU.mult))
                pb.on("vector", epi)

            def sta(sync, s=s):
                pb.wait(sync, "a_epi")
                pb.inc("a_aost", sync.dma_start(ao_dram[s * 128:(s + 1) * 128, :], attn_sb[:, :]), 16)
            pb.on("sync", sta)
            qk_slot_end[s] = pb.val("a_qkmm")

    tps = _ps(nc, es, "a_tps", [128, 2, 1024], BF16)
    psV = _ps(nc, es, "a_psV3", [128, 2, 512], F32)
    abf = _sb(nc, es, "a_abf", [128, 1024], BF16)
    aT = _sb(nc, es, "a_aT", [128, 8, 128], BF16)
    bobc = _sb(nc, es, "a_bobc", [128, 1024], F32)
    rt = _sb(nc, es, "a_rt", [128, 1024], F32)

    def ldwo(g):
        pb.wait(g, "a_qmm")
        pb.wait(g, "a_qst")
        pb.wait(g, "a_vst")
        pb.inc("a_wld2", g.dma_start(wob[:, :, :], wo_r[:, :, :]), 16)
        pb.inc("a_wld2", g.dma_start(bobc[:, :], io["bo_bc"][:, :]), 16)
    pb.on("gpsimd", ldwo)

    def ldstream(sync):
        for i in range(NT):
            pb.inc("a_sld", sync.dma_start(stream[:, i, :],
                                           io["x_tm"][i * 128:(i + 1) * 128, :]), 16)
    pb.on("sync", ldstream)

    evO_after = {}
    tre2_after = {}
    for i in range(NT):
        def lda(sync, i=i):
            pb.wait(sync, "a_aost")
            pb.wait(sync, "a_trm2")
            pb.inc("a_ald", sync.dma_start(abf[:, :], ao_dram[i * 128:(i + 1) * 128, :]), 16)
        pb.on("sync", lda)

        def trp(t, i=i):
            pb.wait(t, "a_ald")
            if i >= 2:
                pb.wait(t, "a_tre2", tre2_after[i - 2])
            for dc in range(7):
                t.transpose(tps[:, i % 2, dc * 128:(dc + 1) * 128],
                            abf[:, dc * 128:(dc + 1) * 128], identb[:, :])
            pb.inc("a_trm2", t.transpose(tps[:, i % 2, 7 * 128:8 * 128],
                                         abf[:, 7 * 128:8 * 128], identb[:, :]))
        pb.on("tensor", trp)

        def tev(v, i=i):
            pb.wait(v, "a_trm2")
            pb.wait(v, "a_omm")
            tre2_after[i] = pb.inc("a_tre2", v.tensor_copy(
                aT[:, :, :], tps[:, i % 2, :].rearrange("p (a b) -> p a b", a=8)))
        pb.on("vector", tev)

        def mmo(t, i=i):
            pb.wait(t, "a_wld2")
            pb.wait(t, "a_tre2", tre2_after[i])
            pb.wait(t, "a_evO", evO_after.get(i - 1, 0))
            for hf in range(2):
                for dc in range(8):
                    ins = t.matmul(psV[:, hf, :], aT[:, dc, :],
                                   wob[:, dc, hf * 512:(hf + 1) * 512],
                                   start=(dc == 0), stop=(dc == 7))
            pb.inc("a_omm", ins)
        pb.on("tensor", mmo)

        def evo(v, i=i):
            pb.wait(v, "a_omm")
            pb.wait(v, "a_sld")
            pb.wait(v, "ac")
            for hf in range(2):
                v.tensor_tensor(rt[:, hf * 512:(hf + 1) * 512], psV[:, hf, :],
                                bobc[:, hf * 512:(hf + 1) * 512], op=ALU.add)
            v.drain()
            for hf in range(2):
                ins = v.tensor_tensor(rt[:, hf * 512:(hf + 1) * 512],
                                      rt[:, hf * 512:(hf + 1) * 512],
                                      stream[:, i, hf * 512:(hf + 1) * 512], op=ALU.add)
            evO_after[i] = pb.inc("a_evO", ins)
        pb.on("vector", evo)

        emit_ln(pb, sc, rt[:, :], stream[:, i, :], ready=("a_evO", evO_after[i]))


def stage_attn(inp):
    d = {}
    perm = PERM
    Wq = np.asarray(inp["L0_attn_Wq"], np.float32)
    Wk = np.asarray(inp["L0_attn_Wk"], np.float32)
    Wv = np.asarray(inp["L0_attn_Wv"], np.float32)
    Wo = np.asarray(inp["L0_attn_Wo"], np.float32)
    bo = np.asarray(inp["L0_attn_bo"], np.float32)
    d["wqT"] = np.ascontiguousarray(Wq[:, perm].T)
    d["wkT"] = np.ascontiguousarray(Wk[:, perm].T)
    d["wvT"] = np.ascontiguousarray(Wv[:, perm].T)
    d["woT"] = np.ascontiguousarray(Wo.T)
    d["bo_bc"] = np.ascontiguousarray(np.broadcast_to(bo, (128, 1024)))
    import ml_dtypes
    theta = (10000.0 ** (-np.arange(0, D, 2, dtype=np.float64) / D))
    ang = np.arange(T, dtype=np.float64)[:, None] * theta[None, :]
    d["cos_k"] = np.ascontiguousarray(np.cos(ang).T.astype(ml_dtypes.bfloat16))
    d["sin_k"] = np.ascontiguousarray(np.sin(ang).T.astype(ml_dtypes.bfloat16))
    i_ = np.arange(128)[:, None, None]
    c_ = np.arange(4)[None, :, None]
    j_ = np.arange(128)[None, None, :]
    d["kabs"] = np.ascontiguousarray((c_ * 128 + i_ - j_).reshape(128, 512).astype(np.float32))
    d["ident"] = np.eye(128, dtype=np.float32)
    return d


def stage_attn_core(inp, core, shared):
    import ml_dtypes
    b, half = core // 2, core % 2
    x = np.asarray(inp["L0_attn_input"], np.float32)[b]
    own = _own_tokens(half)
    d = {}
    d["x_tm"] = np.ascontiguousarray(x[own])
    d["x_fmp"] = np.ascontiguousarray(x[:, PERM].T)
    d["x_fmq"] = np.ascontiguousarray(x[own][:, PERM].T)
    theta = (10000.0 ** (-np.arange(0, D, 2, dtype=np.float64) / D))
    angq = own[:, None].astype(np.float64) * theta[None, :]
    d["cos_q"] = np.ascontiguousarray(np.cos(angq).T.astype(ml_dtypes.bfloat16))
    d["sin_q"] = np.ascontiguousarray(np.sin(angq).T.astype(ml_dtypes.bfloat16))
    chunks = _chunks_of(half)
    qa = np.array([c * 128 for c in chunks])
    dtrue = np.array([(c * 128 + 128 + 511) // 512 for c in chunks])
    visb = np.zeros((128, NT, 8), np.float32)
    for s in range(NT):
        visb[:, s, dtrue[s]:] = -1e30
    d["visb"] = visb
    dgE = np.zeros((128, NT), np.float32)
    dgO = np.zeros((128, NT), np.float32)
    for s in range(NT):
        pe, po = POS_E[s], POS_O[s]
        dgE[:, s] = (qa[s] - pe * 512) if pe == dtrue[s] - 1 else 1e9
        dgO[:, s] = (qa[s] - po * 512) if po == dtrue[s] - 1 else 1e9
    d["dgapE"] = dgE
    d["dgapO"] = dgO
    return d


def build_attn_launch(shared):
    nc = _new_nc()
    hs = {k: nc.dram_tensor(k, list(v.shape), BF16 if v.dtype.kind == "V" or str(v.dtype) == "bfloat16" else F32,
                            kind="ExternalInput").ap()
          for k, v in shared.items()}
    percore_shapes = {"x_tm": (TOK, D), "x_fmp": (D, T), "x_fmq": (D, TOK),
                      "cos_q": (512, TOK), "sin_q": (512, TOK),
                      "visb": (128, NT, 8), "dgapE": (128, NT), "dgapO": (128, NT)}
    for k, shp in percore_shapes.items():
        dt = BF16 if k in ("cos_q", "sin_q") else F32
        hs[k] = nc.dram_tensor(k, list(shp), dt, kind="ExternalInput").ap()
    y_h = nc.dram_tensor("y_out", [TOK, D], F32, kind="ExternalOutput").ap()
    es = contextlib.ExitStack()
    with es:
        pb = PB(nc, es)
        stream = _sb(nc, es, "stream", [128, NT, 1024], F32)
        identb = _sb(nc, es, "identb", [128, 128], BF16)
        csts = emit_const_pool(pb, nc, es, [EPS, 1.0])
        lnsc = mk_ln_scratch(nc, es, "ln_", csts[EPS])

        def ldi(g):
            pb.inc("wld", g.dma_start(identb[:, :], hs["ident"][:, :]), 16)
        pb.on("gpsimd", ldi)

        io = dict(hs)
        io.update(stream=stream, identb=identb, lnsc=lnsc)
        emit_attn(pb, es, io)
        emit_stream_store(pb, stream, y_h, ["lnw"])
    return nc


def emit_moe(pb, es, io):
    nc = pb.nc
    stream = io["stream"]
    sc = io["lnsc"]
    identb = io["identb"]
    NE = 8

    with contextlib.ExitStack() as e1:
        xT = _sb(nc, e1, "m_xT", [128, 8, TOK], BF16)
        acc = _sb(nc, e1, "m_acc", [128, NT, 1024], BF16)
        hbuf = _sb(nc, e1, "m_hbuf", [128, 16, 512], BF16)
        xbf = _sb(nc, e1, "m_xbf", [128, 1024], BF16)
        w1b2 = _sb(nc, e1, "m_w1b2", [128, 4, 2, 8, 128], BF16)
        w2b = _sb(nc, e1, "m_w2b", [128, 16, 512], BF16)
        wgb = _sb(nc, e1, "m_wgb", [128, 8, 8], BF16)
        wgf = _sb(nc, e1, "m_wgf", [128, 8, 8], F32)
        wglo = _sb(nc, e1, "m_wglo", [128, 8, 8], BF16)
        xlo = _sb(nc, e1, "m_xlo", [128, 1024], BF16)
        xloT = _sb(nc, e1, "m_xloT", [128, 8, 128], BF16)
        logit = _sb(nc, e1, "m_logit", [128, NT, 8], F32)
        wcoef = _sb(nc, e1, "m_wcoef", [128, NT, 8], F32)
        mk1s = _sb(nc, e1, "m_mk1s", [128, NT, 8], F32)
        wcB = _sb(nc, e1, "m_wcB", [128, NT, 8], F32)
        pcnt = _sb(nc, e1, "m_pcnt", [128, NT, 8], F32)
        utri = _sb(nc, e1, "m_utri", [128, 128], F32)
        worder = _sb(nc, e1, "m_worder", [16, 2], F32)
        ones_c = _sb(nc, e1, "m_onesc", [128, 1], F32)
        ones_r = _sb(nc, e1, "m_onesr", [1, 128], F32)
        sums_sb = _sb(nc, e1, "m_sums", [1, 16, 8], F32)
        cum_sb = _sb(nc, e1, "m_cum", [1, 16, 8], F32)
        tot2 = _sb(nc, e1, "m_tot2", [1, 2, 8], F32)
        gath = _sb(nc, e1, "m_gath", [16, 8], F32)
        offs2 = _sb(nc, e1, "m_offs2", [2, 8], F32)
        offsr = _sb(nc, e1, "m_offsr", [1, 16], F32)
        rowfull = _sb(nc, e1, "m_rowfull", [1, 16, 8], F32)
        rowb = _sb(nc, e1, "m_rowb", [128, 16, 8], F32)
        posb = _sb(nc, e1, "m_posb", [128, 8], F32)
        kp = _sb(nc, e1, "m_kp", [128, 8], F32)
        ccin = nc.dram_tensor("m_ccin", [2, 8], F32, kind="Internal").ap()
        ccout = nc.dram_tensor("m_ccout", [16, 8], F32, kind="Internal").ap()
        t2a = _sb(nc, e1, "m_t2a", [128, 2, 512], BF16)
        v1c = _sb(nc, e1, "m_v1", [128, 1], F32)
        v2c = _sb(nc, e1, "m_v2", [128, 1], F32)
        g1c = _sb(nc, e1, "m_g1", [128, 1], F32)
        g2c = _sb(nc, e1, "m_g2", [128, 1], F32)
        mk1 = _sb(nc, e1, "m_mk1", [128, 8], F32)
        mk2 = _sb(nc, e1, "m_mk2", [128, 8], F32)
        rt = _sb(nc, e1, "m_rt", [128, 1024], F32)
        tps = _ps(nc, e1, "m_tps", [128, 2, 1024], BF16)
        psA = _ps(nc, e1, "m_psA", [128, 2, 512], F32)
        psB = _ps(nc, e1, "m_psB", [128, 2, 512], F32)
        psO = _ps(nc, e1, "m_psO", [128, 2, 512], F32)

        w1r = io["moe_w1ft"]
        w2r = io["moe_w2ft"]
        wgr = io["moe_wg"].rearrange("(dc p) f -> p dc f", p=128)

        def ld0(g):
            pb.inc("m_wld", g.dma_start(wgb[:, :, :], wgr[:, :, :]), 16)
            pb.inc("m_wld", g.dma_start(wgf[:, :, :], wgr[:, :, :]), 16)
            pb.inc("m_cst", g.memset(acc[:, :, :], 0.0))
        pb.on("gpsimd", ld0)

        def wglo_mk(v):
            pb.wait(v, "m_wld")
            pb.inc("m_wgl", v.tensor_tensor(wglo[:, :, :], wgf[:, :, :], wgb[:, :, :],
                                            op=ALU.subtract))
        pb.on("vector", wglo_mk)

        gev_after = {}
        for i in range(NT):
            def vc(v, i=i):
                pb.wait(v, "trm")
                v.tensor_copy(xbf[:, :], stream[:, i, :])
                v.drain()
                pb.inc("trc", v.tensor_tensor(xlo[:, :], stream[:, i, :], xbf[:, :],
                                              op=ALU.subtract))
            pb.on("vector", vc)

            def tt(t, i=i):
                pb.wait(t, "trc")
                pb.wait(t, "m_trel")
                pb.wait(t, "tre", max(0, pb.val("tre") - 1))
                for dc in range(8):
                    t.transpose(tps[:, i % 2, dc * 128:(dc + 1) * 128],
                                xbf[:, dc * 128:(dc + 1) * 128], identb[:, :])
                pb.inc("trm", t.transpose(tps[:, (i + 1) % 2, 0:128],
                                          xlo[:, 0:128], identb[:, :]))
            pb.on("tensor", tt)

            def ve(v, i=i):
                pb.wait(v, "trm")
                pb.inc("tre", v.tensor_copy(
                    xT[:, :, i * 128:(i + 1) * 128],
                    tps[:, i % 2, :].rearrange("p (a b) -> p a b", a=8)))
            pb.on("vector", ve)

            def tt2(t, i=i):
                pb.wait(t, "tre")
                for dc in range(1, 8):
                    t.transpose(tps[:, (i + 1) % 2, dc * 128:(dc + 1) * 128],
                                xlo[:, dc * 128:(dc + 1) * 128], identb[:, :])
                pb.inc("m_trml", t.transpose(tps[:, (i + 1) % 2, 0:128],
                                             xlo[:, 0:128], identb[:, :]))
            pb.on("tensor", tt2)

            def ve2(v, i=i):
                pb.wait(v, "m_trml")
                pb.wait(v, "m_gmm")
                pb.inc("m_trel", v.tensor_copy(
                    xloT[:, :, :], tps[:, (i + 1) % 2, :].rearrange("p (a b) -> p a b", a=8)))
            pb.on("vector", ve2)

            def mmg(t, i=i):
                pb.wait(t, "m_wld")
                pb.wait(t, "m_wgl")
                pb.wait(t, "m_trel")
                if i >= 1:
                    pb.wait(t, "m_gev", gev_after[i - 1])
                for dc in range(8):
                    t.matmul(psA[:, i % 2, 0:8], xT[:, dc, i * 128:(i + 1) * 128],
                             wgb[:, dc, :], start=(dc == 0), stop=False)
                for dc in range(8):
                    t.matmul(psA[:, i % 2, 0:8], xT[:, dc, i * 128:(i + 1) * 128],
                             wglo[:, dc, :], start=False, stop=False)
                for dc in range(8):
                    ins = t.matmul(psA[:, i % 2, 0:8], xloT[:, dc, :],
                                   wgb[:, dc, :], start=False, stop=(dc == 7))
                pb.inc("m_gmm", ins)
            pb.on("tensor", mmg)

            def evg(v, i=i):
                pb.wait(v, "m_gmm")
                gev_after[i] = pb.inc("m_gev", v.tensor_copy(logit[:, i, :], psA[:, i % 2, 0:8]))
            pb.on("vector", evg)

        for i in range(NT):
            def route(v, i=i):
                pb.wait(v, "m_gev")
                lg = logit[:, i, :]
                v.drain()
                v.tensor_reduce(v1c[:, :], lg, axis=AX, op=ALU.max)
                v.drain()
                v.tensor_scalar(mk1s[:, i, :], lg, v1c[:, :], None, op0=ALU.is_ge)
                v.drain()
                v.scalar_tensor_tensor(mk2[:, :], mk1s[:, i, :], -1e30, lg, op0=ALU.mult, op1=ALU.add)
                v.drain()
                v.tensor_reduce(v2c[:, :], mk2[:, :], axis=AX, op=ALU.max)
                v.drain()
                v.tensor_scalar(mk2[:, :], mk2[:, :], v2c[:, :], None, op0=ALU.is_ge)
                v.drain()
                pb.inc("m_rt1", v.tensor_tensor(v1c[:, :], v1c[:, :], v2c[:, :], op=ALU.subtract))
            pb.on("vector", route)

            def sg(sx, i=i):
                pb.wait(sx, "m_rt1")
                pb.inc("m_sg", sx.activation(g1c[:, :], v1c[:, :], AF.Exp, scale=-1.0))
            pb.on("scalar", sg)

            def route2(v, i=i):
                pb.wait(v, "m_sg")
                v.tensor_scalar(g1c[:, :], g1c[:, :], 1.0, None, op0=ALU.add)
                v.drain()
                v.reciprocal(g1c[:, :], g1c[:, :])
                v.drain()
                v.tensor_scalar(g2c[:, :], g1c[:, :], -1.0, 1.0, op0=ALU.mult, op1=ALU.add)
                v.drain()
                v.tensor_scalar(wcoef[:, i, :], mk1s[:, i, :], g1c[:, :], None, op0=ALU.mult)
                v.tensor_scalar(wcB[:, i, :], mk2[:, :], g2c[:, :], None, op0=ALU.mult)
                v.drain()
                pb.inc("m_rt2", v.tensor_tensor(pcnt[:, i, :], mk1s[:, i, :], mk2[:, :], op=ALU.add))
            pb.on("vector", route2)

        CAP = 5120.0

        def ldcap(g):
            pb.inc("m_capld", g.dma_start(utri[:, :], io["utri"][:, :]), 16)
            pb.inc("m_capld", g.dma_start(worder[:, :], io["worder"][:, :]), 16)
            g.memset(ones_c[:, :], 1.0)
            pb.inc("m_capc", g.memset(ones_r[:, :], 1.0))
        pb.on("gpsimd", ldcap)

        def mmsum(t):
            pb.wait(t, "m_rt2")
            pb.wait(t, "m_capc")
            pb.wait(t, "m_capld")
            for si in range(NT):
                ins = t.matmul(psB[0:1, 0, si * 8:(si + 1) * 8], ones_c[:, :], pcnt[:, si, :],
                               start=(si == 0), stop=(si == NT - 1))
            pb.inc("m_summ", ins)
        pb.on("tensor", mmsum)

        def evsum(v):
            pb.wait(v, "m_summ")
            v.tensor_copy(sums_sb[0:1, :, :], psB[0:1, 0, 0:128].rearrange("p (a b) -> p a b", a=16))
            v.drain()
            v.memset(cum_sb[0:1, 0, :], 0.0)
            v.memset(cum_sb[0:1, 8, :], 0.0)
            v.drain()
            for u in range(2):
                for si in range(1, 8):
                    sl = u * 8 + si
                    v.tensor_tensor(cum_sb[0:1, sl, :], cum_sb[0:1, sl - 1, :],
                                    sums_sb[0:1, sl - 1, :], op=ALU.add)
                    v.drain()
            pb.inc("m_cum", v.tensor_tensor(
                tot2[0:1, :, :], cum_sb[0:1, 7::8, :], sums_sb[0:1, 7::8, :], op=ALU.add))
        pb.on("vector", evsum)

        def ccst(sync):
            pb.wait(sync, "m_cum")
            pb.inc("m_ccst", sync.dma_start(ccin, tot2[0:1, :, :].rearrange("p a b -> p (a b)")), 16)
        pb.on("sync", ccst)

        def ccgo(g):
            pb.wait(g, "m_ccst")
            g.collective_compute("AllGather", ALU.bypass,
                                 replica_groups=[list(range(NCORE))],
                                 ins=[ccin], outs=[ccout]).then_inc(pb.semh("m_cc")[0], 1)
            pb.semh("m_cc")[1] += 1
        pb.on("gpsimd", ccgo)

        def ccld(sync):
            pb.wait(sync, "m_cc")
            pb.inc("m_ccld", sync.dma_start(gath[:, :], ccout), 16)
        pb.on("sync", ccld)

        def mmoff(t):
            pb.wait(t, "m_ccld")
            pb.inc("m_offm", t.matmul(psA[0:2, 0, 0:8], worder[:, :], gath[:, :],
                                      start=True, stop=True))
        pb.on("tensor", mmoff)

        def evoff(v):
            pb.wait(v, "m_offm")
            pb.inc("m_offev", v.tensor_copy(offs2[:, :], psA[0:2, 0, 0:8]))
        pb.on("vector", evoff)

        def offdma(sync):
            pb.wait(sync, "m_offev")
            pb.inc("m_offfl", sync.dma_start(offsr[:, :], offs2[:, :]), 16)
        pb.on("sync", offdma)

        def mkrow(v):
            pb.wait(v, "m_offfl")
            orow = bass.AP(offsr, 0, [[16, 1], [8, 2], [0, 8], [1, 8]])
            pb.inc("m_rowf", v.tensor_tensor(
                rowfull[0:1, :, :].rearrange("p (u s) e -> p u s e", u=2),
                cum_sb[0:1, :, :].rearrange("p (u s) e -> p u s e", u=2),
                orow, op=ALU.add))
        pb.on("vector", mkrow)

        def mmrow(t):
            pb.wait(t, "m_rowf")
            pb.inc("m_rbm", t.matmul(psA[:, 1, 0:128], ones_r[:, :],
                                     rowfull[0:1, :, :].rearrange("p a b -> p (a b)"),
                                     start=True, stop=True))
        pb.on("tensor", mmrow)

        def evrow(v):
            pb.wait(v, "m_rbm")
            pb.inc("m_rbev", v.tensor_copy(rowb[:, :, :],
                                           psA[:, 1, 0:128].rearrange("p (a b) -> p a b", a=16)))
        pb.on("vector", evrow)

        pos_after = {}
        for si in range(NT):
            def mmpos(t, si=si):
                pb.wait(t, "m_rbev")
                if si >= 2:
                    pb.wait(t, "m_wcap", pos_after[si - 2])
                ins = t.matmul(psB[:, si % 2, 0:8], utri[:, :], pcnt[:, si, :],
                               start=True, stop=True)
                pb.inc("m_posm", ins)
            pb.on("tensor", mmpos)

            def evpos(v, si=si):
                pb.wait(v, "m_posm")
                v.tensor_tensor(posb[:, :], psB[:, si % 2, 0:8], rowb[:, si, :], op=ALU.add)
                v.drain()
                v.tensor_scalar(kp[:, :], posb[:, :], CAP, None, op0=ALU.is_lt)
                v.drain()
                v.tensor_tensor(wcoef[:, si, :], wcoef[:, si, :], kp[:, :], op=ALU.mult)
                v.drain()
                v.tensor_tensor(posb[:, :], posb[:, :], mk1s[:, si, :], op=ALU.add)
                v.drain()
                v.tensor_scalar(kp[:, :], posb[:, :], CAP, None, op0=ALU.is_lt)
                v.drain()
                v.tensor_tensor(kp[:, :], wcB[:, si, :], kp[:, :], op=ALU.mult)
                v.drain()
                pos_after[si] = pb.inc("m_wcap", v.tensor_tensor(
                    wcoef[:, si, :], wcoef[:, si, :], kp[:, :], op=ALU.add))
            pb.on("vector", evpos)

        NSL = 4
        for e in range(NE):
            for sl in range(NSL):
                mmh_after, evh_after = {}, {}
                for fi in range(16):
                    bf = fi % 4
                    def ldw1(g, e=e, sl=sl, fi=fi, bf=bf):
                        if fi >= 4:
                            pb.wait(g, "m_hmm", mmh_after[fi - 4])
                        else:
                            pb.wait(g, "m_hmm")
                        pb.inc("m_w1ld", g.dma_start(
                            w1b2[:, bf], w1r[e, fi].rearrange("j p dc f -> p j dc f")), 16)
                    pb.on("gpsimd", ldw1)

                    def mmh(t, e=e, sl=sl, fi=fi, bf=bf):
                        pb.wait(t, "m_w1ld")
                        pb.wait(t, "tre")
                        if fi >= 2:
                            pb.wait(t, "m_hev", evh_after[fi - 2])
                        else:
                            pb.wait(t, "m_hev")
                            pb.wait(t, "m_gev")
                            pb.wait(t, "m_wcap", pb.val("m_wcap"))
                        for dc in range(8):
                            t.matmul(psA[:, fi % 2], w1b2[:, bf, 0, dc, :], xT[:, dc, sl * 512:(sl + 1) * 512],
                                     start=(dc == 0), stop=(dc == 7))
                        for dc in range(8):
                            ins = t.matmul(psB[:, fi % 2], w1b2[:, bf, 1, dc, :], xT[:, dc, sl * 512:(sl + 1) * 512],
                                           start=(dc == 0), stop=(dc == 7))
                        mmh_after[fi] = pb.inc("m_hmm", ins)
                    pb.on("tensor", mmh)

                    def evh1(sx, e=e, fi=fi):
                        pb.wait(sx, "m_hmm", mmh_after[fi])
                        if fi >= 2:
                            pb.wait(sx, "m_hev", evh_after[fi - 2])
                        pb.inc("m_hact", sx.activation(t2a[:, fi % 2, :], psA[:, fi % 2], AF.Gelu))
                    pb.on("scalar", evh1)

                    def evh2(v, e=e, fi=fi):
                        pb.wait(v, "m_hact")
                        pb.wait(v, "m_omm")
                        evh_after[fi] = pb.inc("m_hev", v.tensor_tensor(
                            hbuf[:, fi, :], t2a[:, fi % 2, :], psB[:, fi % 2], op=ALU.mult))
                    pb.on("vector", evh2)

                for hf in range(2):
                    def ldw2(g, e=e, sl=sl, hf=hf):
                        pb.wait(g, "m_omm")
                        pb.inc("m_w2ld", g.dma_start(w2b[:, :, :], w2r[e, hf]), 16)
                    pb.on("gpsimd", ldw2)
                    for tt in range(4):
                        def mmo(t, e=e, sl=sl, tt=tt, hf=hf):
                            pb.wait(t, "m_w2ld")
                            pb.wait(t, "m_hev")
                            pb.wait(t, "m_oev", max(0, pb.val("m_oev") - 1))
                            for ic in range(16):
                                ins = t.matmul(psO[:, tt % 2],
                                               hbuf[:, ic, tt * 128:(tt + 1) * 128],
                                               w2b[:, ic, :],
                                               start=(ic == 0), stop=(ic == 15))
                            pb.inc("m_omm", ins)
                        pb.on("tensor", mmo)

                        def evo(v, e=e, sl=sl, tt=tt, hf=hf):
                            pb.wait(v, "m_omm")
                            pb.wait(v, "m_wcap")
                            pb.wait(v, "m_cst")
                            ti = sl * 4 + tt
                            pb.inc("m_oev", v.scalar_tensor_tensor(
                                acc[:, ti, hf * 512:(hf + 1) * 512], psO[:, tt % 2],
                                wcoef[:, ti, e:e + 1],
                                acc[:, ti, hf * 512:(hf + 1) * 512], op0=ALU.mult, op1=ALU.add))
                        pb.on("vector", evo)

        for i in range(NT):
            def evr(v, i=i):
                pb.wait(v, "m_oev")
                pb.wait(v, "ac")
                pb.inc("m_res", v.tensor_tensor(rt[:, :], acc[:, i, :], stream[:, i, :], op=ALU.add))
            pb.on("vector", evr)
            emit_ln(pb, sc, rt[:, :], stream[:, i, :], ready=("m_res", pb.val("m_res")))


def stage_moe(inp):
    d = {}
    d["moe_wg"] = _t(inp["L0_ffn_Wg"])
    w1 = np.asarray(inp["L0_ffn_W1"], np.float32)
    w1t_ = w1.reshape(8, 8, 128, 32, 128).transpose(0, 3, 2, 1, 4)
    d["moe_w1ft"] = np.ascontiguousarray(
        w1t_.reshape(8, 2, 16, 128, 8, 128).transpose(0, 2, 1, 3, 4, 5))
    w2 = np.asarray(inp["L0_ffn_W2"], np.float32)
    d["moe_w2ft"] = np.ascontiguousarray(
        w2.reshape(8, 16, 128, 2, 512).transpose(0, 3, 2, 1, 4))
    d["ident"] = np.eye(128, dtype=np.float32)
    d["utri"] = np.triu(np.ones((128, 128), np.float32), 1)
    return d


def _run_seq_pos(core, run):
    b, half = core // 2, core % 2
    if half == 0:
        return 4 * b + (0 if run == 0 else 3)
    return 4 * b + (1 if run == 0 else 2)


def stage_moe_core(core):
    w = np.zeros((16, 2), np.float32)
    for rc in range(NCORE):
        for ru in range(2):
            j = rc * 2 + ru
            for myu in range(2):
                if _run_seq_pos(rc, ru) < _run_seq_pos(core, myu):
                    w[j, myu] = 1.0
    return {"worder": w}


def build_moe_launch(staged):
    nc = _new_nc()
    hs = {k: _dram_in(nc, k, v) for k, v in staged.items()}
    hs["worder"] = nc.dram_tensor("worder", [16, 2], F32, kind="ExternalInput").ap()
    x_h = nc.dram_tensor("x_in", [TOK, D], F32, kind="ExternalInput").ap()
    y_h = nc.dram_tensor("y_out", [TOK, D], F32, kind="ExternalOutput").ap()
    es = contextlib.ExitStack()
    with es:
        pb = PB(nc, es)
        stream = _sb(nc, es, "stream", [128, NT, 1024], F32)
        identb = _sb(nc, es, "identb", [128, 128], BF16)
        csts = emit_const_pool(pb, nc, es, [EPS, 1.0])
        lnsc = mk_ln_scratch(nc, es, "ln_", csts[EPS])

        def ldi(g):
            pb.inc("wld", g.dma_start(identb[:, :], hs["ident"][:, :]), 16)
        pb.on("gpsimd", ldi)

        emit_stream_load(pb, stream, x_h)
        io = dict(hs)
        io.update(stream=stream, identb=identb, lnsc=lnsc)
        emit_moe(pb, es, io)
        emit_stream_store(pb, stream, y_h, ["lnw"])
    return nc


def emit_mamba(pb, es, io):
    nc = pb.nc
    stream = io["stream"]
    sc = io["lnsc"]
    identb = io["identb"]

    z_dram = nc.dram_tensor("c_zdram", [TOK, 2048], BF16, kind="Internal").ap()
    xbcr_dram = nc.dram_tensor("c_xbcrdram", [128, 18, 2, 1024], BF16, kind="Internal").ap()
    y_dram = nc.dram_tensor("c_ydram", [TOK, 2048], F32, kind="Internal").ap()
    hl_in = nc.dram_tensor("c_hlin", [128, 108], BF16, kind="Internal").ap()
    hl_out = nc.dram_tensor("c_hlout", [256, 108], BF16, kind="Internal").ap()
    st_in = nc.dram_tensor("c_stin", [257, 2048], F32, kind="Internal").ap()
    st_out = nc.dram_tensor("c_stout", [514, 2048], F32, kind="Internal").ap()
    io["_dbg"] = {"y": y_dram, "z": z_dram, "x": xbcr_dram}

    ipr = io["c_ipT"].rearrange("(dc p) f -> p dc f", p=128)
    opr = io["c_opT"].rearrange("(ic p) f -> p ic f", p=128)

    dtr = _sb(nc, es, "c_dtr", [128, NT, 32], F32)
    acum_all = _sb(nc, es, "c_acumall", [128, NT, 32], F32)
    cdrow_all = _sb(nc, es, "c_cdrow", [1, NT, 32], F32)
    Call = _sb(nc, es, "c_Call", [128, NT, 128], BF16)
    SiniA = _sb(nc, es, "c_SiniA", [128, 2048], F32)
    SiniB = _sb(nc, es, "c_SiniB", [128, 2048], F32)
    identf = _sb(nc, es, "c_identf", [128, 128], F32)
    ones_r = _sb(nc, es, "c_onesr", [1, 128], F32)
    rsum = _sb(nc, es, "c_rsum", [1, 32], F32)
    rrow = _sb(nc, es, "c_rrow", [1, 32], F32)
    crho = _sb(nc, es, "c_crho", [128, 1], F32)
    crho2 = _sb(nc, es, "c_crho2", [1, 1], F32)
    cw1 = _sb(nc, es, "c_cw1", [128, 1], F32)
    cdsel = _sb(nc, es, "c_cdsel", [1, 1], F32)
    cdseli = _sb(nc, es, "c_cdseli", [1, 1], F32)
    cselA = _sb(nc, es, "c_cselA", [128, 1], F32)
    cselE = _sb(nc, es, "c_cselE", [128, 1], F32)
    cselO = _sb(nc, es, "c_cselO", [128, 1], F32)
    haloA = _sb(nc, es, "c_haloA", [128, 18, 3], BF16)
    haloB = _sb(nc, es, "c_haloB", [128, 18, 3], BF16)
    tps = _ps(nc, es, "c_tps", [128, 2, 1024], BF16)
    psR = _ps(nc, es, "c_psR", [128, 1024], F32)
    psY = _ps(nc, es, "c_psY", [128, 512], F32)
    psYo = _ps(nc, es, "c_psYo", [128, 512], F32)
    psU = _ps(nc, es, "c_psU", [128, 512], F32)

    def ld0(g):
        pb.inc("c_wld", g.dma_start(identf[:, :], io["ident"][:, :]), 16)
        pb.inc("c_wld", g.dma_start(crho[:, :], io["c_rho"][:, :]), 16)
        pb.inc("c_wld", g.dma_start(crho2[:, :], io["c_rho2"][:, :]), 16)
        pb.inc("c_wld", g.dma_start(cw1[:, :], io["c_w1"][:, :]), 16)
        pb.inc("c_wld", g.dma_start(cdsel[:, :], io["c_dsel"][:, :]), 16)
        pb.inc("c_wld", g.dma_start(cdseli[:, :], io["c_dseli"][:, :]), 16)
        pb.inc("c_wld", g.dma_start(cselA[:, :], io["c_selA"][:, :]), 16)
        pb.inc("c_wld", g.dma_start(cselE[:, :], io["c_selE"][:, :]), 16)
        pb.inc("c_wld", g.dma_start(cselO[:, :], io["c_selO"][:, :]), 16)
        pb.inc("c_cst", g.memset(ones_r[:, :], 1.0))
    pb.on("gpsimd", ld0)

    with contextlib.ExitStack() as e1:
        xn = _sb(nc, e1, "c_xn", [128, NT, 1024], BF16)
        nwb = _sb(nc, e1, "c_nwb", [128, 1024], F32)
        nbb = _sb(nc, e1, "c_nbb", [128, 1024], F32)
        ipb = _sb(nc, e1, "c_ipb", [128, 8, 2048], BF16)
        xnT = _sb(nc, e1, "c_xnT", [128, 8, 512], BF16)
        xbf1 = _sb(nc, e1, "c_xbf1", [128, 1024], BF16)
        zst = _sb(nc, e1, "c_zst", [128, 2048], BF16)
        xbcst = _sb(nc, e1, "c_xbcst", [128, 18, 512], BF16)

        def ldnw(g):
            pb.inc("c_wld", g.dma_start(nwb[:, :], io["c_normw"][:, :]), 16)
            pb.inc("c_wld", g.dma_start(nbb[:, :], io["c_normb"][:, :]), 16)
        pb.on("gpsimd", ldnw)

        for i in range(NT):
            def wv(v, i=i):
                pb.wait(v, "c_wld")
                pb.inc("c_lnx", v.tensor_copy(sc["lnt"][:, 0:1], stream[:, i, 0:1]))
            pb.on("vector", wv)
            emit_ln(pb, sc, stream[:, i, :], xn[:, i, :],
                    ready=("c_lnx", pb.val("c_lnx")), gain_bc=nwb[:, :], bias_bc=nbb[:, :])

        for sl in range(4):
            tiles = list(range(sl * 4, sl * 4 + 4))
            for pos, i in enumerate(tiles):
                def vc(v, i=i):
                    pb.wait(v, "lnw")
                    pb.wait(v, "c_trm")
                    pb.inc("c_trc", v.tensor_copy(xbf1[:, :], xn[:, i, :]))
                pb.on("vector", vc)

                def tt(t, pos=pos):
                    pb.wait(t, "c_trc")
                    pb.wait(t, "c_tre", max(0, pb.val("c_tre") - 1))
                    for dc in range(7):
                        t.transpose(tps[:, pos % 2, dc * 128:(dc + 1) * 128],
                                    xbf1[:, dc * 128:(dc + 1) * 128], identb[:, :])
                    pb.inc("c_trm", t.transpose(tps[:, pos % 2, 896:1024],
                                                xbf1[:, 896:1024], identb[:, :]))
                pb.on("tensor", tt)

                def ve(v, pos=pos):
                    pb.wait(v, "c_trm")
                    pb.inc("c_tre", v.tensor_copy(
                        xnT[:, :, pos * 128:(pos + 1) * 128],
                        tps[:, pos % 2, :].rearrange("p (a b) -> p a b", a=8)))
                pb.on("vector", ve)

            def ldz(g, sl=sl):
                pb.wait(g, "c_imm")
                pb.inc("c_ipld", g.dma_start(ipb[:, :, 0:2048], ipr[:, :, 0:2048]), 16)
            pb.on("gpsimd", ldz)
            for tt_ in range(4):
                for cg in range(4):
                    def mmz(t, sl=sl, tt_=tt_, cg=cg):
                        pb.wait(t, "c_ipld")
                        pb.wait(t, "c_tre")
                        pb.wait(t, "c_zev", max(0, pb.val("c_zev") - 1))
                        for dc in range(8):
                            ins = t.matmul(psR[:, (cg % 2) * 512:(cg % 2) * 512 + 512],
                                           xnT[:, dc, tt_ * 128:(tt_ + 1) * 128],
                                           ipb[:, dc, cg * 512:(cg + 1) * 512],
                                           start=(dc == 0), stop=(dc == 7))
                        pb.inc("c_imm", ins)
                    pb.on("tensor", mmz)

                    def evz(v, sl=sl, tt_=tt_, cg=cg):
                        pb.wait(v, "c_imm")
                        if cg == 0:
                            pb.wait(v, "c_zst")
                        pb.inc("c_zev", v.tensor_copy(zst[:, cg * 512:(cg + 1) * 512],
                                                      psR[:, (cg % 2) * 512:(cg % 2) * 512 + 512]))
                    pb.on("vector", evz)

                def stz(sync, sl=sl, tt_=tt_):
                    pb.wait(sync, "c_zev")
                    i = sl * 4 + tt_
                    pb.inc("c_zst", sync.dma_start(z_dram[i * 128:(i + 1) * 128, :], zst[:, :]), 16)
                pb.on("sync", stz)

            def ldxbc(g, sl=sl):
                pb.wait(g, "c_imm")
                pb.inc("c_ipld", g.dma_start(ipb[:, :, 0:2048], ipr[:, :, 2048:4096]), 16)
            pb.on("gpsimd", ldxbc)
            for fc in range(16):
                def mmx(t, sl=sl, fc=fc):
                    pb.wait(t, "c_ipld")
                    pb.wait(t, "c_xev", max(0, pb.val("c_xev") - 1))
                    for dc in range(8):
                        ins = t.matmul(psR[:, (fc % 2) * 512:(fc % 2) * 512 + 512],
                                       ipb[:, dc, fc * 128:(fc + 1) * 128],
                                       xnT[:, dc, :], start=(dc == 0), stop=(dc == 7))
                    pb.inc("c_imm", ins)
                pb.on("tensor", mmx)

                def evx(v, sl=sl, fc=fc):
                    pb.wait(v, "c_imm")
                    if fc == 0:
                        pb.wait(v, "c_xst")
                    pb.inc("c_xev", v.tensor_copy(xbcst[:, fc, :],
                                                  psR[:, (fc % 2) * 512:(fc % 2) * 512 + 512]))
                pb.on("vector", evx)

            def ldxbc2(g, sl=sl):
                pb.wait(g, "c_imm")
                pb.inc("c_ipld", g.dma_start(ipb[:, :, 0:256], ipr[:, :, 4096:4352]), 16)
            pb.on("gpsimd", ldxbc2)
            for fc in range(16, 18):
                def mmx(t, sl=sl, fc=fc):
                    pb.wait(t, "c_ipld")
                    pb.wait(t, "c_xev", max(0, pb.val("c_xev") - 1))
                    for dc in range(8):
                        ins = t.matmul(psR[:, (fc % 2) * 512:(fc % 2) * 512 + 512],
                                       ipb[:, dc, (fc - 16) * 128:(fc - 15) * 128],
                                       xnT[:, dc, :], start=(dc == 0), stop=(dc == 7))
                    pb.inc("c_imm", ins)
                pb.on("tensor", mmx)

                def evx(v, sl=sl, fc=fc):
                    pb.wait(v, "c_imm")
                    pb.inc("c_xev", v.tensor_copy(xbcst[:, fc, :],
                                                  psR[:, (fc % 2) * 512:(fc % 2) * 512 + 512]))
                pb.on("vector", evx)

            def stx(sync, sl=sl):
                pb.wait(sync, "c_xev")
                pb.inc("c_xst", sync.dma_start(
                    xbcr_dram[:, :, sl // 2, (sl % 2) * 512:(sl % 2) * 512 + 512],
                    xbcst[:, :, :]), 16)
            pb.on("sync", stx)

            def lddt(g, sl=sl):
                pb.wait(g, "c_imm")
                pb.inc("c_ipld", g.dma_start(ipb[:, :, 0:32], ipr[:, :, 4352:4384]), 16)
            pb.on("gpsimd", lddt)
            for tt_ in range(4):
                def mmd(t, sl=sl, tt_=tt_):
                    pb.wait(t, "c_ipld")
                    pb.wait(t, "c_dev", max(0, pb.val("c_dev") - 1))
                    for dc in range(8):
                        ins = t.matmul(psR[:, (tt_ % 2) * 512:(tt_ % 2) * 512 + 32],
                                       xnT[:, dc, tt_ * 128:(tt_ + 1) * 128],
                                       ipb[:, dc, 0:32], start=(dc == 0), stop=(dc == 7))
                    pb.inc("c_imm", ins)
                pb.on("tensor", mmd)

                def evd(v, sl=sl, tt_=tt_):
                    pb.wait(v, "c_imm")
                    pb.inc("c_dev", v.tensor_copy(dtr[:, sl * 4 + tt_, :],
                                                  psR[:, (tt_ % 2) * 512:(tt_ % 2) * 512 + 32]))
                pb.on("vector", evd)

    def hput(sync):
        pb.wait(sync, "c_xst")
        pb.inc("c_hst", sync.dma_start(
            hl_in.rearrange("p (k f) -> p k f", k=2)[:, 0, :].rearrange("p (fc c) -> p fc c", fc=18),
            xbcr_dram[:, :, 0, 1021:1024]), 16)
        pb.inc("c_hst", sync.dma_start(
            hl_in.rearrange("p (k f) -> p k f", k=2)[:, 1, :].rearrange("p (fc c) -> p fc c", fc=18),
            xbcr_dram[:, :, 1, 1021:1024]), 16)
    pb.on("sync", hput)

    def hcc(g):
        pb.wait(g, "c_hst")
        g.collective_compute("AllGather", ALU.bypass,
                             replica_groups=[[0, 1], [2, 3], [4, 5], [6, 7]],
                             ins=[hl_in], outs=[hl_out]).then_inc(pb.semh("c_hcc")[0], 1)
        pb.semh("c_hcc")[1] += 1
    pb.on("gpsimd", hcc)

    with contextlib.ExitStack() as e2:
        heven = _sb(nc, e2, "c_heven", [128, 2, 18, 3], BF16)
        hodd = _sb(nc, e2, "c_hodd", [128, 2, 18, 3], BF16)

        def hget(sync):
            pb.wait(sync, "c_hcc")
            pb.inc("c_hld", sync.dma_start(
                heven[:, :, :, :].rearrange("p k fc c -> p (k fc c)"), hl_out[0:128, :]), 16)
            pb.inc("c_hld", sync.dma_start(
                hodd[:, :, :, :].rearrange("p k fc c -> p (k fc c)"), hl_out[128:256, :]), 16)
        pb.on("sync", hget)

        def hblend(v):
            pb.wait(v, "c_hld")
            pb.wait(v, "c_wld")
            v.tensor_scalar(haloA[:, :, :], heven[:, 0], cselA[:, :], None, op0=ALU.mult)
            v.drain()
            v.tensor_scalar(haloB[:, :, :], hodd[:, 1], cselE[:, :], None, op0=ALU.mult)
            v.drain()
            pb.inc("c_halo", v.scalar_tensor_tensor(
                haloB[:, :, :].rearrange("p a b -> p (a b)"),
                hodd[:, 0].rearrange("p a b -> p (a b)"),
                cselO[:, :], haloB[:, :, :].rearrange("p a b -> p (a b)"),
                op0=ALU.mult, op1=ALU.add))
        pb.on("vector", hblend)

    with contextlib.ExitStack() as e3:
        xconv = _sb(nc, e3, "c_xconv", [128, 18, 515], BF16)
        xbcp = _sb(nc, e3, "c_xbcp", [128, 18, 512], BF16)
        convw = _sb(nc, e3, "c_convw_s", [128, 18, 4], F32)
        convb = _sb(nc, e3, "c_convb_s", [128, 18], F32)
        acc1 = _sb(nc, e3, "c_acc1", [128, 512], F32)
        acc2 = _sb(nc, e3, "c_acc2", [128, 512], F32)
        dtb = _sb(nc, e3, "c_dtb_s", [128, 32], F32)
        Albc = _sb(nc, e3, "c_Albc", [128, 32], F32)
        Abc = _sb(nc, e3, "c_Abc", [128, 32], F32)
        Dbc = _sb(nc, e3, "c_Dbc", [128, 32], F32)
        dt_tm = _sb(nc, e3, "c_dttm", [128, 32], F32)
        sp1 = _sb(nc, e3, "c_sp1", [128, 32], F32)
        sp2 = _sb(nc, e3, "c_sp2", [128, 32], F32)
        a_tm = _sb(nc, e3, "c_atm", [128, 32], F32)
        acum_fm = _sb(nc, e3, "c_acumfm", [32, 128], F32)
        acum_flat = _sb(nc, e3, "c_acumflat", [1, 4096], F32)
        ea_tm = _sb(nc, e3, "c_eatm", [128, 32], F32)
        dd_tm = _sb(nc, e3, "c_ddtm", [128, 32], F32)
        decay_tm = _sb(nc, e3, "c_decaytm", [128, 32], F32)
        dxr = _sb(nc, e3, "c_dxr", [1, 32], F32)
        dbc = _sb(nc, e3, "c_dbc", [128, 32], F32)
        cdxr = _sb(nc, e3, "c_cdxr", [1, 32], F32)
        cdbc = _sb(nc, e3, "c_cdbc", [128, 32], F32)
        S = _sb(nc, e3, "c_S", [128, 2048], F32)
        Gm = _sb(nc, e3, "c_Gm", [128, 128], BF16)
        Btm = _sb(nc, e3, "c_Btm", [128, 128], BF16)
        xs_tm = _sb(nc, e3, "c_xstm", [128, 2048], BF16)
        X1 = _sb(nc, e3, "c_X1", [128, 2048], BF16)
        X2 = _sb(nc, e3, "c_X2", [128, 2048], BF16)
        D1 = _sb(nc, e3, "c_D1", [128, 1024], F32)
        Ebuf = _sb(nc, e3, "c_Ebuf", [128, 1024], BF16)
        Mq = _sb(nc, e3, "c_Mq", [128, 1024], BF16)
        ytile = _sb(nc, e3, "c_ytile", [128, 2048], F32)
        C32f = _sb(nc, e3, "c_C32f", [128, 128], F32)
        ytmp = _sb(nc, e3, "c_ytmp", [128, 512], F32)
        ytmp2 = _sb(nc, e3, "c_ytmp2", [128, 512], F32)

        def ldc(g):
            pb.inc("c_s0", g.memset(S[:, :], 0.0))
            pb.inc("c_wld", g.dma_start(convw[:, :, :], io["c_convw"][:, :, :]), 16)
            pb.inc("c_wld", g.dma_start(convb[:, :], io["c_convb"][:, :]), 16)
            pb.inc("c_wld", g.dma_start(dtb[:, :], io["c_dtb"][:, :]), 16)
            pb.inc("c_wld", g.dma_start(Albc[:, :], io["c_Alog"][:, :]), 16)
            pb.inc("c_wld", g.dma_start(Dbc[:, :], io["c_D"][:, :]), 16)
        pb.on("gpsimd", ldc)

        def mkA(sx):
            pb.wait(sx, "c_wld")
            pb.inc("c_ac0", sx.activation(Abc[:, :], Albc[:, :], AF.Exp, scale=1.0))
        pb.on("scalar", mkA)

        def mkA2(v):
            pb.wait(v, "c_ac0")
            pb.inc("c_ac1", v.tensor_scalar(Abc[:, :], Abc[:, :], -1.0, None, op0=ALU.mult))
        pb.on("vector", mkA2)

        for sl in range(4):
            buf = sl % 2
            runstart = (sl % 2 == 0)
            run = sl // 2
            def ldcv(sync, sl=sl, run=run, runstart=runstart):
                pb.wait(sync, "c_xst")
                pb.wait(sync, "c_cv1")
                if runstart:
                    pb.inc("c_cvld", sync.dma_start(xconv[:, :, 3:515],
                                                    xbcr_dram[:, :, run, 0:512]), 16)
                else:
                    pb.inc("c_cvld", sync.dma_start(xconv[:, :, 0:515],
                                                    xbcr_dram[:, :, run, 509:1024]), 16)
            pb.on("sync", ldcv)

            if runstart:
                def cph(v, sl=sl, run=run):
                    pb.wait(v, "c_cvld")
                    pb.wait(v, "c_halo")
                    src = haloA if run == 0 else haloB
                    pb.inc("c_cvh", v.tensor_copy(xconv[:, :, 0:3], src[:, :, :]))
                pb.on("vector", cph)

            for fc in range(18):
                def cvx(v, sl=sl, fc=fc, buf=buf):
                    pb.wait(v, "c_cvld")
                    pb.wait(v, "c_cvh")
                    pb.wait(v, "c_wld")
                    pb.wait(v, "c_xsm")
                    pb.wait(v, "c_gm")
                    pb.wait(v, "c_btm")
                    pb.wait(v, "c_c32")
                    pb.wait(v, "c_cv2")
                    v.tensor_scalar(acc1[:, :], xconv[:, fc, 0:512], convw[:, fc, 0:1],
                                    None, op0=ALU.mult)
                    v.drain()
                    v.scalar_tensor_tensor(acc2[:, :], xconv[:, fc, 1:513], convw[:, fc, 1:2],
                                           acc1[:, :], op0=ALU.mult, op1=ALU.add)
                    v.drain()
                    v.scalar_tensor_tensor(acc1[:, :], xconv[:, fc, 2:514], convw[:, fc, 2:3],
                                           acc2[:, :], op0=ALU.mult, op1=ALU.add)
                    v.drain()
                    pb.inc("c_cv1", v.scalar_tensor_tensor(
                        acc2[:, :], xconv[:, fc, 3:515], convw[:, fc, 3:4],
                        acc1[:, :], op0=ALU.mult, op1=ALU.add))
                pb.on("vector", cvx)

                def cvs(sx, sl=sl, fc=fc, buf=buf):
                    pb.wait(sx, "c_cv1")
                    ins = sx.activation(xbcp[:, fc, :], acc2[:, :], AF.Silu,
                                        bias=convb[:, fc:fc + 1])
                    pb.inc("c_cv2", ins)
                    if fc == 17:
                        pb.inc("c_cvdone", sx.activation(
                            xbcp[:, 17, 0:1], acc2[:, 0:1], AF.Silu,
                            bias=convb[:, 17:18]))
                pb.on("scalar", cvs)

            for s4 in range(4):
                s = sl * 4 + s4
                ccol = s4 * 128
                def dta(v, s=s):
                    pb.wait(v, "c_cv2")
                    pb.wait(v, "c_ac1")
                    if s == 1 and io.get("_dbg0"):
                        pb.wait(v, "c_d0")
                    pb.inc("c_dt0", v.tensor_tensor(dt_tm[:, :], dtr[:, s, :], dtb[:, :], op=ALU.add))
                pb.on("vector", dta)

                def dtb_(sx, s=s):
                    pb.wait(sx, "c_dt0")
                    sx.activation(sp1[:, :], dt_tm[:, :], AF.Abs)
                    sx.drain()
                    sx.activation(sp1[:, :], sp1[:, :], AF.Exp, scale=-1.0)
                    sx.drain()
                    sx.activation(sp1[:, :], sp1[:, :], AF.Ln, bias=sc["one"])
                    sx.drain()
                    pb.inc("c_dt1", sx.activation(sp2[:, :], dt_tm[:, :], AF.Relu))
                pb.on("scalar", dtb_)

                def dtc(v, s=s):
                    pb.wait(v, "c_dt1")
                    v.tensor_tensor(dt_tm[:, :], sp1[:, :], sp2[:, :], op=ALU.add)
                    v.drain()
                    pb.inc("c_dt2", v.tensor_tensor(a_tm[:, :], dt_tm[:, :], Abc[:, :], op=ALU.mult))
                pb.on("vector", dtc)

                def mmac(t, s=s):
                    pb.wait(t, "c_dt2")
                    pb.wait(t, "c_wld")
                    pb.wait(t, "c_D2")
                    pb.wait(t, "c_acev", max(0, pb.val("c_acev")))
                    pb.inc("c_acm", t.matmul(psR[:, 0:32], io["utri_incl_sb"][:, :],
                                             a_tm[:, :], start=True, stop=True))
                pb.on("tensor", mmac)

                def evac_(v, s=s):
                    pb.wait(v, "c_acm")
                    pb.inc("c_acev", v.tensor_copy(acum_all[:, s, :], psR[:, 0:32]))
                pb.on("vector", evac_)

                def mmtr(t, s=s):
                    pb.wait(t, "c_acev")
                    pb.wait(t, "c_ypsfree", max(0, pb.val("c_ypsfree")))
                    pb.wait(t, "c_fmev", max(0, pb.val("c_fmev")))
                    pb.inc("c_trf", t.transpose(psYo[0:32, 0:128], acum_all[:, s, :], identf[:, :]))
                pb.on("tensor", mmtr)

                def evfm(v, s=s):
                    pb.wait(v, "c_trf")
                    pb.inc("c_fmev", v.tensor_copy(acum_fm[:, :], psYo[0:32, 0:128]))
                pb.on("vector", evfm)

                def flat(sync, s=s):
                    pb.wait(sync, "c_fmev")
                    pb.inc("c_flat", sync.dma_start(acum_flat[:, :], acum_fm[:, :]), 16)
                pb.on("sync", flat)

                def rsm(v, s=s):
                    pb.wait(v, "c_flat")
                    if s == 0:
                        v.memset(rsum[:, :], 0.0)
                        v.drain()
                    if s == 8:
                        v.tensor_scalar(rsum[:, :], rsum[:, :], crho2[:, :], None, op0=ALU.mult)
                        v.drain()
                    v.tensor_copy(cdrow_all[:, s, :], rsum[:, :])
                    v.drain()
                    arow = bass.AP(acum_flat, 127, [[4096, 1], [128, 32]])
                    pb.inc("c_rsm", v.tensor_tensor(rsum[:, :], rsum[:, :], arow, op=ALU.add))
                pb.on("vector", rsm)

                def eaa(sx, s=s):
                    pb.wait(sx, "c_acev")
                    ins = sx.activation(ea_tm[:, :], acum_all[:, s, :], AF.Exp)
                    pb.inc("c_ea", ins)
                pb.on("scalar", eaa)

                def mmlast(t, s=s):
                    pb.wait(t, "c_flat")
                    pb.wait(t, "c_cst")
                    pb.wait(t, "c_dcev", max(0, pb.val("c_dcev")))
                    arow = bass.AP(acum_flat, 127, [[4096, 1], [128, 32]])
                    pb.inc("c_lbm", t.matmul(psR[:, 512:544], ones_r[:, :], arow,
                                             start=True, stop=True))
                pb.on("tensor", mmlast)

                def dca(v, s=s):
                    pb.wait(v, "c_lbm")
                    pb.inc("c_dc0", v.tensor_tensor(dd_tm[:, :], psR[:, 512:544],
                                                    acum_all[:, s, :], op=ALU.subtract))
                pb.on("vector", dca)

                def dcb(sx, s=s):
                    pb.wait(sx, "c_dc0")
                    arow = bass.AP(acum_flat, 127, [[4096, 1], [128, 32]])
                    sx.activation(decay_tm[:, :], dd_tm[:, :], AF.Exp)
                    sx.drain()
                    pb.inc("c_dc1", sx.activation(dxr[:, :], arow, AF.Exp))
                pb.on("scalar", dcb)

                def mmdbc(t, s=s):
                    pb.wait(t, "c_dc1")
                    pb.inc("c_dbm", t.matmul(psR[:, 544:576], ones_r[:, :], dxr[:, :],
                                             start=True, stop=True))
                pb.on("tensor", mmdbc)

                def dcc(v, s=s):
                    pb.wait(v, "c_dbm")
                    pb.inc("c_dcev", v.tensor_copy(dbc[:, :], psR[:, 544:576]))
                pb.on("vector", dcc)

                for g2 in range(2):
                    def ttx(t, s=s, buf=buf, ccol=ccol, g2=g2):
                        pb.wait(t, "c_cv2")
                        if s == 1 and io.get("_dbg0"):
                            pb.wait(t, "c_d0")
                        pb.wait(t, "c_btev")
                        pb.wait(t, "c_xsev", max(0, pb.val("c_xsev") - 1))
                        for ic in range(8):
                            ins = t.transpose(tps[:, g2, ic * 128:(ic + 1) * 128],
                                              xbcp[:, g2 * 8 + ic, ccol:ccol + 128],
                                              identb[:, :])
                        pb.inc("c_xsm", ins)
                    pb.on("tensor", ttx)

                    def evxs(v, s=s, g2=g2):
                        pb.wait(v, "c_xsm")
                        pb.inc("c_xsev", v.tensor_copy(
                            xs_tm[:, g2 * 1024:(g2 + 1) * 1024], tps[:, g2, :]))
                    pb.on("vector", evxs)

                def mkX(v, s=s):
                    pb.wait(v, "c_xsev")
                    pb.wait(v, "c_dt2")
                    pb.wait(v, "c_dcev")
                    dtap = bass.AP(dt_tm, 0, [[32, 128], [1, 32], [0, 64]])
                    v.tensor_tensor(X1[:, :].rearrange("p (h q) -> p h q", h=32), xs_tm[:, :].rearrange("p (h q) -> p h q", h=32), dtap, op=ALU.mult)
                    v.drain()
                    dcap = bass.AP(decay_tm, 0, [[32, 128], [1, 32], [0, 64]])
                    pb.inc("c_X", v.tensor_tensor(X2[:, :].rearrange("p (h q) -> p h q", h=32), X1[:, :].rearrange("p (h q) -> p h q", h=32), dcap, op=ALU.mult))
                pb.on("vector", mkX)

                def evc32(v, s=s, buf=buf, ccol=ccol):
                    pb.wait(v, "c_cv2")
                    v.tensor_copy(Call[:, s, :], xbcp[:, 17, ccol:ccol + 128])
                    v.drain()
                    pb.inc("c_c32", v.tensor_copy(C32f[:, :], xbcp[:, 17, ccol:ccol + 128]))
                pb.on("vector", evc32)

                def mmg(t, s=s, buf=buf, ccol=ccol):
                    pb.wait(t, "c_cv2")
                    pb.wait(t, "c_gev", max(0, pb.val("c_gev")))
                    pb.inc("c_gm", t.matmul(psR[:, 0:128], xbcp[:, 16, ccol:ccol + 128],
                                            xbcp[:, 17, ccol:ccol + 128],
                                            start=True, stop=True))
                pb.on("tensor", mmg)

                def evg(v, s=s):
                    pb.wait(v, "c_gm")
                    pb.inc("c_gev", v.tensor_tensor(Gm[:, :], psR[:, 0:128],
                                                    io["triu01_sb"][:, :], op=ALU.mult))
                pb.on("vector", evg)

                def mmbt(t, s=s, buf=buf, ccol=ccol):
                    pb.wait(t, "c_cv2")
                    pb.wait(t, "c_btev", max(0, pb.val("c_btev")))
                    pb.inc("c_btm", t.transpose(tps[:, 0, 0:128],
                                                xbcp[:, 16, ccol:ccol + 128], identb[:, :]))
                pb.on("tensor", mmbt)

                def evbt(v, s=s):
                    pb.wait(v, "c_btm")
                    pb.inc("c_btev", v.tensor_copy(Btm[:, :], tps[:, 0, 0:128]))
                pb.on("vector", evbt)

                if s == 8:
                    def rho8(v):
                        pb.wait(v, "c_supd")
                        pb.wait(v, "c_snap")
                        pb.inc("c_supd", v.tensor_scalar(S[:, :], S[:, :], crho[:, :],
                                                         None, op0=ALU.mult))
                    pb.on("vector", rho8)

                for q in range(4):
                    def mmrow(t, s=s, q=q):
                        pb.wait(t, "c_flat")
                        pb.wait(t, "c_gev")
                        pb.wait(t, "c_dcev")
                        pb.wait(t, "c_D2")
                        for j in range(2):
                            ins = t.matmul(psR[:, j * 512:(j + 1) * 512], ones_r[:, :],
                                           acum_flat[:, q * 1024 + j * 512:q * 1024 + (j + 1) * 512],
                                           start=True, stop=True)
                        pb.inc("c_row", ins)
                    pb.on("tensor", mmrow)

                    def mkE(v, s=s, q=q):
                        pb.wait(v, "c_row")
                        acap = bass.AP(acum_all, s * 32 + q * 8, [[NT * 32, 128], [1, 8], [0, 128]])
                        v.tensor_tensor(D1[:, :].rearrange("p (h l) -> p h l", h=8),
                                        psR[:, 0:1024].rearrange("p (h l) -> p h l", h=8),
                                        acap, op=ALU.subtract)
                        v.drain()
                        pb.inc("c_D2", v.tensor_scalar(D1[:, :], D1[:, :], 0.0, None, op0=ALU.min))
                    pb.on("vector", mkE)

                    def expE(sx, s=s, q=q):
                        pb.wait(sx, "c_D2")
                        pb.inc("c_E", sx.activation(Ebuf[:, :], D1[:, :], AF.Exp))
                    pb.on("scalar", expE)

                    def mkM(v, s=s, q=q):
                        pb.wait(v, "c_E")
                        pb.wait(v, "c_gev")
                        gap = bass.AP(Gm, 0, [[128, 128], [0, 8], [1, 128]])
                        pb.inc("c_M", v.tensor_tensor(
                            Mq[:, :].rearrange("p (h l) -> p h l", h=8),
                            Ebuf[:, :].rearrange("p (h l) -> p h l", h=8), gap, op=ALU.mult))
                    pb.on("vector", mkM)

                    def mmyd(t, s=s, q=q):
                        pb.wait(t, "c_M")
                        pb.wait(t, "c_X")
                        pb.wait(t, "c_ypsfree", max(0, pb.val("c_ypsfree")))
                        for hh in range(8):
                            ins = t.matmul(psY[:, hh * 64:(hh + 1) * 64],
                                           Mq[:, hh * 128:(hh + 1) * 128],
                                           X1[:, (q * 8 + hh) * 64:(q * 8 + hh + 1) * 64],
                                           start=(hh == 0), stop=(hh == 7))
                        pb.inc("c_ydm", ins)
                    pb.on("tensor", mmyd)

                    def mmyo(t, s=s, q=q):
                        pb.wait(t, "c_c32")
                        pb.wait(t, "c_s0")
                        pb.wait(t, "c_supd")
                        pb.wait(t, "c_ypsfree", max(0, pb.val("c_ypsfree")))
                        pb.inc("c_yom", t.matmul(psYo[:, :], C32f[:, :],
                                                 S[:, q * 512:(q + 1) * 512],
                                                 start=True, stop=True))
                    pb.on("tensor", mmyo)

                    def mmu(t, s=s, q=q):
                        pb.wait(t, "c_btev")
                        pb.wait(t, "c_X")
                        pb.wait(t, "c_supd", max(0, pb.val("c_supd")))
                        pb.inc("c_um", t.matmul(psU[:, :], Btm[:, :],
                                                X2[:, q * 512:(q + 1) * 512],
                                                start=True, stop=True))
                    pb.on("tensor", mmu)

                    def yasm(v, s=s, q=q):
                        pb.wait(v, "c_ydm")
                        pb.wait(v, "c_yom")
                        pb.wait(v, "c_ea")
                        eap = bass.AP(ea_tm, q * 8, [[32, 128], [1, 8], [0, 64]])
                        v.tensor_tensor(ytmp[:, :].rearrange("p (h w) -> p h w", h=8),
                                        psYo[:, :].rearrange("p (h w) -> p h w", h=8),
                                        eap, op=ALU.mult)
                        v.drain()
                        pb.inc("c_ypsfree", v.tensor_tensor(ytmp[:, :], ytmp[:, :], psY[:, :], op=ALU.add))
                        v.drain()
                        dap = bass.AP(Dbc, q * 8, [[32, 128], [1, 8], [0, 64]])
                        v.tensor_tensor(ytmp2[:, :].rearrange("p (h w) -> p h w", h=8),
                                        xs_tm[:, q * 512:(q + 1) * 512].rearrange("p (h w) -> p h w", h=8),
                                        dap, op=ALU.mult)
                        v.drain()
                        pb.inc("c_yas", v.tensor_tensor(ytile[:, q * 512:(q + 1) * 512],
                                                        ytmp[:, :], ytmp2[:, :], op=ALU.add))
                    pb.on("vector", yasm)

                    def supd(v, s=s, q=q):
                        pb.wait(v, "c_um")
                        pb.wait(v, "c_yom")
                        v.tensor_tensor(S[:, q * 512:(q + 1) * 512].rearrange("p (h w) -> p h w", h=8),
                                        S[:, q * 512:(q + 1) * 512].rearrange("p (h w) -> p h w", h=8),
                                        bass.AP(dbc, q * 8, [[32, 128], [1, 8], [0, 64]]), op=ALU.mult)
                        v.drain()
                        pb.inc("c_supd", v.tensor_tensor(
                            S[:, q * 512:(q + 1) * 512], S[:, q * 512:(q + 1) * 512],
                            psU[:, :], op=ALU.add))
                    pb.on("vector", supd)

                def sty(sync, s=s):
                    pb.wait(sync, "c_yas")
                    pb.inc("c_yst", sync.dma_start(y_dram[s * 128:(s + 1) * 128, :], ytile[:, :]), 16)
                pb.on("sync", sty)

                if s == 0 and io.get("_dbg0"):
                    def d0(sync):
                        pb.wait(sync, "c_yas")
                        pb.inc("c_d0", sync.dma_start(io["_dbg0"]["x1"], X1[:, :]), 16)
                        pb.inc("c_d0", sync.dma_start(io["_dbg0"]["xstm"], xs_tm[:, :]), 16)
                        pb.inc("c_d0", sync.dma_start(io["_dbg0"]["gm"], Gm[:, :]), 16)
                        pb.inc("c_d0", sync.dma_start(io["_dbg0"]["acum"], acum_all[:, 0, :]), 16)
                        pb.inc("c_d0", sync.dma_start(io["_dbg0"]["dt"], dt_tm[:, :]), 16)
                        pb.inc("c_d0", sync.dma_start(io["_dbg0"]["xbcp"], xbcp[:, :, :]), 16)
                    pb.on("sync", d0)

                if s == 7 or s == 15:
                    def snap(sync, s=s):
                        pb.wait(sync, "c_supd")
                        row = 0 if s == 7 else 128
                        pb.inc("c_snap", sync.dma_start(st_in[row:row + 128, :], S[:, :]), 16)
                    pb.on("sync", snap)

        def sprep(v):
            pb.wait(v, "c_rsm")
            v.drain()
            v.memset(acum_flat[0:1, 0:2048], 0.0)
            v.drain()
            pb.inc("c_sprep", v.tensor_copy(acum_flat[0:1, 0:32], rsum[:, :]))
        pb.on("vector", sprep)

        def sput(sync):
            pb.wait(sync, "c_snap")
            pb.wait(sync, "c_sprep")
            pb.inc("c_sst", sync.dma_start(st_in[256:257, :], acum_flat[0:1, 0:2048]), 16)
        pb.on("sync", sput)

        def scc(g):
            pb.wait(g, "c_sst")
            g.collective_compute("AllGather", ALU.bypass,
                                 replica_groups=[[0, 1], [2, 3], [4, 5], [6, 7]],
                                 ins=[st_in], outs=[st_out]).then_inc(pb.semh("c_scc")[0], 1)
            pb.semh("c_scc")[1] += 1
        pb.on("gpsimd", scc)

        def sget(sync):
            pb.wait(sync, "c_scc")
            pb.wait(sync, "c_yst")
            pb.inc("c_sld", sync.dma_start(SiniA[:, :], st_out[0:128, :]), 16)
            pb.inc("c_sld", sync.dma_start(SiniB[:, :], st_out[385:513, :]), 16)
            pb.inc("c_sld", sync.dma_start(ytile[:, :], st_out[0:128, :]), 16)
            pb.inc("c_sld", sync.dma_start(rrow[:, :], st_out[513:514, 0:32]), 16)
        pb.on("sync", sget)

        def sblend1(sx):
            pb.wait(sx, "c_sld")
            pb.inc("c_sb0", sx.activation(dxr[:, :], rrow[:, :], AF.Exp))
        pb.on("scalar", sblend1)

        def sblend2(v):
            pb.wait(v, "c_sb0")
            pb.wait(v, "c_wld")
            pb.inc("c_sb1", v.tensor_scalar(dxr[:, :], dxr[:, :], cdsel[:, :], cdseli[:, :],
                                            op0=ALU.mult, op1=ALU.add))
        pb.on("vector", sblend2)

        def mmds(t):
            pb.wait(t, "c_sb1")
            pb.inc("c_sb2", t.matmul(psR[:, 544:576], ones_r[:, :], dxr[:, :],
                                     start=True, stop=True))
        pb.on("tensor", mmds)

        def sblend3(v):
            pb.wait(v, "c_sb2")
            v.tensor_copy(dbc[:, :], psR[:, 544:576])
            v.drain()
            v.tensor_scalar(SiniA[:, :], SiniA[:, :], cselA[:, :], None, op0=ALU.mult)
            v.drain()
            v.tensor_scalar(SiniB[:, :], SiniB[:, :], cw1[:, :], None, op0=ALU.mult)
            v.drain()
            v.tensor_tensor(ytile[:, :].rearrange("p (h w) -> p h w", h=32),
                            ytile[:, :].rearrange("p (h w) -> p h w", h=32),
                            bass.AP(dbc, 0, [[32, 128], [1, 32], [0, 64]]), op=ALU.mult)
            v.drain()
            pb.inc("c_sini", v.tensor_tensor(SiniB[:, :], SiniB[:, :], ytile[:, :], op=ALU.add))
        pb.on("vector", sblend3)

    with contextlib.ExitStack() as e4:
        opw = _sb(nc, e4, "c_opw", [128, 16, 1024], BF16)
        rmsw = _sb(nc, e4, "c_rmsw_s", [128, 2048], F32)
        ysl = _sb(nc, e4, "c_ysl", [128, 2048], F32)
        zsl = _sb(nc, e4, "c_zsl", [128, 2048], BF16)
        zss = _sb(nc, e4, "c_zss", [128, 2048], BF16)
        y3 = _sb(nc, e4, "c_y3", [128, 2048], BF16)
        y3T = _sb(nc, e4, "c_y3T", [128, 16, 128], BF16)
        Scor = _sb(nc, e4, "c_Scor", [128, 2048], F32)
        cd_bc = _sb(nc, e4, "c_cdbc2", [128, 32], F32)
        ea2 = _sb(nc, e4, "c_ea2", [128, 32], F32)
        ycq = _sb(nc, e4, "c_ycq", [128, 512], F32)
        C32g = _sb(nc, e4, "c_C32g", [128, 128], F32)
        rstd = _sb(nc, e4, "c_rstd", [128, 1], F32)
        rt4 = _sb(nc, e4, "c_rt4", [128, 1024], F32)

        def ldop(g):
            pb.wait(g, "c_sini")
            pb.inc("c_wld2", g.dma_start(opw[:, :, :], opr[:, :, :]), 16)
            pb.inc("c_wld2", g.dma_start(rmsw[:, :], io["c_rmsw"][:, :]), 16)
        pb.on("gpsimd", ldop)

        for s in range(NT):
            Sini = SiniA if s < 8 else SiniB

            def ldy(sync, s=s):
                pb.wait(sync, "c_yst")
                pb.wait(sync, "c_y3", pb.val("c_y3"))
                pb.inc("c_yld", sync.dma_start(ysl[:, :], y_dram[s * 128:(s + 1) * 128, :]), 16)
                pb.inc("c_yld", sync.dma_start(zsl[:, :], z_dram[s * 128:(s + 1) * 128, :]), 16)
            pb.on("sync", ldy)

            def cexp(sx, s=s):
                pb.wait(sx, "c_sini")
                pb.wait(sx, "c_cd1", max(0, pb.val("c_cd1")))
                sx.activation(ea2[:, :], acum_all[:, s, :], AF.Exp)
                sx.drain()
                pb.inc("c_cd0", sx.activation(rrow[:, :], cdrow_all[:, s, :], AF.Exp))
            pb.on("scalar", cexp)

            def mmcd(t, s=s):
                pb.wait(t, "c_cd0")
                pb.wait(t, "c_opev")
                pb.inc("c_cdm", t.matmul(psR[:, 544:576], ones_r[:, :], rrow[:, :],
                                         start=True, stop=True))
            pb.on("tensor", mmcd)

            def mkscor(v, s=s, Sini=Sini):
                pb.wait(v, "c_cdm")
                pb.wait(v, "c_ycm", pb.val("c_ycm"))
                v.tensor_copy(cd_bc[:, :], psR[:, 544:576])
                v.drain()
                v.tensor_copy(C32g[:, :], Call[:, s, :])
                v.drain()
                pb.inc("c_cd1", v.tensor_tensor(
                    Scor[:, :].rearrange("p (h w) -> p h w", h=32),
                    Sini[:, :].rearrange("p (h w) -> p h w", h=32),
                    bass.AP(cd_bc, 0, [[32, 128], [1, 32], [0, 64]]), op=ALU.mult))
            pb.on("vector", mkscor)

            for q in range(4):
                def mmyc(t, s=s, q=q):
                    pb.wait(t, "c_cd1")
                    pb.wait(t, "c_ycev", max(0, pb.val("c_ycev")))
                    pb.inc("c_ycm", t.matmul(psYo[:, :], C32g[:, :],
                                             Scor[:, q * 512:(q + 1) * 512],
                                             start=True, stop=True))
                pb.on("tensor", mmyc)

                def evyc(v, s=s, q=q):
                    pb.wait(v, "c_ycm")
                    pb.wait(v, "c_yld")
                    eap = bass.AP(ea2, q * 8, [[32, 128], [1, 8], [0, 64]])
                    pb.inc("c_ycev", v.tensor_tensor(
                        ycq[:, :].rearrange("p (h w) -> p h w", h=8),
                        psYo[:, :].rearrange("p (h w) -> p h w", h=8),
                        eap, op=ALU.mult))
                    v.drain()
                    pb.inc("c_yc2", v.tensor_tensor(
                        ysl[:, q * 512:(q + 1) * 512],
                        ysl[:, q * 512:(q + 1) * 512], ycq[:, :], op=ALU.add))
                pb.on("vector", evyc)

            def zsi(sx, s=s):
                pb.wait(sx, "c_yc2", pb.val("c_yc2"))
                pb.inc("c_zs", sx.activation(zss[:, :], zsl[:, :], AF.Silu))
            pb.on("scalar", zsi)

            def ymul(v, s=s):
                pb.wait(v, "c_zs")
                pb.inc("c_y2", v.tensor_tensor(ysl[:, :], ysl[:, :], zss[:, :], op=ALU.mult))
            pb.on("vector", ymul)

            def rms1(sx, s=s):
                pb.wait(sx, "c_y2")
                pb.inc("c_rms0", sx.activation(sc["junk"][:, 0:2048], ysl[:, :], AF.Square,
                                               accum_out=rstd[:, :]))
            pb.on("scalar", rms1)

            def rms2(v, s=s):
                pb.wait(v, "c_rms0")
                pb.inc("c_rms1", v.tensor_scalar(rstd[:, :], rstd[:, :], 1.0 / 2048, float(EPS),
                                                 op0=ALU.mult, op1=ALU.add))
            pb.on("vector", rms2)

            def rms3(sx, s=s):
                pb.wait(sx, "c_rms1")
                pb.inc("c_rms2", sx.activation(rstd[:, :], rstd[:, :], AF.Sqrt))
            pb.on("scalar", rms3)

            def rms4(v, s=s):
                pb.wait(v, "c_rms2")
                pb.wait(v, "c_wld2")
                v.reciprocal(sc["c1"][:, :], rstd[:, :])
                v.drain()
                v.tensor_scalar(ysl[:, :], ysl[:, :], sc["c1"][:, :], None, op0=ALU.mult)
                v.drain()
                pb.inc("c_y3", v.tensor_tensor(y3[:, :], ysl[:, :], rmsw[:, :], op=ALU.mult))
            pb.on("vector", rms4)

            for g2 in range(2):
                def tty(t, s=s, g2=g2):
                    pb.wait(t, "c_y3")
                    pb.wait(t, "c_ytev", max(0, pb.val("c_ytev") - 1))
                    for ic in range(8):
                        ins = t.transpose(tps[:, g2, ic * 128:(ic + 1) * 128],
                                          y3[:, (g2 * 8 + ic) * 128:(g2 * 8 + ic + 1) * 128],
                                          identb[:, :])
                    pb.inc("c_ytm", ins)
                pb.on("tensor", tty)

                def evyt(v, s=s, g2=g2):
                    pb.wait(v, "c_ytm")
                    pb.inc("c_ytev", v.tensor_copy(
                        y3T[:, g2 * 8:(g2 + 1) * 8, :],
                        tps[:, g2, :].rearrange("p (a b) -> p a b", a=8)))
                pb.on("vector", evyt)

            for hf in range(2):
                def mmop(t, s=s, hf=hf):
                    pb.wait(t, "c_ytev")
                    pb.wait(t, "c_wld2")
                    pb.wait(t, "c_opev", max(0, pb.val("c_opev") - 1))
                    for ic in range(16):
                        ins = t.matmul(psR[:, (hf % 2) * 512:(hf % 2) * 512 + 512],
                                       y3T[:, ic, :], opw[:, ic, hf * 512:(hf + 1) * 512],
                                       start=(ic == 0), stop=(ic == 15))
                    pb.inc("c_opm", ins)
                pb.on("tensor", mmop)

                def evop(v, s=s, hf=hf):
                    pb.wait(v, "c_opm")
                    pb.wait(v, "ac")
                    pb.inc("c_opev", v.scalar_tensor_tensor(
                        rt4[:, hf * 512:(hf + 1) * 512], stream[:, s, hf * 512:(hf + 1) * 512],
                        2.0, psR[:, (hf % 2) * 512:(hf % 2) * 512 + 512],
                        op0=ALU.mult, op1=ALU.add))
                pb.on("vector", evop)

            emit_ln(pb, sc, rt4[:, :], stream[:, s, :], ready=("c_opev", pb.val("c_opev")))


def stage_mamba(inp):
    d = {}
    d["c_ipT"] = _t(inp["L1_attn_in_proj_w"])
    d["c_opT"] = _t(inp["L1_attn_out_proj_w"])
    d["c_normw"] = np.ascontiguousarray(np.broadcast_to(np.asarray(inp["L1_attn_norm_w"], np.float32), (128, 1024)))
    d["c_normb"] = np.ascontiguousarray(np.broadcast_to(np.asarray(inp["L1_attn_norm_b"], np.float32), (128, 1024)))
    d["c_convw"] = np.ascontiguousarray(np.asarray(inp["L1_attn_conv_w"], np.float32).reshape(18, 128, 4).transpose(1, 0, 2))
    d["c_convb"] = np.ascontiguousarray(np.asarray(inp["L1_attn_conv_b"], np.float32).reshape(18, 128).T)
    d["c_dtb"] = np.ascontiguousarray(np.broadcast_to(np.asarray(inp["L1_attn_dt_bias"], np.float32), (128, 32)))
    d["c_Alog"] = np.ascontiguousarray(np.broadcast_to(np.asarray(inp["L1_attn_A_log"], np.float32), (128, 32)))
    d["c_D"] = np.ascontiguousarray(np.broadcast_to(np.asarray(inp["L1_attn_D"], np.float32), (128, 32)))
    d["c_rmsw"] = np.ascontiguousarray(np.broadcast_to(np.asarray(inp["L1_attn_rms_w"], np.float32), (128, 2048)))
    d["utri_incl"] = np.triu(np.ones((128, 128), np.float32), 0)
    d["triu01"] = np.triu(np.ones((128, 128), np.float32), 0)
    d["ident"] = np.eye(128, dtype=np.float32)
    return d


def stage_mamba_core(core):
    half = core % 2
    e = (half == 0)
    d = {}
    d["c_rho"] = np.full((128, 1), 0.0 if e else 1.0, np.float32)
    d["c_rho2"] = np.full((1, 1), 0.0 if e else 1.0, np.float32)
    d["c_w1"] = np.full((128, 1), 1.0 if e else 0.0, np.float32)
    d["c_dsel"] = np.full((1, 1), 1.0 if e else 0.0, np.float32)
    d["c_dseli"] = np.full((1, 1), 0.0 if e else 1.0, np.float32)
    d["c_selA"] = np.full((128, 1), 0.0 if e else 1.0, np.float32)
    d["c_selE"] = np.full((128, 1), 1.0 if e else 0.0, np.float32)
    d["c_selO"] = np.full((128, 1), 0.0 if e else 1.0, np.float32)
    return d


MAMBA_DEBUG = False

def build_mamba_launch(staged):
    nc = _new_nc()
    hs = {k: _dram_in(nc, k, v) for k, v in staged.items()}
    for k in ("c_rho", "c_w1", "c_selA", "c_selE", "c_selO"):
        hs[k] = nc.dram_tensor(k, [128, 1], F32, kind="ExternalInput").ap()
    for k in ("c_rho2", "c_dsel", "c_dseli"):
        hs[k] = nc.dram_tensor(k, [1, 1], F32, kind="ExternalInput").ap()
    x_h = nc.dram_tensor("x_in", [TOK, D], F32, kind="ExternalInput").ap()
    y_h = nc.dram_tensor("y_out", [TOK, D], F32, kind="ExternalOutput").ap()
    es = contextlib.ExitStack()
    with es:
        pb = PB(nc, es)
        stream = _sb(nc, es, "stream", [128, NT, 1024], F32)
        identb = _sb(nc, es, "identb", [128, 128], BF16)
        utri_sb = _sb(nc, es, "utri_sb", [128, 128], F32)
        triu_sb = _sb(nc, es, "triu_sb", [128, 128], F32)
        csts = emit_const_pool(pb, nc, es, [EPS, 1.0])
        lnsc = mk_ln_scratch(nc, es, "ln_", csts[EPS])
        lnsc["one"] = csts[1.0]

        def ldi(g):
            pb.inc("wld", g.dma_start(identb[:, :], hs["ident"][:, :]), 16)
            pb.inc("wld", g.dma_start(utri_sb[:, :], hs["utri_incl"][:, :]), 16)
            pb.inc("wld", g.dma_start(triu_sb[:, :], hs["triu01"][:, :]), 16)
        pb.on("gpsimd", ldi)

        def wldw(t):
            pb.wait(t, "wld")
        pb.on("tensor", wldw)

        emit_stream_load(pb, stream, x_h)
        io = dict(hs)
        io.update(stream=stream, identb=identb, lnsc=lnsc,
                  utri_incl_sb=utri_sb, triu01_sb=triu_sb)
        if MAMBA_DEBUG:
            io["_dbg0"] = {
                "x1": nc.dram_tensor("dbg_x1", [128, 2048], BF16, kind="ExternalOutput").ap(),
                "xstm": nc.dram_tensor("dbg_xstm", [128, 2048], BF16, kind="ExternalOutput").ap(),
                "gm": nc.dram_tensor("dbg_gm", [128, 128], BF16, kind="ExternalOutput").ap(),
                "acum": nc.dram_tensor("dbg_acum", [128, 32], F32, kind="ExternalOutput").ap(),
                "dt": nc.dram_tensor("dbg_dt", [128, 32], F32, kind="ExternalOutput").ap(),
                "xbcp": nc.dram_tensor("dbg_xbcp", [128, 18, 512], BF16, kind="ExternalOutput").ap(),
            }
        emit_mamba(pb, es, io)
        if MAMBA_DEBUG:
            dbgy = nc.dram_tensor("dbg_y", [TOK, 2048], F32, kind="ExternalOutput").ap()
            dbgz = nc.dram_tensor("dbg_z", [TOK, 2048], BF16, kind="ExternalOutput").ap()
            dbgx = nc.dram_tensor("dbg_xbcr", [128, 18, 2, 1024], BF16, kind="ExternalOutput").ap()
            ydr = io["_dbg"]["y"]
            zdr = io["_dbg"]["z"]
            xdr = io["_dbg"]["x"]
            def dbg(sync):
                pb.wait(sync, "c_yst")
                pb.inc("dbgo", sync.dma_start(dbgy, ydr), 16)
                pb.inc("dbgo", sync.dma_start(dbgz, zdr), 16)
                pb.inc("dbgo", sync.dma_start(dbgx, xdr), 16)
                pb.wait(sync, "dbgo")
            pb.on("sync", dbg)
        emit_stream_store(pb, stream, y_h, ["lnw"])
    return nc


def _t(a):
    return np.ascontiguousarray(np.asarray(a).T.astype(np.float32))


def _tile_kf(wT, KC, FC, FW):
    K, F = wT.shape
    return np.ascontiguousarray(
        wT.reshape(KC, 128, FC, FW).transpose(2, 1, 0, 3))


def stage_ffn(inp):
    d = {}
    d["w1ft"] = _tile_kf(_t(inp["L1_ffn_w1_w"]), 8, 32, 128)
    d["w2ft"] = _tile_kf(_t(inp["L1_ffn_w2_w"]), 8, 32, 128)
    d["h1ft"] = _tile_kf(_t(inp["L1_ffn_h1_w"]), 32, 32, 128)
    d["h2ft"] = _tile_kf(_t(inp["L1_ffn_h2_w"]), 32, 32, 128)
    d["w3ft"] = _tile_kf(_t(inp["L1_ffn_w3_w"]), 32, 2, 512)
    d["s1c"] = np.ascontiguousarray(np.asarray(inp["L1_ffn_w1_s"], np.float32).reshape(32, 128).T)
    d["s2c"] = np.ascontiguousarray(np.asarray(inp["L1_ffn_w2_s"], np.float32).reshape(32, 128).T)
    d["sh1c"] = np.ascontiguousarray(np.asarray(inp["L1_ffn_h1_s"], np.float32).reshape(32, 128).T)
    d["sh2c"] = np.ascontiguousarray(np.asarray(inp["L1_ffn_h2_s"], np.float32).reshape(32, 128).T)
    d["s3bc"] = np.ascontiguousarray(np.broadcast_to(np.asarray(inp["L1_ffn_w3_s"], np.float32), (128, 1024)))
    d["ident"] = np.eye(128, dtype=np.float32)
    return d


def emit_phase_barrier(pb, sem, target):
    for eng in ("sync", "gpsimd", "tensor", "vector", "scalar"):
        def w(e, sem=sem, target=target):
            pb.wait(e, sem, target)
        pb.on(eng, w)


def build_full_launch(inp_shapes):
    nc = _new_nc()
    hs = {}
    for name, (shape, dt) in inp_shapes.items():
        hs[name] = nc.dram_tensor(name, list(shape), dt, kind="ExternalInput").ap()
    y_h = nc.dram_tensor("y_out", [TOK, D], F32, kind="ExternalOutput").ap()
    es = contextlib.ExitStack()
    with es:
        pb = PB(nc, es)
        stream = _sb(nc, es, "stream", [128, NT, 1024], F32)
        identb = _sb(nc, es, "identb", [128, 128], BF16)
        csts = emit_const_pool(pb, nc, es, [EPS, 1.0])
        lnsc = mk_ln_scratch(nc, es, "ln_", csts[EPS])
        lnsc["one"] = csts[1.0]

        def ldi(g):
            pb.inc("wld", g.dma_start(identb[:, :], hs["ident"][:, :]), 16)
        pb.on("gpsimd", ldi)

        io = dict(hs)
        io.update(stream=stream, identb=identb, lnsc=lnsc)

        with contextlib.ExitStack() as esA:
            emit_attn(pb, esA, io)
        emit_phase_barrier(pb, "lnw", 16)

        with contextlib.ExitStack() as esB:
            emit_moe(pb, esB, io)
        emit_phase_barrier(pb, "lnw", 32)

        with contextlib.ExitStack() as esC:
            utri_sb = _sb(nc, esC, "utri_sb", [128, 128], F32)
            triu_sb = _sb(nc, esC, "triu_sb", [128, 128], F32)

            def ldu(g):
                pb.inc("c_wld", g.dma_start(utri_sb[:, :], hs["utri_incl"][:, :]), 16)
                pb.inc("c_wld", g.dma_start(triu_sb[:, :], hs["triu01"][:, :]), 16)
            pb.on("gpsimd", ldu)
            io2 = dict(io)
            io2.update(utri_incl_sb=utri_sb, triu01_sb=triu_sb)
            emit_mamba(pb, esC, io2)
        emit_phase_barrier(pb, "lnw", 64)

        with contextlib.ExitStack() as esD:
            emit_ffn(pb, esD, io)
        emit_stream_store(pb, stream, y_h, ["lnw"])
    return nc


def stage_all(inp):
    shared = {}
    shared.update(stage_attn(inp))
    shared.update(stage_moe(inp))
    shared.update(stage_mamba(inp))
    shared.update(stage_ffn(inp))
    maps = []
    for core in range(NCORE):
        d = dict(shared)
        d.update(stage_attn_core(inp, core, shared))
        d.update(stage_moe_core(core))
        d.update(stage_mamba_core(core))
        maps.append(d)
    return maps


_DT_MAP = {"float32": F32, "bfloat16": BF16}


def kernel_fused(**inputs):
    inp = {k: np.asarray(v) for k, v in inputs.items()}
    maps = stage_all(inp)
    inp_shapes = {k: (v.shape, _DT_MAP[str(v.dtype)]) for k, v in maps[0].items()}
    nc = build_full_launch(inp_shapes)
    res = run_bass_kernel_spmd(nc, maps, core_ids=list(range(NCORE)))
    return _reassemble([r["y_out"] for r in res.results])


def _reassemble(per_core):
    out = np.zeros((B, T, D), np.float32)
    for core in range(NCORE):
        b, half = core // 2, core % 2
        out[b][_own_tokens(half)] = per_core[core]
    return out


def kernel(**inputs):
    return kernel_fused(**inputs)


def kernel_multilaunch(**inputs):
    inp = {k: np.asarray(v) for k, v in inputs.items()}

    sh = stage_attn(inp)
    nc_a = build_attn_launch(sh)
    maps = []
    for core in range(NCORE):
        d = dict(sh)
        d.update(stage_attn_core(inp, core, sh))
        maps.append(d)
    res = run_bass_kernel_spmd(nc_a, maps, core_ids=list(range(NCORE)))
    x1 = [r["y_out"] for r in res.results]

    sm = stage_moe(inp)
    nc_b = build_moe_launch(sm)
    maps = []
    for core in range(NCORE):
        d = dict(sm)
        d.update(stage_moe_core(core))
        d["x_in"] = np.ascontiguousarray(x1[core])
        maps.append(d)
    res = run_bass_kernel_spmd(nc_b, maps, core_ids=list(range(NCORE)))
    x2 = [r["y_out"] for r in res.results]

    sc_ = stage_mamba(inp)
    nc_c = build_mamba_launch(sc_)
    maps = []
    for core in range(NCORE):
        d = dict(sc_)
        d.update(stage_mamba_core(core))
        d["x_in"] = np.ascontiguousarray(x2[core])
        maps.append(d)
    res = run_bass_kernel_spmd(nc_c, maps, core_ids=list(range(NCORE)))
    x3 = [r["y_out"] for r in res.results]

    sf = stage_ffn(inp)
    nc_d = build_ffn_launch(sf)
    maps = []
    for core in range(NCORE):
        d = dict(sf)
        d["x_in"] = np.ascontiguousarray(x3[core])
        maps.append(d)
    res = run_bass_kernel_spmd(nc_d, maps, core_ids=list(range(NCORE)))
    x4 = [r["y_out"] for r in res.results]

    return _reassemble(x4)
